# version 01
# baseline (speedup 1.0000x reference)
import numpy as np
import ml_dtypes
import concourse.bass as bass
import concourse.mybir as mybir
from concourse.bass_utils import run_bass_kernel_spmd
from contextlib import ExitStack

F32 = mybir.dt.float32
BF16 = mybir.dt.bfloat16
AF = mybir.ActivationFunctionType
ALU = mybir.AluOpType
AX = mybir.AxisListType
BFD = ml_dtypes.bfloat16

D = 1024
T = 2048
NCH = 8
P = 128
LN_EPS = 1e-5
HF = 258
LTF = HF + T
NW = T // 256
MCH = 22
DS_SHIFT = 20.0

ENGINES = ("sp", "pe", "dve", "act", "pool")
ENG_ATTR = {"sp": "sync", "pe": "tensor", "dve": "vector", "act": "scalar", "pool": "gpsimd"}


NDMASEM = 32


class Prog:

    def __init__(self, nc):
        self.nc = nc
        self.tasks = {e: [] for e in ENGINES}
        self.n = {e: 0 for e in ENGINES}
        self.lastw = {}
        self.cc_n = 0
        self.preamble = []
        self.dma_rr = 0
        self.dma_use = [0] * NDMASEM

    def _merge(self, eng, deps):
        merged = {}
        if self.preamble:
            deps = list(deps) + self.preamble
        for d in deps:
            if d is None:
                continue
            e2, c2 = d
            if e2 == eng and c2 >= self.n.get(eng, 0) + 1:
                continue
            if merged.get(e2, 0) < c2:
                merged[e2] = c2
        waits = []
        for e2, c2 in merged.items():
            if self.lastw.get((eng, e2), 0) >= c2:
                continue
            self.lastw[(eng, e2)] = c2
            waits.append((e2, c2))
        return waits

    def fence(self):
        toks = [(e, self.n[e]) for e in ENGINES if self.n[e] > 0]
        toks += [(f"dq{i}", 16 * self.dma_use[i]) for i in range(NDMASEM) if self.dma_use[i]]
        toks += [(f"cc{i}", 1) for i in range(self.cc_n)]
        self.preamble = toks
        return toks

    def task(self, eng, deps, emit, inc=1):
        waits = self._merge(eng, deps)
        self.n[eng] += inc
        tok = (eng, self.n[eng])
        self.tasks[eng].append((waits, emit, tok))
        return tok

    def dma(self, out, in_, deps, eng="sp", **kw):
        si = self.dma_rr % NDMASEM
        self.dma_rr += 1
        deps = list(deps)
        if self.dma_use[si] > 0:
            deps.append((f"dq{si}", 16 * self.dma_use[si]))
        self.dma_use[si] += 1
        tok = (f"dq{si}", 16 * self.dma_use[si])
        waits = self._merge(eng, deps)
        self.n[eng] += 0
        self.tasks[eng].append((waits, lambda e: e.dma_start(out=out, in_=in_, **kw), tok))
        return tok

    def barrier(self, cc_in_ap, cc_out_ap, deps):
        waits = self._merge("pool", deps)
        self.cc_n += 1
        tok = (f"cc{self.cc_n - 1}", 1)

        def emit(e):
            return e.collective_compute(
                "AllGather", ALU.bypass,
                replica_groups=[[0, 1], [2, 3], [4, 5], [6, 7]],
                ins=[cc_in_ap], outs=[cc_out_ap])

        self.tasks["pool"].append((waits, emit, tok))
        return tok

    def emit_all(self, block, mksem):
        sems = {e: mksem(f"s_{e}") for e in ENGINES}
        for i in range(NDMASEM):
            if self.dma_use[i]:
                sems[f"dq{i}"] = mksem(f"s_dq{i}")
        for i in range(self.cc_n):
            sems[f"cc{i}"] = mksem(f"s_cc{i}")

        def make(eng):
            def body(engine):
                for waits, emit, tok in self.tasks[eng]:
                    for e2, c2 in waits:
                        engine.wait_ge(sems[e2], c2)
                    ins = emit(engine)
                    ins.then_inc(sems[tok[0]], 16 if tok[0].startswith("dq") else 1)
                if eng == "sp":
                    for e2 in ENGINES:
                        if e2 != "sp" and self.n[e2] > 0:
                            engine.wait_ge(sems[e2], self.n[e2])
                    for i in range(NDMASEM):
                        if self.dma_use[i]:
                            engine.wait_ge(sems[f"dq{i}"], 16 * self.dma_use[i])
                    for i in range(self.cc_n):
                        engine.wait_ge(sems[f"cc{i}"], 1)
            return body

        for eng in ENGINES:
            getattr(block, ENG_ATTR[eng])(make(eng))


class Buf:
    __slots__ = ("ap", "_w", "readers", "tag")

    def __init__(self, ap=None, tag=""):
        self.ap = ap
        self._w = {}
        self.readers = []
        self.tag = tag

    def rd(self):
        return [(e, c) for e, c in self._w.items()]

    def wd(self):
        return list(self.readers) + self.rd()

    def r(self, tok):
        self.readers.append(tok)

    def wr(self, tok, reset=False):
        if reset:
            self._w = {}
        e, c = tok
        if self._w.get(e, 0) < c:
            self._w[e] = c
        self.readers = [t for t in self.readers if t[0] != e or t[1] > c] if False else self.readers
        return tok




class Arena:

    def __init__(self, k, name, nbytes, stack=None):
        self.buf = Buf((stack or k.stack).enter_context(
            k.nc.sbuf_tensor(name, [P, nbytes], mybir.dt.uint8)).ap(), tag=name)
        self.nbytes = nbytes
        self.off = 0

    def reset(self):
        self.off = 0

    def alloc(self, name, shape, dt):
        assert len(shape) == 2
        esz = {F32: 4, BF16: 2}[dt]
        width = shape[1] * esz
        off = (self.off + 63) & ~63
        self.off = off + width
        assert self.off <= self.nbytes, (name, self.off, self.nbytes)
        ap = self.buf.ap[0:shape[0], off:off + width].bitcast(dt)
        return Buf(ap, tag=name)


def rotary_tables(n=4096, dim_head=64, window=256):
    half = np.arange(0, dim_head, 2, dtype=np.float64)
    inv_freq = 1.0 / 10000.0 ** (half / dim_head)
    t = np.arange(n, dtype=np.float64)
    freqs = t[:, None] * inv_freq[None, :]
    freqs = np.concatenate([freqs, freqs], -1)
    base_scale = (half + 0.4 * dim_head) / (1.4 * dim_head)
    power = (t - n // 2) / (window // 2)
    scale = base_scale[None, :] ** power[:, None]
    scale = np.concatenate([scale, scale], -1)
    return (np.cos(freqs).astype(np.float32), np.sin(freqs).astype(np.float32),
            scale.astype(np.float32))


def build_R():
    R = np.zeros((64, 64), dtype=np.float32)
    for e in range(32):
        R[e, 32 + e] = -1.0
        R[32 + e, e] = 1.0
    R2 = np.zeros((128, 128), dtype=np.float32)
    R2[:64, :64] = R
    R2[64:, 64:] = R
    return R2


class K:
    def __init__(self):
        self.nc = bass.Bass("TRN2", target_bir_lowering=False)
        self.stack = ExitStack()
        self.E = self.stack.enter_context
        self.p = Prog(self.nc)
        self.ins = {}
        self.in_aps = {}
        self.psum = None
        self._psi = 0
        self.fconst_vals = []
        self._fconst_map = {}
        self._fc_buf = None
        self.p_ctx = {}

    def fconst(self, val, parts=1):
        val = float(val)
        if val not in self._fconst_map:
            idx = len(self.fconst_vals)
            assert idx < 64
            self.fconst_vals.append(val)
            self._fconst_map[val] = idx
        idx = self._fconst_map[val]
        return self._fc_buf.ap[0:parts, idx:idx + 1]

    def inp(self, name, shape, dt=F32):
        if name not in self.in_aps:
            self.ins[name] = (tuple(shape), dt)
            self.in_aps[name] = self.nc.dram_tensor(name, list(shape), dt, kind="ExternalInput").ap()
        return self.in_aps[name]

    def outp(self, name, shape, dt=F32):
        return self.nc.dram_tensor(name, list(shape), dt, kind="ExternalOutput").ap()

    def sb(self, name, shape, dt):
        return Buf(self.E(self.nc.sbuf_tensor(name, list(shape), dt)).ap(), tag=name)

    def dram(self, name, shape, dt=F32, shared=False):
        return Buf(self.nc.dram_tensor(name, list(shape), dt,
                                       addr_space=("Shared" if shared else "Local")).ap(), tag=name)

    def init_psum(self):
        self.psum = [Buf(self.E(self.nc.psum_tensor(f"ps{i}", [P, 512], F32)).ap(), tag=f"ps{i}")
                     for i in range(8)]

    def bank(self):
        b = self.psum[self._psi % 8]
        self._psi += 1
        return b


def stage_attn_a(k: K, xo: Buf, C):
    nc, p = k.nc, k.p
    TT = 512
    ones = C["ones"]
    _stk = ExitStack()

    def _sb(name, shape, dt):
        return Buf(_stk.enter_context(nc.sbuf_tensor(name, list(shape), dt)).ap(), tag=name)

    s_f = _sb("a_s", [P, NCH * TT], F32)
    wbt = _sb("a_wbt", [P, NCH * D], BF16)
    wbft = _sb("a_wbft", [P, NCH * D], BF16)
    for wname, dstb in (("aWBt", wbt), ("aWBft", wbft)):
        src = k.inp(wname, [P, NCH * D])
        for i in range(2):
            sl = slice(i * 4096, (i + 1) * 4096)
            ld = p.dma(s_f.ap[:, :], src[:, sl], s_f.wd())
            s_f.wr(ld, reset=True)
            ct = p.task("pool", s_f.rd() + dstb.wd(),
                        lambda e, d=dstb.ap, s=s_f.ap, sl=sl: e.tensor_copy(d[:, sl], s[:, :]))
            s_f.r(ct)
            dstb.wr(ct)
    w12 = _sb("a_w12", [P, 2 * NCH], BF16)
    w12f = _sb("a_w12f", [P, 2 * NCH], F32)
    w12f.wr(p.dma(w12f.ap[:, :], k.inp("aW12", [P, 2 * NCH]), []))
    t = p.task("dve", w12f.rd() + w12.wd(), lambda e: e.tensor_copy(w12.ap[:, :], w12f.ap[:, :]))
    w12f.r(t); w12.wr(t)
    bvec = _sb("a_b", [1, 1], F32)
    bvec.wr(p.dma(bvec.ap[:, :], k.inp("ab", [1, 1]), []))
    flags = _sb("a_flags", [P, 2], F32)
    flags.wr(p.dma(flags.ap[:, :], k.inp("flags", [P, 2]), []))

    TH = T + 1
    ub = [_sb(f"a_u{i}", [P, NCH * TT], F32) for i in range(2)]
    uhalo = _sb("a_uh", [P, NCH], F32)
    uhalo.wr(p.dma(uhalo.ap[:, :].rearrange("p c -> p c 1" if False else "p (c o) -> p c o", o=1),
                   k.inp("a_uhalo", [P, NCH, 1]), []))
    xov = xo.ap.rearrange("p (c t) -> p c t", c=NCH)

    nx_bf = _sb("a_nxbf", [P, NCH * TH], BF16)
    nxbv = nx_bf.ap.rearrange("p (c t) -> p c t", c=NCH)
    nx32 = [_sb(f"a_nx32_{i}", [P, NCH * TT], F32) for i in range(2)]
    scr_bf = _sb("a_scrbf", [P, NCH * TT], BF16)
    sq_bf = _sb("a_sqbf", [P, NCH * TT], BF16)
    rows = _sb("a_rows", [1, 4 * TT], F32)
    rows_bf = _sb("a_rowsbf", [1, 4 * TT], BF16)
    xf_bf = _sb("a_xf", [P, NCH * TT], BF16)
    xb_bf = _sb("a_xb", [P, NCH * TT], BF16)
    g_bf = _sb("a_g", [1, TT], BF16)
    fixbuf = _sb("a_fix", [P, NCH], F32)

    ntile = T // TT
    tiles = [(i * TT, TT) for i in range(ntile)] + [(T, 1)]

    def emit_ln(kk):
        t0, w = tiles[kk]
        if t0 < T:
            u = ub[kk % 2]
            src = u.ap[:, :NCH * w].rearrange("p (c t) -> p c t", c=NCH)
            ldu = p.dma(src, xov[:, :, t0:t0 + w], xo.rd() + u.wd())
            u.wr(ldu, reset=True)
        else:
            u = uhalo
            src = uhalo.ap.rearrange("p (c o) -> p c o", o=1)
        sbv = scr_bf.ap[:, :NCH * w].rearrange("p (c t) -> p c t", c=NCH)
        tc = p.task("pool", u.rd() + scr_bf.wd(), lambda e, d=sbv, s=src: e.tensor_copy(d, s))
        scr_bf.wr(tc, reset=True); u.r(tc)
        sqv = sq_bf.ap[:, :NCH * w].rearrange("p (c t) -> p c t", c=NCH)
        tq = p.task("act", [tc] + sq_bf.wd(), lambda e, d=sqv, s=sbv: e.activation(d, s, AF.Square))
        sq_bf.wr(tq, reset=True); scr_bf.r(tq)
        bsum, bsq = k.bank(), k.bank()

        def mm_s(e, sbv=sbv, bs=bsum.ap, w=w):
            last = None
            for ci in range(NCH):
                last = e.matmul(bs[0:1, :w], ones.ap[:, 0:1], sbv[:, ci, :], start=(ci == 0), stop=(ci == NCH - 1))
            return last
        ts = p.task("pe", [tc] + bsum.wd() + ones.rd(), mm_s)
        bsum.wr(ts, reset=True); scr_bf.r(ts); ones.r(ts)

        def mm_q(e, sqv=sqv, bq=bsq.ap, w=w):
            last = None
            for ci in range(NCH):
                last = e.matmul(bq[0:1, :w], ones.ap[:, 0:1], sqv[:, ci, :], start=(ci == 0), stop=(ci == NCH - 1))
            return last
        tq2 = p.task("pe", [tq] + bsq.wd() + ones.rd(), mm_q)
        bsq.wr(tq2, reset=True); sq_bf.r(tq2); ones.r(tq2)

        mu = rows.ap[:, 0:w]; m2 = rows.ap[:, TT:TT + w]
        var = rows.ap[:, 2 * TT:2 * TT + w]; rstd = rows.ap[:, 3 * TT:3 * TT + w]
        t1 = p.task("dve", [ts] + rows.wd() + bsum.wd(), lambda e: e.tensor_scalar_mul(mu, bsum.ap[0:1, :w], 1.0 / D))
        bsum.r(t1)
        t2 = p.task("dve", [tq2, t1] + bsq.wd(), lambda e: e.tensor_scalar_mul(m2, bsq.ap[0:1, :w], 1.0 / D))
        bsq.r(t2)
        t3 = p.task("dve", [t2], lambda e: e.tensor_tensor(var, mu, mu, ALU.mult))
        t4 = p.task("dve", [t3], lambda e: e.tensor_tensor(var, m2, var, ALU.subtract))
        t5 = p.task("act", [t4], lambda e, b=k.fconst(LN_EPS): e.activation(rstd, var, AF.Ln, bias=b))
        t6 = p.task("act", [t5], lambda e: e.activation(rstd, rstd, AF.Exp, scale=-0.5))
        rows.wr(t6, reset=True)
        t8 = p.task("dve", [t6], lambda e: e.tensor_tensor(mu, mu, rstd, ALU.mult))
        t9 = p.task("dve", [t8], lambda e: e.tensor_scalar_mul(mu, mu, -1.0))
        rows.wr(t9)
        rh = rows_bf.ap[:, 0:w]; rl = rows_bf.ap[:, TT:TT + w]
        mh = rows_bf.ap[:, 2 * TT:2 * TT + w]; ml = rows_bf.ap[:, 3 * TT:3 * TT + w]
        c1 = p.task("dve", [t9] + rows_bf.wd(), lambda e: e.tensor_copy(rh, rstd))
        c2 = p.task("dve", [c1], lambda e: e.tensor_tensor(rl, rstd, rh, ALU.subtract))
        c3 = p.task("dve", [c2], lambda e: e.tensor_copy(mh, mu))
        c4 = p.task("dve", [c3], lambda e: e.tensor_tensor(ml, mu, mh, ALU.subtract))
        rows_bf.wr(c4, reset=True); rows.r(c4)
        br1, br2 = k.bank(), k.bank()
        tr1 = p.task("pe", [c4] + br1.wd() + ones.rd(), lambda e: (
            e.matmul(br1.ap[:, :w], ones.ap[0:1, :], rh, start=True, stop=False),
            e.matmul(br1.ap[:, :w], ones.ap[0:1, :], rl, start=False, stop=True))[-1])
        tr2 = p.task("pe", [c4] + br2.wd() + ones.rd(), lambda e: (
            e.matmul(br2.ap[:, :w], ones.ap[0:1, :], mh, start=True, stop=False),
            e.matmul(br2.ap[:, :w], ones.ap[0:1, :], ml, start=False, stop=True))[-1])
        br1.wr(tr1, reset=True); br2.wr(tr2, reset=True); rows_bf.r(tr2); ones.r(tr1); ones.r(tr2)
        if t0 < T:
            db = nx32[kk % 2]
            dv = db.ap[:, :NCH * w].rearrange("p (c t) -> p c t", c=NCH)
            tA = p.task("dve", [tr1] + u.rd() + db.wd(),
                        lambda e: e.tensor_tensor(dv, src, br1.ap[:, None, 0:w].broadcast_to([P, NCH, w]), ALU.mult))
            tB = p.task("dve", [tA, tr2],
                        lambda e: e.tensor_tensor(dv, dv, br2.ap[:, None, 0:w].broadcast_to([P, NCH, w]), ALU.add))
            db.wr(tB, reset=True); br1.r(tA); br2.r(tB); u.r(tA)
            tC = p.task("pool", [tB] + nx_bf.wd(),
                        lambda e: e.tensor_copy(nxbv[:, :, t0:t0 + w], dv))
            nx_bf.wr(tC); db.r(tC)
            if _A_PART == 1:
                stn = p.dma(xov[:, :, t0:t0 + w], dv, [tB] + xo.wd())
                xo.wr(stn); db.r(stn)
        else:
            dsth = nxbv[:, :, T:T + 1]
            tA = p.task("dve", [tr1] + u.rd() + nx_bf.wd(),
                        lambda e: e.tensor_tensor(dsth, src, br1.ap[:, None, 0:1].broadcast_to([P, NCH, 1]), ALU.mult))
            tB = p.task("dve", [tA, tr2],
                        lambda e: e.tensor_tensor(dsth, dsth, br2.ap[:, None, 0:1].broadcast_to([P, NCH, 1]), ALU.add))
            nx_bf.wr(tB); br1.r(tA); br2.r(tB); u.r(tA)

    wbtv = wbt.ap.rearrange("p (c e) -> p c e", c=NCH)
    wbftv = wbft.ap.rearrange("p (c e) -> p c e", c=NCH)

    def emit_proj(kk):
        t0, w = tiles[kk]
        nx32b = nx32[kk % 2]
        nx32v = nx32b.ap.rearrange("p (c t) -> p c t", c=NCH)
        for eo in range(NCH):
            bxf, bxb = k.bank(), k.bank()

            def mm_e(e, eo=eo, bf=bxf.ap, bb=bxb.ap):
                last = None
                for ci in range(NCH):
                    last = e.matmul(bf[:, :w], wbtv[:, ci, eo * P:(eo + 1) * P], nxbv[:, ci, t0:t0 + w],
                                    start=(ci == 0), stop=(ci == NCH - 1))
                for ci in range(NCH):
                    last = e.matmul(bb[:, :w], wbftv[:, ci, eo * P:(eo + 1) * P], nxbv[:, ci, t0 + 1:t0 + 1 + w],
                                    start=(ci == 0), stop=(ci == NCH - 1))
                return last
            tmm = p.task("pe", nx_bf.rd() + bxf.wd() + bxb.wd() + wbt.rd() + wbft.rd(), mm_e)
            bxf.wr(tmm, reset=True); bxb.wr(tmm, reset=True)
            nx_bf.r(tmm); wbt.r(tmm); wbft.r(tmm)
            xfv = xf_bf.ap[:, eo * TT:eo * TT + w]
            xbv = xb_bf.ap[:, eo * TT:eo * TT + w]
            sv = s_f.ap[:, eo * TT:eo * TT + w]
            e3 = p.task("dve", [tmm] + s_f.wd() + bxf.wd(), lambda e, d=sv, s=bxf.ap: e.tensor_copy(d, s[:, :w]))
            bxf.r(e3)
            e4 = p.task("dve", [e3] + bxb.wd(), lambda e, d=sv, s=bxb.ap: e.tensor_tensor(d, d, s[:, :w], ALU.add))
            s_f.wr(e4)
            bxb.r(e4)
            e1 = p.task("act", [tmm] + xf_bf.wd() + bxf.wd(), lambda e, d=xfv, s=bxf.ap: e.copy(d, s[:, :w]))
            bxf.r(e1)
            e2 = p.task("act", [tmm, e1] + xb_bf.wd() + bxb.wd(), lambda e, d=xbv, s=bxb.ap: e.copy(d, s[:, :w]))
            xf_bf.wr(e1); xb_bf.wr(e2)
            bxb.r(e2)
        if _A_PART == 2:
            for eo in range(NCH):
                sv = s_f.ap[:, eo * TT:eo * TT + w]
                stn = p.dma(xov[:, eo, t0:t0 + w], sv, s_f.rd() + xo.wd())
                xo.wr(stn); s_f.r(stn)
            return
        if kk == 0:
            t_a = p.task("dve", xf_bf.rd() + nx_bf.rd() + fixbuf.wd(),
                         lambda e: [e.tensor_tensor(fixbuf.ap[:, eo:eo + 1], nxbv[:, eo, 0:1],
                                                    xf_bf.ap[:, eo * TT:eo * TT + 1], ALU.subtract)
                                    for eo in range(NCH)][-1])
            t_b = p.task("dve", [t_a] + flags.rd(),
                         lambda e: e.tensor_scalar_mul(fixbuf.ap[:, :], fixbuf.ap[:, :], flags.ap[:, 0:1]))
            t_c = p.task("dve", [t_b] + xf_bf.wd(),
                         lambda e: [e.tensor_tensor(xf_bf.ap[:, eo * TT:eo * TT + 1],
                                                    xf_bf.ap[:, eo * TT:eo * TT + 1],
                                                    fixbuf.ap[:, eo:eo + 1], ALU.add)
                                    for eo in range(NCH)][-1])
            t_d = p.task("dve", [t_b] + s_f.wd(),
                         lambda e: [e.tensor_tensor(s_f.ap[:, eo * TT:eo * TT + 1],
                                                    s_f.ap[:, eo * TT:eo * TT + 1],
                                                    fixbuf.ap[:, eo:eo + 1], ALU.add)
                                    for eo in range(NCH)][-1])
            s_f.wr(t_d); xf_bf.wr(t_c); fixbuf.wr(t_b, reset=True)
            fixbuf.r(t_d); fixbuf.r(t_c); flags.r(t_b); nx_bf.r(t_a)
        if kk == ntile - 1:
            wl = w - 1
            t_a = p.task("dve", xb_bf.rd() + nx_bf.rd() + fixbuf.wd(),
                         lambda e: [e.tensor_tensor(fixbuf.ap[:, eo:eo + 1], nxbv[:, eo, t0 + wl:t0 + wl + 1],
                                                    xb_bf.ap[:, eo * TT + wl:eo * TT + wl + 1], ALU.subtract)
                                    for eo in range(NCH)][-1])
            t_b = p.task("dve", [t_a] + flags.rd(),
                         lambda e: e.tensor_scalar_mul(fixbuf.ap[:, :], fixbuf.ap[:, :], flags.ap[:, 1:2]))
            t_c = p.task("dve", [t_b] + xb_bf.wd(),
                         lambda e: [e.tensor_tensor(xb_bf.ap[:, eo * TT + wl:eo * TT + wl + 1],
                                                    xb_bf.ap[:, eo * TT + wl:eo * TT + wl + 1],
                                                    fixbuf.ap[:, eo:eo + 1], ALU.add)
                                    for eo in range(NCH)][-1])
            t_d = p.task("dve", [t_b] + s_f.wd(),
                         lambda e: [e.tensor_tensor(s_f.ap[:, eo * TT + wl:eo * TT + wl + 1],
                                                    s_f.ap[:, eo * TT + wl:eo * TT + wl + 1],
                                                    fixbuf.ap[:, eo:eo + 1], ALU.add)
                                    for eo in range(NCH)][-1])
            s_f.wr(t_d); xb_bf.wr(t_c); fixbuf.wr(t_b, reset=True)
            fixbuf.r(t_d); fixbuf.r(t_c); flags.r(t_b); nx_bf.r(t_a)
        bg = k.bank()

        def mm_g(e):
            last = None
            for eo in range(NCH):
                last = e.matmul(bg.ap[0:1, :w], w12.ap[:, 2 * eo:2 * eo + 1],
                                xf_bf.ap[:, eo * TT:eo * TT + w], start=(eo == 0), stop=False)
                last = e.matmul(bg.ap[0:1, :w], w12.ap[:, 2 * eo + 1:2 * eo + 2],
                                xb_bf.ap[:, eo * TT:eo * TT + w], start=False, stop=(eo == NCH - 1))
            return last
        tg = p.task("pe", xf_bf.rd() + xb_bf.rd() + bg.wd() + w12.rd(), mm_g)
        bg.wr(tg, reset=True); xf_bf.r(tg); xb_bf.r(tg); w12.r(tg)
        tsg = p.task("act", [tg] + g_bf.wd() + bvec.rd() + bg.wd(),
                     lambda e: e.activation(g_bf.ap[:, :w], bg.ap[0:1, :w], AF.Sigmoid, bias=bvec.ap[0:1, 0:1]))
        g_bf.wr(tsg, reset=True); bg.r(tsg); bvec.r(tsg)
        bgr = k.bank()
        tgr = p.task("pe", [tsg] + bgr.wd() + ones.rd(),
                     lambda e: e.matmul(bgr.ap[:, :w], ones.ap[0:1, :], g_bf.ap[:, :w], start=True, stop=True))
        bgr.wr(tgr, reset=True); g_bf.r(tgr); ones.r(tgr)
        utile = ub[kk % 2]
        uvt = utile.ap[:, :NCH * w].rearrange("p (c t) -> p c t", c=NCH)
        for eo in range(NCH):
            sv = s_f.ap[:, eo * TT:eo * TT + w]
            nv = nx32v[:, eo, :w]
            uvv = uvt[:, eo, :w]
            tD1 = p.task("dve", [tgr] + s_f.rd() + nx32b.rd(),
                         lambda e, sv=sv, nv=nv: e.tensor_tensor(sv, sv, nv, ALU.subtract))
            tD2 = p.task("dve", [tD1], lambda e, sv=sv: e.tensor_tensor(sv, sv, bgr.ap[:, :w], ALU.mult))
            tD3 = p.task("dve", [tD2], lambda e, sv=sv, nv=nv: e.tensor_tensor(sv, sv, nv, ALU.add))
            tD4 = p.task("dve", [tD3] + utile.rd(), lambda e, sv=sv, uvv=uvv: e.tensor_tensor(sv, sv, uvv, ALU.add))
            s_f.wr(tD4); bgr.r(tD2); nx32b.r(tD3); utile.r(tD4)
            st = p.dma(xov[:, eo, t0:t0 + w], sv, [tD4] + xo.wd())
            xo.wr(st)
            s_f.r(st)

    emit_ln(0)
    for kk in range(1, len(tiles)):
        emit_ln(kk)
        if _A_PART >= 2:
            emit_proj(kk - 1)
    _stk.close()
    p.fence()


def to_fm(u2d):
    t2 = u2d.shape[0]
    v = np.ascontiguousarray(u2d.T.astype(np.float32)).reshape(NCH, P, t2)
    return np.ascontiguousarray(v.transpose(1, 0, 2)).reshape(P, NCH * t2)


def from_fm(arr, t2):
    v = arr.reshape(P, NCH, t2).transpose(1, 0, 2).reshape(D, t2)
    return np.ascontiguousarray(v.T)


def lhsT_fm(Wde):
    dd, ee = Wde.shape
    assert dd == D
    v = np.ascontiguousarray(Wde.astype(np.float32)).reshape(NCH, P, ee)
    return np.ascontiguousarray(v.transpose(1, 0, 2)).reshape(P, NCH * ee)


def host_inputs(full):
    u = np.asarray(full["L0_attn_a_u"], dtype=np.float32)
    WB = np.asarray(full["L0_attn_a_WB"], dtype=np.float32)
    WBf = np.asarray(full["L0_attn_a_WBf"], dtype=np.float32)
    w1 = np.asarray(full["L0_attn_a_w1"], dtype=np.float32)[0]
    w2 = np.asarray(full["L0_attn_a_w2"], dtype=np.float32)[0]
    b = np.asarray(full["L0_attn_a_b"], dtype=np.float32)

    aWBt = lhsT_fm(np.ascontiguousarray(WB.T))
    aWBft = lhsT_fm(np.ascontiguousarray(WBf.T))
    aW12 = np.zeros((P, 2 * NCH), dtype=np.float32)
    for c in range(NCH):
        aW12[:, 2 * c] = w1[c * P:(c + 1) * P]
        aW12[:, 2 * c + 1] = w2[c * P:(c + 1) * P]

    Wqkv = np.asarray(full["L0_ffn_Wqkv"], dtype=np.float32)
    Wo6 = np.asarray(full["L0_ffn_Wo"], dtype=np.float32)
    W16 = np.asarray(full["L0_ffn_W1"], dtype=np.float32)
    W26 = np.asarray(full["L0_ffn_W2"], dtype=np.float32)
    b16 = np.asarray(full["L0_ffn_b1"], dtype=np.float32)
    b26 = np.asarray(full["L0_ffn_b2"], dtype=np.float32)
    lng6 = np.asarray(full["L0_ffn_ln_g"], dtype=np.float32)
    lnb6 = np.asarray(full["L0_ffn_ln_b"], dtype=np.float32)
    qs6 = np.asarray(full["L0_ffn_q_scale"], dtype=np.float32)
    ks6 = np.asarray(full["L0_ffn_k_scale"], dtype=np.float32)

    def lhsT_g(W, nchunks):
        dd, ee = W.shape
        v = np.ascontiguousarray(W).reshape(nchunks, P, ee)
        return np.ascontiguousarray(v.transpose(1, 0, 2)).reshape(P, nchunks * ee)

    fWq = np.concatenate([lhsT_g(Wqkv[l][:, 0:D], NCH) for l in range(6)], 1)
    fWk = np.concatenate([lhsT_g(Wqkv[l][:, D:2 * D], NCH) for l in range(6)], 1)
    fWv = np.concatenate([lhsT_g(Wqkv[l][:, 2 * D:3 * D], NCH) for l in range(6)], 1)
    fWo = np.concatenate([lhsT_g(Wo6[l], NCH) for l in range(6)], 1)
    MP = MCH * P
    f1a_l, f1g_l, f2_l, fb_l = [], [], [], []
    for l in range(6):
        W1a = np.zeros((D, MP), np.float32); W1a[:, :2730] = W16[l][:, :2730]
        W1g = np.zeros((D, MP), np.float32); W1g[:, :2730] = W16[l][:, 2730:]
        a4 = W1a.reshape(NCH, P, MCH, P).transpose(1, 2, 0, 3).reshape(P, MCH * NCH * P)
        g4 = W1g.reshape(NCH, P, MCH, P).transpose(1, 2, 0, 3).reshape(P, MCH * NCH * P)
        f1a_l.append(np.ascontiguousarray(a4)); f1g_l.append(np.ascontiguousarray(g4))
        W2p = np.zeros((MP, D), np.float32); W2p[:2730] = W26[l]
        f2_l.append(lhsT_g(W2p, MCH))
        b1a = np.zeros((MCH * P,), np.float32); b1a[:2730] = b16[l][:2730]
        b1g = np.zeros((MCH * P,), np.float32); b1g[:2730] = b16[l][2730:]
        cols = np.zeros((P, 2 * MCH + 3 * NCH + 4), np.float32)
        cols[:, 0:MCH] = b1a.reshape(MCH, P).T
        cols[:, MCH:2 * MCH] = b1g.reshape(MCH, P).T
        cols[:, 2 * MCH:2 * MCH + NCH] = b26[l].reshape(NCH, P).T
        cols[:, 2 * MCH + NCH:2 * MCH + 2 * NCH] = lng6[l].reshape(NCH, P).T
        cols[:, 2 * MCH + 2 * NCH:2 * MCH + 3 * NCH] = lnb6[l].reshape(NCH, P).T
        e64 = np.arange(P) % 64
        cols[:, 2 * MCH + 3 * NCH + 0] = qs6[l][e64]
        cols[:, 2 * MCH + 3 * NCH + 1] = qs6[l][(e64 + 32) % 64]
        cols[:, 2 * MCH + 3 * NCH + 2] = ks6[l][e64]
        cols[:, 2 * MCH + 3 * NCH + 3] = ks6[l][(e64 + 32) % 64]
        fb_l.append(cols)
    fW1a = np.concatenate(f1a_l, 1); fW1g = np.concatenate(f1g_l, 1)
    fW2 = np.concatenate(f2_l, 1); fbALL = np.concatenate(fb_l, 1)

    cosT, sinT, scaleT = rotary_tables(4096, 64, 256)
    tq1 = cosT * scaleT * 8.0
    tq2 = sinT * scaleT * 8.0
    tk1 = cosT / scaleT
    tk2 = sinT / scaleT
    e64 = np.arange(P) % 64

    def tab_for(half):
        base = half * T
        out = np.zeros((P, 4 * (T + 256)), np.float32)
        for ti, tab in enumerate((tq1, tq2, tk1, tk2)):
            for j in range(T + 256):
                t = base + j - 256
                if t < 0:
                    continue
                out[:, ti * (T + 256) + j] = tab[t][e64]
        return out.astype(BFD)
    tab_h = [tab_for(0), tab_for(1)]

    def mk_masks(first):
        out = np.zeros((P, 2 * 4 * 256), dtype=np.float32)
        for mtype in range(2):
            for jc in range(4):
                for j in range(P):
                    for i in range(256):
                        delta = 256 + i - (jc * 128 + j)
                        ok = 0 <= delta <= 256
                        if mtype == 0 and first and jc < 2:
                            ok = False
                        out[j, (mtype * 4 + jc) * 256 + i] = 1.0 if ok else 0.0
        return out.astype(BFD)
    masks_first = mk_masks(True)
    masks_rest = mk_masks(False)

    R2 = build_R()
    fRm = np.ascontiguousarray(R2.T).astype(BFD)
    fblk2 = np.zeros((P, 8 * 16), np.float32)
    for hp in range(8):
        fblk2[0:64, hp * 16 + 2 * hp] = 1.0
        fblk2[64:128, hp * 16 + 2 * hp + 1] = 1.0
    fblk2 = fblk2.astype(BFD)
    fselh = np.zeros((16, 8 * P), np.float32)
    for hp in range(8):
        fselh[2 * hp, hp * P:hp * P + 64] = 1.0
        fselh[2 * hp + 1, hp * P + 64:(hp + 1) * P] = 1.0
    fselh = fselh.astype(BFD)

    mWin_f = np.asarray(full.get("L0_attn_b_W_in", np.zeros((D, 4096))), dtype=np.float32)
    mXp_f = np.asarray(full.get("L0_attn_b_W_xproj", np.zeros((2048, 96))), dtype=np.float32)
    mWdt_f = np.asarray(full.get("L0_attn_b_W_dt", np.zeros((64, 2048))), dtype=np.float32)
    mWout_f = np.asarray(full.get("L0_attn_b_W_out", np.zeros((2048, D))), dtype=np.float32)
    mcw_f = np.asarray(full.get("L0_attn_b_conv_w", np.zeros((2048, 1, 4))), dtype=np.float32)
    mcb_f = np.asarray(full.get("L0_attn_b_conv_b", np.zeros((2048,))), dtype=np.float32)
    mbdt_f = np.asarray(full.get("L0_attn_b_b_dt", np.zeros((2048,))), dtype=np.float32)
    mD_f = np.asarray(full.get("L0_attn_b_D", np.zeros((2048,))), dtype=np.float32)
    mAlog_f = np.asarray(full.get("L0_attn_b_A_log", np.zeros((2048, 16))), dtype=np.float32)
    mlng_f = np.asarray(full.get("L0_attn_b_ln_g", np.zeros((D,))), dtype=np.float32)
    mlnb_f = np.asarray(full.get("L0_attn_b_ln_b", np.zeros((D,))), dtype=np.float32)

    def mk_mamba(half):
        ch0 = half * 1024
        winsel = np.concatenate([mWin_f[:, ch0:ch0 + 1024],
                                 mWin_f[:, 2048 + ch0:2048 + ch0 + 1024]], axis=1)
        mWin = np.ascontiguousarray(winsel.reshape(NCH, P, 2048).transpose(1, 0, 2)).reshape(P, NCH * 2048)
        xps = mXp_f[ch0:ch0 + 1024]
        mXp = np.ascontiguousarray(xps.reshape(NCH, P, 96).transpose(1, 0, 2)).reshape(P, NCH * 96)
        mWdt = np.zeros((P, NCH * P), np.float32)
        mWdt[0:64, :] = mWdt_f[:, ch0:ch0 + 1024]
        wos = mWout_f[ch0:ch0 + 1024]
        mWout = np.ascontiguousarray(wos.reshape(NCH, P, 1024).transpose(1, 0, 2)).reshape(P, NCH * 1024)
        vec = np.zeros((P, 200), np.float32)
        cws = mcw_f[ch0:ch0 + 1024, 0, :]
        vec[:, 0:32] = cws.reshape(NCH, P, 4).transpose(1, 0, 2).reshape(P, 32)
        vec[:, 32:40] = mcb_f[ch0:ch0 + 1024].reshape(NCH, P).T
        vec[:, 40:48] = mbdt_f[ch0:ch0 + 1024].reshape(NCH, P).T
        vec[:, 48:56] = mD_f[ch0:ch0 + 1024].reshape(NCH, P).T
        vec[:, 56:184] = mAlog_f[ch0:ch0 + 1024].reshape(NCH, P, 16).transpose(1, 0, 2).reshape(P, 128)
        vec[:, 184:192] = mlng_f.reshape(NCH, P).T
        vec[:, 192:200] = mlnb_f.reshape(NCH, P).T
        selbc = np.zeros((32, 32 * P), np.float32)
        for j in range(32):
            selbc[j, j * P:(j + 1) * P] = 1.0
        return dict(mWin=mWin, mXp=mXp, mWdt=mWdt, mWout=mWout,
                    mvec=vec, mselbc=selbc.astype(BFD))
    mamba_maps = [mk_mamba(0), mk_mamba(1)]

    maps = []
    for core in range(8):
        bidx, half = core // 2, core % 2
        rows = u[bidx, half * T:(half + 1) * T]
        fl = np.zeros((P, 2), dtype=np.float32)
        fl[:, 0] = 1.0 if half == 0 else 0.0
        fl[:, 1] = 1.0 if half == 1 else 0.0
        m = {
            "x_fm": to_fm(rows),
            "aWBt": aWBt, "aWBft": aWBft, "aW12": aW12,
            "ab": b.reshape(1, 1).astype(np.float32),
            "flags": fl,
            "c_ones_in": np.ones((P, P), dtype=BFD),
            "fWq": fWq, "fWk": fWk, "fWv": fWv, "fWo": fWo,
            "fW1a": fW1a, "fW1g": fW1g, "fW2": fW2, "fb": fbALL,
            "ftab": tab_h[half],
            "fmasks": masks_first if half == 0 else masks_rest,
            "fRm": fRm, "fblk2": fblk2, "fselh": fselh,
            "fnotfirst": np.full((P, 1), 0.0 if half == 0 else 1.0, dtype=np.float32),
        }
        hrow = u[bidx, T] if half == 0 else np.zeros((D,), dtype=np.float32)
        m["a_uhalo"] = np.ascontiguousarray(
            hrow.astype(np.float32).reshape(NCH, P).transpose(1, 0)[:, :, None])
        m.update(mamba_maps[half])
        maps.append(m)
    return maps


_N_STAGES = 4
_A_PART = 3
_A_SUB = 9
_F_LAYERS = 6
_F_ATTN = True
_F_MLP = True
_F_ASCHED = 4
_F_AWIN = 5
_F_TRD = 1
_F_SCH2 = 2


class Xchg:
    def __init__(self, k):
        self.k = k
        self.bar_in = k.dram("bar_in", [1, 4])
        self.bar_out = k.dram("bar_out", [1, 8])
        self.last_bar = None
        self.last_reads = []
        t0 = k.p.dma(self.bar_in.ap[:, :], k._fc_buf.ap[0:1, 0:4], k._fc_buf.rd())
        self.bar_in.wr(t0)

    def send_recv(self, sh_pair, src_fn, dst_fn, wdeps, rdeps=()):
        k, p = self.k, self.k.p
        if not hasattr(self, "xcount"):
            self.xcount = 0
        sh = sh_pair[self.xcount % 2]
        self.xcount += 1

        wdeps = list(wdeps) + sh.wd() + ([self.last_bar] if self.last_bar else [])
        si = p.dma_rr % NDMASEM
        p.dma_rr += 1
        if p.dma_use[si] > 0:
            wdeps.append((f"dq{si}", 16 * p.dma_use[si]))
        p.dma_use[si] += 1
        wtok = (f"dq{si}", 16 * p.dma_use[si])
        p.tasks["sp"].append((p._merge("sp", wdeps),
                              lambda e: e.dma_start(out=src_fn(k.p_ctx, sh.ap)[0], in_=src_fn(k.p_ctx, sh.ap)[1]), wtok))
        sh.wr(wtok)
        bar = p.barrier(self.bar_in.ap[:, :], self.bar_out.ap[:, :],
                        [wtok] + self.last_reads + ([self.last_bar] if self.last_bar else []))
        self.last_bar = bar
        rdeps2 = [bar] + list(rdeps)
        si = p.dma_rr % NDMASEM
        p.dma_rr += 1
        if p.dma_use[si] > 0:
            rdeps2.append((f"dq{si}", 16 * p.dma_use[si]))
        p.dma_use[si] += 1
        rtok = (f"dq{si}", 16 * p.dma_use[si])
        p.tasks["sp"].append((p._merge("sp", rdeps2),
                              lambda e: e.dma_start(out=dst_fn(k.p_ctx, sh.ap)[0], in_=dst_fn(k.p_ctx, sh.ap)[1]), rtok))
        sh.r(rtok)
        self.last_reads = [rtok]
        return rtok


def _ds1(ap, slot):
    v = ap[bass.ds(slot, 1)]
    return v.squeeze(0)


def xc_dma_dyn(xc, mk_pair, deps):
    k, p = xc.k, xc.k.p
    si = p.dma_rr % NDMASEM
    p.dma_rr += 1
    deps = list(deps)
    if p.dma_use[si] > 0:
        deps.append((f"dq{si}", 16 * p.dma_use[si]))
    p.dma_use[si] += 1
    tok = (f"dq{si}", 16 * p.dma_use[si])

    def emit(e, mk_pair=mk_pair):
        try:
            o, i = mk_pair(k.p_ctx)
            return e.dma_start(out=o, in_=i)
        except Exception:
            import traceback as _tb
            _tb.print_exc()
            try:
                o, i = mk_pair(k.p_ctx)
                print("OUT AP:", o)
                print("IN AP:", i)
            except Exception:
                pass
            raise RuntimeError(f"xc_dma_dyn failed for fn defined at line "
                               f"{mk_pair.__code__.co_firstlineno}")
    p.tasks["sp"].append((p._merge("sp", deps), emit, tok))
    return tok


def xc_bar(xc, deps):
    deps = list(deps) + xc.last_reads + ([xc.last_bar] if xc.last_bar else [])
    b = xc.k.p.barrier(xc.bar_in.ap[:, :], xc.bar_out.ap[:, :], deps)
    xc.last_bar = b
    xc.last_reads = []
    return b


def emit_outer_ln_tile(k, C, xo, xov, dstb, dstv, t0, w, B):
    p = k.p
    ones = C["ones"]
    scr_bf, sq_bf, rows, rows_bf, xo_tile = B["scr"], B["sq"], B["rows"], B["rowsbf"], B["xot"]
    src = xo_tile.ap[:, :NCH * w].rearrange("p (c t) -> p c t", c=NCH)
    ldu = p.dma(src, xov[:, :, t0:t0 + w], xo.rd() + xo_tile.wd())
    xo_tile.wr(ldu, reset=True)
    sbv = scr_bf.ap[:, :NCH * w].rearrange("p (c t) -> p c t", c=NCH)
    tc = p.task("pool", xo_tile.rd() + scr_bf.wd(), lambda e: e.tensor_copy(sbv, src))
    scr_bf.wr(tc, reset=True); xo_tile.r(tc)
    sqv = sq_bf.ap[:, :NCH * w].rearrange("p (c t) -> p c t", c=NCH)
    tq = p.task("act", [tc] + sq_bf.wd(), lambda e: e.activation(sqv, sbv, AF.Square))
    sq_bf.wr(tq, reset=True); scr_bf.r(tq)
    bsum, bsq = k.bank(), k.bank()

    def mm_s(e):
        last = None
        for ci in range(NCH):
            last = e.matmul(bsum.ap[0:1, :w], ones.ap[:, 0:1], sbv[:, ci, :], start=(ci == 0), stop=(ci == NCH - 1))
        return last
    ts = p.task("pe", [tc] + bsum.wd() + ones.rd(), mm_s)
    bsum.wr(ts, reset=True); scr_bf.r(ts); ones.r(ts)

    def mm_q(e):
        last = None
        for ci in range(NCH):
            last = e.matmul(bsq.ap[0:1, :w], ones.ap[:, 0:1], sqv[:, ci, :], start=(ci == 0), stop=(ci == NCH - 1))
        return last
    tq2 = p.task("pe", [tq] + bsq.wd() + ones.rd(), mm_q)
    bsq.wr(tq2, reset=True); sq_bf.r(tq2); ones.r(tq2)
    mu = rows.ap[:, 0:w]; m2 = rows.ap[:, 512:512 + w]
    var = rows.ap[:, 1024:1024 + w]; rstd = rows.ap[:, 1536:1536 + w]
    t1 = p.task("dve", [ts] + rows.wd() + bsum.wd(), lambda e: e.tensor_scalar_mul(mu, bsum.ap[0:1, :w], 1.0 / D))
    bsum.r(t1)
    t2 = p.task("dve", [tq2, t1] + bsq.wd(), lambda e: e.tensor_scalar_mul(m2, bsq.ap[0:1, :w], 1.0 / D))
    bsq.r(t2)
    t3 = p.task("dve", [t2], lambda e: e.tensor_tensor(var, mu, mu, ALU.mult))
    t4 = p.task("dve", [t3], lambda e: e.tensor_tensor(var, m2, var, ALU.subtract))
    t5 = p.task("act", [t4], lambda e, b=k.fconst(LN_EPS): e.activation(rstd, var, AF.Ln, bias=b))
    t6 = p.task("act", [t5], lambda e: e.activation(rstd, rstd, AF.Exp, scale=-0.5))
    rows.wr(t6)
    t8 = p.task("dve", [t6], lambda e: e.tensor_tensor(mu, mu, rstd, ALU.mult))
    t9 = p.task("dve", [t8], lambda e: e.tensor_scalar_mul(mu, mu, -1.0))
    rows.wr(t9)
    rh = rows_bf.ap[:, 0:w]; rl = rows_bf.ap[:, 512:512 + w]
    mh = rows_bf.ap[:, 1024:1024 + w]; ml = rows_bf.ap[:, 1536:1536 + w]
    c1 = p.task("dve", [t9] + rows_bf.wd(), lambda e: e.tensor_copy(rh, rstd))
    c2 = p.task("dve", [c1], lambda e: e.tensor_tensor(rl, rstd, rh, ALU.subtract))
    c3 = p.task("dve", [c2], lambda e: e.tensor_copy(mh, mu))
    c4 = p.task("dve", [c3], lambda e: e.tensor_tensor(ml, mu, mh, ALU.subtract))
    rows_bf.wr(c4, reset=True); rows.r(c4)
    br1, br2 = k.bank(), k.bank()
    tr1 = p.task("pe", [c4] + br1.wd() + ones.rd(), lambda e: (
        e.matmul(br1.ap[:, :w], ones.ap[0:1, :], rh, start=True, stop=False),
        e.matmul(br1.ap[:, :w], ones.ap[0:1, :], rl, start=False, stop=True))[-1])
    tr2 = p.task("pe", [c4] + br2.wd() + ones.rd(), lambda e: (
        e.matmul(br2.ap[:, :w], ones.ap[0:1, :], mh, start=True, stop=False),
        e.matmul(br2.ap[:, :w], ones.ap[0:1, :], ml, start=False, stop=True))[-1])
    br1.wr(tr1, reset=True); br2.wr(tr2, reset=True); rows_bf.r(tr2); ones.r(tr1); ones.r(tr2)
    tA = p.task("dve", [tr1] + xo_tile.rd() + dstb.wd() + br1.wd(),
                lambda e: e.tensor_tensor(dstv, src, br1.ap[:, None, 0:w].broadcast_to([P, NCH, w]), ALU.mult))
    br1.r(tA); xo_tile.r(tA)
    tB2 = p.task("dve", [tA, tr2] + br2.wd(),
                 lambda e: e.tensor_tensor(dstv, dstv, br2.ap[:, None, 0:w].broadcast_to([P, NCH, w]), ALU.add))
    dstb.wr(tB2)
    br2.r(tB2)
    return tB2


def stream_weight_bf16(k, dst, dst_off, src_ap, src_off, n, stgs, piece=4096):
    p = k.p
    for i in range(0, n, piece):
        wp = min(piece, n - i)
        stg = stgs[(i // piece) % len(stgs)]
        ld = p.dma(stg.ap[:, :wp], src_ap[:, src_off + i:src_off + i + wp], stg.wd())
        stg.wr(ld, reset=True)
        ct = p.task("pool", stg.rd() + dst.wd(),
                    lambda e, a=dst.ap[:, dst_off + i:dst_off + i + wp], s=stg.ap[:, :wp]: e.tensor_copy(a, s))
        stg.r(ct)
        dst.wr(ct)


HF = 258
LTF = HF + T
MCH = 22


def stage_ffn(k, xo, C, xc, tag):
    p = k.p
    nc = k.nc
    ones = C["ones"]
    Rm, blk2, selh, masks = C["Rmat"], C["blk2"], C["selh"], C["masks"]
    shx = C["shxp"]
    xov = xo.ap.rearrange("p (c t) -> p c t", c=NCH)

    fWq = k.inp("fWq", [P, 6 * NCH * D]); fWk = k.inp("fWk", [P, 6 * NCH * D])
    fWv = k.inp("fWv", [P, 6 * NCH * D]); fWo = k.inp("fWo", [P, 6 * NCH * D])
    fW1a = k.inp("fW1a", [P, 6 * MCH * NCH * P]); fW1g = k.inp("fW1g", [P, 6 * MCH * NCH * P])
    fW2 = k.inp("fW2", [P, 6 * MCH * D])
    fb = k.inp("fb", [P, 6 * (2 * MCH + 3 * NCH + 4)])
    ftab = k.inp("ftab", [P, 4 * (T + 256)], BF16)
    fnotfirst = k.inp("fnotfirst", [P, 1])

    with ExitStack() as stk:
        E = stk.enter_context

        def sb(name, shape, dt):
            return Buf(E(nc.sbuf_tensor(tag + name, list(shape), dt)).ap(), tag=tag + name)

        p.fence()
        AR = Arena(k, tag + "arena", 120 * 1024, stack=stk)
        xs = sb("xs", [P, NCH * LTF], F32)
        xsv = xs.ap.rearrange("p (c t) -> p c t", c=NCH)
        nf = sb("nf", [P, 1], F32)
        nf.wr(p.dma(nf.ap[:, :], fnotfirst[:, :], []))
        stgs = [sb(f"stg{i}", [P, 1024], F32) for i in range(2)]
        colstg = sb("colstg", [P, 2 * NCH], F32)

        def emit_ln_tiles(A, t0, w, src_of_chunk, dst_of_chunk, gb=None, src_buf=None,
                          dst_buf=None, fix_src=None):
            scr_bf, sq_bf, rows, rows_bf, asc = A["scr"], A["sq"], A["rows"], A["rowsbf"], A["asc"]
            sbv = scr_bf.ap[:, :NCH * w].rearrange("p (c t) -> p c t", c=NCH)

            def emsh(e):
                last = None
                for c in range(NCH):
                    last = e.tensor_copy(sbv[:, c, :], src_of_chunk(c))
                return last
            sdep = src_buf if src_buf is not None else xs
            tc = p.task("pool", sdep.rd() + scr_bf.wd(), emsh)
            scr_bf.wr(tc, reset=True); sdep.r(tc)
            if fix_src is not None:
                tc2 = p.task("pool", [tc] + fix_src[1].rd() + scr_bf.wd(),
                             lambda e: e.tensor_copy(sbv[:, NCH // 2:NCH, 0:1], fix_src[0]))
                scr_bf.wr(tc2); fix_src[1].r(tc2)
                tc = tc2
            sqv = sq_bf.ap[:, :NCH * w].rearrange("p (c t) -> p c t", c=NCH)
            tq = p.task("act", [tc] + sq_bf.wd(), lambda e: e.activation(sqv, sbv, AF.Square))
            sq_bf.wr(tq, reset=True); scr_bf.r(tq)
            bsum, bsq = k.bank(), k.bank()

            def mm_s(e):
                last = None
                for ci in range(NCH):
                    last = e.matmul(bsum.ap[0:1, :w], ones.ap[:, 0:1], sbv[:, ci, :],
                                    start=(ci == 0), stop=(ci == NCH - 1))
                return last
            ts = p.task("pe", [tc] + bsum.wd() + ones.rd(), mm_s)
            bsum.wr(ts, reset=True); scr_bf.r(ts)

            def mm_q(e):
                last = None
                for ci in range(NCH):
                    last = e.matmul(bsq.ap[0:1, :w], ones.ap[:, 0:1], sqv[:, ci, :],
                                    start=(ci == 0), stop=(ci == NCH - 1))
                return last
            tq2 = p.task("pe", [tq] + bsq.wd() + ones.rd(), mm_q)
            bsq.wr(tq2, reset=True); sq_bf.r(tq2)
            mu = rows.ap[:, 0:w]; m2 = rows.ap[:, 512:512 + w]
            var = rows.ap[:, 1024:1024 + w]; rstd = rows.ap[:, 1536:1536 + w]
            t1 = p.task("dve", [ts] + rows.wd() + bsum.wd(), lambda e: e.tensor_scalar_mul(mu, bsum.ap[0:1, :w], 1.0 / D))
            bsum.r(t1)
            t2 = p.task("dve", [tq2, t1] + bsq.wd(), lambda e: e.tensor_scalar_mul(m2, bsq.ap[0:1, :w], 1.0 / D))
            bsq.r(t2)
            t3 = p.task("dve", [t2], lambda e: e.tensor_tensor(var, mu, mu, ALU.mult))
            t4 = p.task("dve", [t3], lambda e: e.tensor_tensor(var, m2, var, ALU.subtract))
            t5 = p.task("act", [t4], lambda e, b=k.fconst(LN_EPS): e.activation(rstd, var, AF.Ln, bias=b))
            t6 = p.task("act", [t5], lambda e: e.activation(rstd, rstd, AF.Exp, scale=-0.5))
            t8 = p.task("dve", [t6], lambda e: e.tensor_tensor(mu, mu, rstd, ALU.mult))
            t9 = p.task("dve", [t8], lambda e: e.tensor_scalar_mul(mu, mu, -1.0))
            rows.wr(t9)
            rh = rows_bf.ap[:, 0:w]; rl = rows_bf.ap[:, 512:512 + w]
            mh = rows_bf.ap[:, 1024:1024 + w]; ml = rows_bf.ap[:, 1536:1536 + w]
            c1 = p.task("dve", [t9] + rows_bf.wd(), lambda e: e.tensor_copy(rh, rstd))
            c2 = p.task("dve", [c1], lambda e: e.tensor_tensor(rl, rstd, rh, ALU.subtract))
            c3 = p.task("dve", [c2], lambda e: e.tensor_copy(mh, mu))
            c4 = p.task("dve", [c3], lambda e: e.tensor_tensor(ml, mu, mh, ALU.subtract))
            rows_bf.wr(c4, reset=True); rows.r(c4)
            br1, br2 = k.bank(), k.bank()
            tr1 = p.task("pe", [c4] + br1.wd() + ones.rd(), lambda e: (
                e.matmul(br1.ap[:, :w], ones.ap[0:1, :], rh, start=True, stop=False),
                e.matmul(br1.ap[:, :w], ones.ap[0:1, :], rl, start=False, stop=True))[-1])
            tr2 = p.task("pe", [c4] + br2.wd() + ones.rd(), lambda e: (
                e.matmul(br2.ap[:, :w], ones.ap[0:1, :], mh, start=True, stop=False),
                e.matmul(br2.ap[:, :w], ones.ap[0:1, :], ml, start=False, stop=True))[-1])
            br1.wr(tr1, reset=True); br2.wr(tr2, reset=True); rows_bf.r(tr2)
            toks = []
            for c in range(NCH):
                tB2 = p.task("dve", [tr1] + scr_bf.rd() + asc.wd() + br1.wd(),
                             lambda e, c=c: e.tensor_tensor(asc.ap[:, :w], sbv[:, c, :], br1.ap[:, :w], ALU.mult))
                br1.r(tB2); scr_bf.r(tB2)
                tC2 = p.task("dve", [tB2] + br2.wd(),
                             lambda e: e.tensor_tensor(asc.ap[:, :w], asc.ap[:, :w], br2.ap[:, :w], ALU.add))
                br2.r(tC2)
                asc.wr(tC2, reset=True)
                ddep = [] if dst_buf is None else dst_buf.wd()
                if gb is None:
                    tD = p.task("dve", [tC2] + ddep,
                                lambda e, c=c: e.tensor_copy(dst_of_chunk(c), asc.ap[:, :w]))
                else:
                    gcol, bcol, gbuf = gb
                    tD = p.task("dve", [tC2] + ddep + gbuf.rd(),
                                lambda e, c=c, gcol=gcol, bcol=bcol: e.tensor_scalar(
                                    dst_of_chunk(c), asc.ap[:, :w], gcol(c), bcol(c), ALU.mult, ALU.add))
                    gbuf.r(tD)
                asc.r(tD)
                if dst_buf is not None:
                    dst_buf.wr(tD)
                toks.append(tD)
            return toks

        if True:
            AR.reset()
            sb2 = AR.alloc
            A = {"scr": sb2("scr", [P, NCH * 512], BF16), "sq": sb2("sq", [P, NCH * 512], BF16),
                 "rows": sb2("rows", [1, 4 * 512], F32), "rowsbf": sb2("rowsbf", [1, 4 * 512], BF16),
                 "asc": sb2("asc", [P, 512], F32), "xot": sb2("xot", [P, NCH * 512], F32)}
            xot = A["xot"]
            for kk in range(4):
                t0 = kk * 512
                xsrc = xot.ap.rearrange("p (c t) -> p c t", c=NCH)
                ld = p.dma(xsrc, xov[:, :, t0:t0 + 512], xo.rd() + xot.wd())
                xot.wr(ld, reset=True)
                emit_ln_tiles(A, t0, 512,
                              lambda c, xsrc=xsrc: xsrc[:, c, :],
                              lambda c, t0=t0: xsv[:, c, HF + t0:HF + t0 + 512],
                              src_buf=xot, dst_buf=xs)
            p.fence()

        def halo_exchange_full():
            def wfn(ctx, shap):
                return (_ds1(shap, ctx["my"]).rearrange("p (c t) -> p c t", c=NCH),
                        xsv[:, :, HF + T - HF:HF + T])

            def rfn(ctx, shap):
                return (xsv[:, :, 0:HF], _ds1(shap, ctx["oth"]).rearrange("p (c t) -> p c t", c=NCH))
            rt = xc.send_recv(shx, wfn, rfn, xs.rd(), xs.wd())
            xs.wr(rt)
            tz = p.task("dve", [rt] + xs.wd() + nf.rd(),
                        lambda e: e.tensor_scalar_mul(xsv[:, :, 0:HF], xsv[:, :, 0:HF], nf.ap[:, 0:1]))
            xs.wr(tz)
            nf.r(tz)
            return tz

        def halo_exchange_col():
            tc1 = p.task("dve", xs.rd() + colstg.wd(),
                         lambda e: e.tensor_copy(colstg.ap[:, 0:NCH], xsv[:, :, HF + T - 1]))
            colstg.wr(tc1, reset=True)

            def wfn(ctx, shap):
                return (_ds1(shap, ctx["my"])[:, 0:NCH], colstg.ap[:, 0:NCH])

            def rfn(ctx, shap):
                return (colstg.ap[:, NCH:2 * NCH], _ds1(shap, ctx["oth"])[:, 0:NCH])
            rt = xc.send_recv(shx, wfn, rfn, colstg.rd(), colstg.wd())
            colstg.wr(rt)
            tz = p.task("dve", [rt] + xs.wd() + nf.rd() + colstg.rd(),
                        lambda e: e.tensor_scalar_mul(xsv[:, :, HF - 1], colstg.ap[:, NCH:2 * NCH], nf.ap[:, 0:1]))
            xs.wr(tz)
            nf.r(tz); colstg.r(tz)
            return tz

        halo_exchange_full()

        for layer in range(_F_LAYERS):
            lw = layer * NCH * D
            off = layer * (2 * MCH + 3 * NCH + 4)

            if _F_ATTN:
                p.fence()
                AR.reset()
                sb2 = AR.alloc
                wq = sb2("wq", [P, NCH * D], BF16); wk_ = sb2("wk", [P, NCH * D], BF16)
                wv = sb2("wv", [P, NCH * D], BF16); wo = sb2("wo", [P, NCH * D], BF16)
                kring = sb2("kring", [P, 8 * 4 * 256], BF16)
                qsb = sb2("qsb", [P, 2 * 8 * 256], BF16)
                vring = sb2("vring", [P, 4 * 16 * 65], BF16)
                vsc2 = sb2("vsc2", [P, 4 * 2 * 65], BF16)
                probs = sb2("probs", [P, 4 * 512], BF16)
                negB = sb2("negB", [P, 4 * 16], F32)
                ebuf = sb2("ebuf", [P, 4 * 16], F32)
                sums = sb2("sums", [1, 512], F32)
                outbf = sb2("outbf", [P, NCH * 256], BF16)
                xbf = [sb2("xbf", [P, NCH * 256], BF16)]
                tabs = sb2("tabs", [P, 4 * 256], BF16)
                qksc = sb2("qksc", [P, 4], F32)
                sqb = sb2("sqb", [P, 256], BF16)
                rn32 = sb2("rn32", [P, 256], F32)
                rnb = sb2("rnb", [P, 256], BF16)
                qh32 = sb2("qh32", [P, 256], F32)
                qhb = sb2("qhb", [P, 256], BF16)
                u1b = sb2("u1b", [P, 256], F32)
                repsb = sb2("repsb", [P, 256], F32)
                svcol = sb2("svcol", [P, 4 * 7], F32)
                svcolv = svcol.ap.rearrange("p (c w) -> p c w", c=4)
                tsv = p.task("pool", xs.rd() + svcol.wd(),
                             lambda e: [e.tensor_copy(
                                 svcolv[:, :, w_ - 1:w_],
                                 xsv[:, NCH // 2:NCH, HF + w_ * 256 - 1:HF + w_ * 256])
                                 for w_ in range(1, NW)][-1])
                svcol.wr(tsv, reset=True); xs.r(tsv)

                stream_weight_bf16(k, wq, 0, fWq, lw, NCH * D, stgs, piece=1024)
                stream_weight_bf16(k, wk_, 0, fWk, lw, NCH * D, stgs, piece=1024)
                stream_weight_bf16(k, wv, 0, fWv, lw, NCH * D, stgs, piece=1024)
                stream_weight_bf16(k, wo, 0, fWo, lw, NCH * D, stgs, piece=1024)
                ldq = p.dma(qksc.ap[:, :], fb[:, off + 2 * MCH + 3 * NCH:off + 2 * MCH + 3 * NCH + 4], qksc.wd())
                qksc.wr(ldq, reset=True)

                vov = vring.ap.rearrange("p (s h e) -> p s h e", s=4, h=16)
                tvo = p.task("dve", vring.wd(), lambda e: e.memset(vov[:, :, :, 64:65], 1.0))
                vring.wr(tvo)

                wqv = wq.ap.rearrange("p (c e) -> p c e", c=NCH)
                wkv2 = wk_.ap.rearrange("p (c e) -> p c e", c=NCH)
                wvv = wv.ap.rearrange("p (c e) -> p c e", c=NCH)
                wov = wo.ap.rearrange("p (c e) -> p c e", c=NCH)

                def kslot(w):
                    return 3 if w == 0 else (w % 3)

                def vslot(jcg):
                    return (jcg + 4) % 4

                def cast_shift_win(wk0, buf):
                    def em(e):
                        last = None
                        for c in range(NCH):
                            off2 = HF + wk0 * 256 - (1 if c >= NCH // 2 else 0)
                            last = e.tensor_copy(buf.ap[:, c * 256:(c + 1) * 256], xsv[:, c, off2:off2 + 256])
                        return last
                    t = p.task("pool", xs.rd() + buf.wd(), em)
                    buf.wr(t, reset=True); xs.r(t)
                    if wk0 >= 1:
                        t2 = p.task("pool", [t] + svcol.rd() + buf.wd(),
                                    lambda e, buf=buf, wk0=wk0: e.tensor_copy(
                                        buf.ap[:, :].rearrange("p (c t) -> p c t", c=NCH)[:, NCH // 2:NCH, 0:1],
                                        svcolv[:, :, wk0 - 1:wk0]))
                        buf.wr(t2); svcol.r(t2)
                        t = t2
                    return t

                def load_tabs(wk0):
                    toks = []
                    for ti in range(4):
                        ld = p.dma(tabs.ap[:, ti * 256:(ti + 1) * 256],
                                   ftab[:, ti * (T + 256) + (wk0 + 1) * 256: ti * (T + 256) + (wk0 + 2) * 256],
                                   tabs.wd())
                        tabs.wr(ld)
                        toks.append(ld)
                    tt = p.task("pool", toks + tabs.wd() + qksc.rd(), lambda e: [
                        e.tensor_scalar_mul(tabs.ap[:, ti * 256:(ti + 1) * 256],
                                            tabs.ap[:, ti * 256:(ti + 1) * 256],
                                            qksc.ap[:, ti:ti + 1]) for ti in range(4)][-1])
                    tabs.wr(tt); qksc.r(tt)
                    return tt, 0

                def rotary(psumb, mmtok, hp, is_q, dst_buf, dst_ap, tabtok, tabsl):
                    W = 256
                    t1 = p.task("act", [mmtok] + sqb.wd() + psumb.wd(),
                                lambda e: e.activation(sqb.ap[:, :W], psumb.ap[:, :W], AF.Square))
                    psumb.r(t1); sqb.wr(t1, reset=True)
                    bn = k.bank()
                    tn = p.task("pe", [t1] + bn.wd() + blk2.rd(),
                                lambda e: e.matmul(bn.ap[0:16, :W], blk2.ap[:, hp * 16:(hp + 1) * 16],
                                                   sqb.ap[:, :W], start=True, stop=True))
                    bn.wr(tn, reset=True); blk2.r(tn); sqb.r(tn)
                    tmx = p.task("dve", [tn] + rn32.wd() + bn.wd(),
                                 lambda e: e.tensor_scalar_max(rn32.ap[0:16, :W], bn.ap[0:16, :W], 1e-24))
                    bn.r(tmx); rn32.wr(tmx, reset=True)
                    tl = p.task("act", [tmx], lambda e: e.activation(rn32.ap[0:16, :W], rn32.ap[0:16, :W], AF.Ln))
                    te = p.task("act", [tl], lambda e: e.activation(rn32.ap[0:16, :W], rn32.ap[0:16, :W], AF.Exp, scale=-0.5))
                    rn32.wr(te)
                    tb = p.task("dve", [te] + rnb.wd(),
                                lambda e: e.tensor_copy(rnb.ap[0:16, :W], rn32.ap[0:16, :W]))
                    rnb.wr(tb, reset=True); rn32.r(tb)
                    brep = k.bank()
                    trep = p.task("pe", [tb] + brep.wd() + selh.rd(),
                                  lambda e: e.matmul(brep.ap[:, :W], selh.ap[:, hp * P:(hp + 1) * P],
                                                     rnb.ap[0:16, :W], start=True, stop=True))
                    brep.wr(trep, reset=True); selh.r(trep); rnb.r(trep)
                    t3a = p.task("act", [trep] + qh32.wd() + brep.wd(),
                                 lambda e: e.copy(qh32.ap[:, :W], brep.ap[:, :W]))
                    brep.r(t3a); qh32.wr(t3a, reset=True)
                    t3 = p.task("dve", [t3a] + psumb.wd(),
                                lambda e: e.tensor_tensor(qh32.ap[:, :W], psumb.ap[:, :W], qh32.ap[:, :W], ALU.mult))
                    psumb.r(t3); qh32.wr(t3)
                    t4 = p.task("pool", [t3] + qhb.wd(), lambda e: e.tensor_copy(qhb.ap[:, :W], qh32.ap[:, :W]))
                    qhb.wr(t4, reset=True); qh32.r(t4)
                    bR = k.bank()
                    t5 = p.task("pe", [t4] + bR.wd() + Rm.rd(),
                                lambda e: e.matmul(bR.ap[:, :W], Rm.ap[:, :], qhb.ap[:, :W], start=True, stop=True))
                    bR.wr(t5, reset=True); Rm.r(t5); qhb.r(t5)
                    ti = 0 if is_q else 2
                    tb1 = tabs.ap[:, ti * 256:(ti + 1) * 256]
                    tb2 = tabs.ap[:, (ti + 1) * 256:(ti + 2) * 256]
                    t6 = p.task("dve", [t4, tabtok] + u1b.wd() + tabs.rd(),
                                lambda e: e.tensor_tensor(u1b.ap[:, :W], qhb.ap[:, :W], tb1, ALU.mult))
                    u1b.wr(t6, reset=True); qhb.r(t6); tabs.r(t6)
                    t7 = p.task("dve", [t5, t6] + bR.wd() + qh32.wd(),
                                lambda e: e.tensor_tensor(qh32.ap[:, :W], bR.ap[:, :W], tb2, ALU.mult))
                    bR.r(t7); qh32.wr(t7, reset=True); tabs.r(t7)
                    t8 = p.task("dve", [t7] + dst_buf.wd(),
                                lambda e: e.tensor_tensor(dst_ap, u1b.ap[:, :W], qh32.ap[:, :W], ALU.add))
                    dst_buf.wr(t8); u1b.r(t8); qh32.r(t8)
                    return t8

                def qkv_win(wk0, parts="qkv"):
                    buf = xbf[0]
                    tc = cast_shift_win(wk0, buf)
                    tabtok, tabsl = (load_tabs(wk0) if ("q" in parts or "k" in parts) else (None, 0))
                    bufv = buf.ap.rearrange("p (c t) -> p c t", c=NCH)
                    for hp in (range(NCH) if "q" in parts or "k" in parts else []):
                        if wk0 >= 0 and "q" in parts:
                            bq = k.bank()

                            def mmq(e, hp=hp, bq=bq, bufv=bufv):
                                last = None
                                for ci in range(NCH):
                                    last = e.matmul(bq.ap[:, :256], wqv[:, ci, hp * P:(hp + 1) * P],
                                                    bufv[:, ci, :], start=(ci == 0), stop=(ci == NCH - 1))
                                return last
                            tq = p.task("pe", [tc] + bq.wd() + wq.rd(), mmq)
                            bq.wr(tq, reset=True); wq.r(tq); buf.r(tq)
                            qoff = 2048 if wk0 == 0 else 0
                            rotary(bq, tq, hp, True, qsb,
                                   qsb.ap[:, qoff + hp * 256:qoff + (hp + 1) * 256], tabtok, tabsl)
                        bk = k.bank()

                        def mmk(e, hp=hp, bk=bk, bufv=bufv):
                            last = None
                            for ci in range(NCH):
                                last = e.matmul(bk.ap[:, :256], wkv2[:, ci, hp * P:(hp + 1) * P],
                                                bufv[:, ci, :], start=(ci == 0), stop=(ci == NCH - 1))
                            return last
                        tk = p.task("pe", [tc] + bk.wd() + wk_.rd(), mmk)
                        bk.wr(tk, reset=True); wk_.r(tk); buf.r(tk)
                        ks = kslot(wk0) if wk0 != -1 else 2
                        rotary(bk, tk, hp, False, kring,
                               kring.ap[:, hp * 1024 + ks * 256: hp * 1024 + ks * 256 + 256],
                               tabtok, tabsl)
                    for s in (range(2) if "v" in parts else []):
                        jcg = 2 * wk0 + s
                        sv = vslot(jcg)
                        bv1, bv2 = k.bank(), k.bank()

                        def mmv(e, s=s, bv1=bv1, bv2=bv2, bufv=bufv):
                            last = None
                            for ci in range(NCH):
                                last = e.matmul(bv1.ap[:, :512], bufv[:, ci, s * 128:(s + 1) * 128],
                                                wvv[:, ci, 0:512], start=(ci == 0), stop=(ci == NCH - 1))
                            for ci in range(NCH):
                                last = e.matmul(bv2.ap[:, :512], bufv[:, ci, s * 128:(s + 1) * 128],
                                                wvv[:, ci, 512:1024], start=(ci == 0), stop=(ci == NCH - 1))
                            return last
                        tv = p.task("pe", [tc] + bv1.wd() + bv2.wd() + wv.rd(), mmv)
                        bv1.wr(tv, reset=True); bv2.wr(tv, reset=True); wv.r(tv); buf.r(tv)
                        vdst = vring.ap[:, sv * 1040:(sv + 1) * 1040].rearrange("p (h e) -> p h e", h=16)
                        ev1 = p.task("dve", [tv] + vring.wd() + bv1.wd(),
                                     lambda e, vdst=vdst, bv1=bv1: e.tensor_copy(
                                         vdst[:, 0:8, 0:64], bv1.ap[:, :512].rearrange("p (h e) -> p h e", h=8)))
                        bv1.r(ev1)
                        ev2 = p.task("dve", [ev1] + bv2.wd(),
                                     lambda e, vdst=vdst, bv2=bv2: e.tensor_copy(
                                         vdst[:, 8:16, 0:64], bv2.ap[:, :512].rearrange("p (h e) -> p h e", h=8)))
                        bv2.r(ev2)
                        vring.wr(ev2)

                def attn_win(w):
                    ks0 = kslot(w - 1) if w >= 1 else 2
                    ks1 = kslot(w)
                    qoff = 2048 if w == 0 else 0
                    mtype = 0 if w == 0 else 1
                    for hp in range(NCH):
                        sb_h = {}
                        for jcp in range(2):
                            bh0, bh1 = k.bank(), k.bank()
                            sb_h[(jcp, 0)], sb_h[(jcp, 1)] = bh0, bh1

                            def mmsc(e, jcp=jcp, bh0=bh0, bh1=bh1, hp=hp, qoff=qoff,
                                     ks0=ks0, ks1=ks1):
                                last = None
                                for jc in (2 * jcp, 2 * jcp + 1):
                                    kcol = hp * 1024 + (ks0 if jc < 2 else ks1) * 256 + (jc % 2) * 128
                                    for h2 in range(2):
                                        bb = (bh0, bh1)[h2]
                                        last = e.matmul(bb.ap[:, (jc % 2) * 256:(jc % 2) * 256 + 256],
                                                        kring.ap[h2 * 64:(h2 + 1) * 64, kcol:kcol + 128],
                                                        qsb.ap[h2 * 64:(h2 + 1) * 64,
                                                               qoff + hp * 256:qoff + (hp + 1) * 256],
                                                        start=True, stop=True)
                                return last
                            tsc = p.task("pe", kring.rd() + qsb.rd() + bh0.wd() + bh1.wd(), mmsc)
                            bh0.wr(tsc, reset=True); bh1.wr(tsc, reset=True)
                            kring.r(tsc); qsb.r(tsc)
                            if not _F_TRD:
                                continue
                            for h2 in range(2):
                                cb = h2 * 32 + hp * 4 + jcp * 2
                                trd = p.task("dve", [tsc] + negB.wd() + sb_h[(jcp, h2)].wd(),
                                             lambda e, cb=cb, bb=sb_h[(jcp, h2)]: e.tensor_reduce(
                                                 negB.ap[:, cb:cb + 2],
                                                 bb.ap[:, :512].rearrange("p (j i) -> p j i", j=2),
                                                 AX.X, ALU.max, negate=True))
                                negB.wr(trd); sb_h[(jcp, h2)].r(trd)
                        if _F_AWIN < 2:
                            continue
                        teb = p.task("act", negB.rd() + ebuf.wd(),
                                     lambda e, hp=hp, b=k.fconst(-DS_SHIFT, P): [e.activation(
                                         ebuf.ap[:, h2 * 32 + hp * 4 + jcp * 2:h2 * 32 + hp * 4 + jcp * 2 + 2],
                                         negB.ap[:, h2 * 32 + hp * 4 + jcp * 2:h2 * 32 + hp * 4 + jcp * 2 + 2],
                                         AF.Exp, scale=-1.0, bias=b)
                                         for h2 in range(2) for jcp in range(2)][-1])
                        ebuf.wr(teb); negB.r(teb)
                        for jc in range(4):
                            for h2 in range(2):
                                cb = h2 * 32 + hp * 4 + jc
                                tex = p.task("act", negB.rd() + probs.wd() + sb_h[(jc // 2, h2)].wd(),
                                             lambda e, jc=jc, h2=h2, cb=cb, bsc=sb_h[(jc // 2, h2)]: e.activation(
                                                 probs.ap[:, jc * 512 + h2 * 256:jc * 512 + h2 * 256 + 256],
                                                 bsc.ap[:, (jc % 2) * 256:(jc % 2) * 256 + 256], AF.Exp,
                                                 bias=negB.ap[:, cb:cb + 1]))
                                probs.wr(tex); sb_h[(jc // 2, h2)].r(tex); negB.r(tex)
                        for jc in range(4):
                            tmk = p.task("pool", probs.rd() + masks.rd(),
                                         lambda e, jc=jc, mtype=mtype: e.tensor_tensor(
                                             probs.ap[:, jc * 512:(jc + 1) * 512].rearrange("p (h i) -> p h i", h=2),
                                             probs.ap[:, jc * 512:(jc + 1) * 512].rearrange("p (h i) -> p h i", h=2),
                                             masks.ap[:, (mtype * 4 + jc) * 256:(mtype * 4 + jc + 1) * 256]
                                             [:, None, :].broadcast_to([P, 2, 256]), ALU.mult))
                            probs.wr(tmk); masks.r(tmk)
                        if _F_AWIN < 3:
                            continue
                        vs2v = vsc2.ap.rearrange("p (jc h e) -> p jc h e", jc=4, h=2)
                        ebv = ebuf.ap.rearrange("p (h c) -> p h c", h=2)
                        for jc in range(4):
                            sv = vslot(2 * (w - 1) + jc)
                            tvs = p.task("pool", vring.rd() + ebuf.rd() + vsc2.wd(),
                                         lambda e, jc=jc, sv=sv, hp=hp, vs2v=vs2v, ebv=ebv: e.tensor_tensor(
                                             vs2v[:, jc, :, :],
                                             vring.ap[:, sv * 1040 + 2 * hp * 65: sv * 1040 + 2 * hp * 65 + 130]
                                             .rearrange("p (h e) -> p h e", h=2),
                                             ebv[:, :, hp * 4 + jc:hp * 4 + jc + 1]
                                             .broadcast_to([P, 2, 65]), ALU.mult))
                            vsc2.wr(tvs); vring.r(tvs); ebuf.r(tvs)
                        bav = k.bank()

                        def mmav(e, hp=hp, bav=bav, vs2v=vs2v):
                            last = None
                            for h2 in range(2):
                                for jc in range(4):
                                    last = e.matmul(bav.ap[0:65, h2 * 256:(h2 + 1) * 256],
                                                    vs2v[:, jc, h2, :],
                                                    probs.ap[:, jc * 512 + h2 * 256:jc * 512 + h2 * 256 + 256],
                                                    start=(jc == 0), stop=(jc == 3))
                            return last
                        tav = p.task("pe", probs.rd() + vsc2.rd() + bav.wd(), mmav)
                        bav.wr(tav, reset=True); probs.r(tav); vsc2.r(tav)
                        if _F_AWIN < 4:
                            continue
                        ts1 = p.task("act", [tav] + sums.wd() + bav.wd(),
                                     lambda e, bav=bav: (e.copy(sums.ap[0:1, 0:256], bav.ap[64:65, 0:256]),
                                                e.copy(sums.ap[0:1, 256:512], bav.ap[64:65, 256:512]))[-1])
                        sums.wr(ts1, reset=True); bav.r(ts1)
                        tr = p.task("dve", [ts1], lambda e: e.reciprocal(sums.ap[0:1, :], sums.ap[0:1, :]))
                        sums.wr(tr)
                        tsb = p.task("dve", [tr] + rnb.wd(), lambda e: e.tensor_copy(rnb.ap[0:1, 0:256], sums.ap[0:1, 0:256]))
                        rnb.wr(tsb, reset=True); sums.r(tsb)
                        brp = k.bank()
                        trp = p.task("pe", [tsb] + brp.wd() + ones.rd(),
                                     lambda e, brp=brp: e.matmul(brp.ap[0:64, 0:256], ones.ap[0:1, 0:64], rnb.ap[0:1, 0:256],
                                                        start=True, stop=True))
                        brp.wr(trp, reset=True)
                        tsb3 = p.task("dve", [trp] + rnb.wd() + sums.rd(),
                                      lambda e: e.tensor_copy(rnb.ap[0:1, 0:256], sums.ap[0:1, 256:512]))
                        rnb.wr(tsb3)
                        trp2 = p.task("pe", [tsb3] + brp.wd() + ones.rd(),
                                      lambda e, brp=brp: e.matmul(brp.ap[64:128, 0:256], ones.ap[0:1, 0:64], rnb.ap[0:1, 0:256],
                                                         start=True, stop=True))
                        brp.wr(trp2); ones.r(trp2); rnb.r(trp2)
                        tre = p.task("act", [trp2] + repsb.wd() + brp.wd(),
                                     lambda e, brp=brp: e.copy(repsb.ap[:, 0:256], brp.ap[:, 0:256]))
                        brp.r(tre); repsb.wr(tre, reset=True)
                        for h2 in range(2):
                            tdv = p.task("dve", [tre, tav] + outbf.wd() + bav.wd(),
                                         lambda e, h2=h2, hp=hp, bav=bav: e.tensor_tensor(
                                             outbf.ap[h2 * 64:(h2 + 1) * 64, hp * 256:(hp + 1) * 256],
                                             bav.ap[0:64, h2 * 256:(h2 + 1) * 256],
                                             repsb.ap[h2 * 64:(h2 + 1) * 64, 0:256], ALU.mult))
                            outbf.wr(tdv); bav.r(tdv); repsb.r(tdv)
                    for m in (range(NCH) if _F_AWIN >= 5 else []):
                        bo = k.bank()

                        def mmo(e, m=m, bo=bo):
                            last = None
                            for kc in range(NCH):
                                last = e.matmul(bo.ap[:, :256], wov[:, kc, m * P:(m + 1) * P],
                                                outbf.ap[:, kc * 256:(kc + 1) * 256],
                                                start=(kc == 0), stop=(kc == NCH - 1))
                            return last
                        tmo = p.task("pe", outbf.rd() + bo.wd() + wo.rd(), mmo)
                        bo.wr(tmo, reset=True); outbf.r(tmo); wo.r(tmo)
                        tres = p.task("dve", [tmo] + xs.wd() + bo.wd(),
                                      lambda e, m=m, bo=bo, w=w: e.tensor_tensor(
                                          xsv[:, m, HF + w * 256:HF + (w + 1) * 256],
                                          xsv[:, m, HF + w * 256:HF + (w + 1) * 256],
                                          bo.ap[:, :256], ALU.add))
                        xs.wr(tres); bo.r(tres)

                if _F_ASCHED >= 1:
                    qkv_win(0)
                if _F_ASCHED >= 2:
                    for w_ in range(1, NW):
                        qkv_win(w_)
                        if _F_ASCHED >= 3:
                            attn_win(w_)
                if _F_ASCHED >= 4:
                    qkv_win(-1)
                    qkv_win(0, parts="v")
                    halo_exchange_col()
                    attn_win(0)
                p.fence()

            if _F_MLP:
                p.fence()
                AR.reset()
                sb2 = AR.alloc
                A = {"scr": sb2("scr", [P, NCH * 512], BF16), "sq": sb2("sq", [P, NCH * 512], BF16),
                     "rows": sb2("rows", [1, 4 * 512], F32), "rowsbf": sb2("rowsbf", [1, 4 * 512], BF16),
                     "asc": sb2("asc", [P, 512], F32)}
                xnbf = sb2("xnbf", [P, NCH * 1024], BF16)
                ag = sb2("ag", [P, MCH * 1024], BF16)
                w1p = [sb2(f"w1p{i}", [P, 2 * NCH * P], BF16) for i in range(2)]
                w2p = [sb2(f"w2p{i}", [P, D], BF16) for i in range(2)]
                asc2 = sb2("asc2", [P, 512], F32)
                gsc2 = sb2("gsc2", [P, 512], F32)
                lngb = sb2("lngb", [P, 2 * NCH], F32)
                b1ab = sb2("b1ab", [P, 2 * MCH], F32)
                b2c = sb2("b2c", [P, NCH], F32)
                svcol2 = sb2("svcol2", [P, 4], F32)
                tsv2 = p.task("pool", xs.rd() + svcol2.wd(),
                              lambda e: e.tensor_copy(
                                  svcol2.ap[:, 0:4].rearrange("p (c o) -> p c o", o=1),
                                  xsv[:, NCH // 2:NCH, HF + 1023:HF + 1024]))
                svcol2.wr(tsv2, reset=True); xs.r(tsv2)
                ldb = p.dma(b1ab.ap[:, :], fb[:, off:off + 2 * MCH], b1ab.wd())
                b1ab.wr(ldb, reset=True)
                ldb2 = p.dma(b2c.ap[:, :], fb[:, off + 2 * MCH:off + 2 * MCH + NCH], b2c.wd())
                b2c.wr(ldb2, reset=True)
                ldg = p.dma(lngb.ap[:, :], fb[:, off + 2 * MCH + NCH:off + 2 * MCH + 3 * NCH], lngb.wd())
                lngb.wr(ldg, reset=True)
                xnv = xnbf.ap.rearrange("p (c t) -> p c t", c=NCH)
                agv = ag.ap.rearrange("p (c t) -> p c t", c=MCH)

                for th in range(2):
                    tb0 = th * 1024
                    kks = ((1, 0) if th == 0 else (0, 1))
                    for kk2 in kks:
                        t0 = tb0 + kk2 * 512

                        def src_of(c, t0=t0):
                            off2 = HF + t0 - (1 if c >= NCH // 2 else 0)
                            return xsv[:, c, off2:off2 + 512]
                        fx = None
                        if th == 1 and kk2 == 0:
                            fx = (svcol2.ap[:, 0:4].rearrange("p (c o) -> p c o", o=1), svcol2)
                        emit_ln_tiles(A, t0, 512, src_of,
                                      lambda c, t0=t0, tb0=tb0: xnv[:, c, t0 - tb0:t0 - tb0 + 512],
                                      gb=(lambda c: lngb.ap[:, c:c + 1],
                                          lambda c: lngb.ap[:, NCH + c:NCH + c + 1], lngb),
                                      dst_buf=xnbf, fix_src=fx)
                    for i in range(MCH):
                        pc = w1p[i % 2]
                        stream_weight_bf16(k, pc, 0, fW1a, (layer * MCH + i) * NCH * P, NCH * P, stgs, piece=1024)
                        stream_weight_bf16(k, pc, NCH * P, fW1g, (layer * MCH + i) * NCH * P, NCH * P, stgs, piece=1024)
                        pcv = pc.ap.rearrange("p (g c e) -> p g c e", g=2, c=NCH)
                        for kk2 in range(2):
                            t0 = kk2 * 512
                            bA, bG = k.bank(), k.bank()

                            def mma(e, bA=bA, bG=bG, t0=t0, pcv=pcv):
                                last = None
                                for ci in range(NCH):
                                    last = e.matmul(bA.ap[:, :512], pcv[:, 0, ci, :], xnv[:, ci, t0:t0 + 512],
                                                    start=(ci == 0), stop=(ci == NCH - 1))
                                for ci in range(NCH):
                                    last = e.matmul(bG.ap[:, :512], pcv[:, 1, ci, :], xnv[:, ci, t0:t0 + 512],
                                                    start=(ci == 0), stop=(ci == NCH - 1))
                                return last
                            tm = p.task("pe", pc.rd() + xnbf.rd() + bA.wd() + bG.wd(), mma)
                            bA.wr(tm, reset=True); bG.wr(tm, reset=True); pc.r(tm); xnbf.r(tm)
                            ta = p.task("dve", [tm] + asc2.wd() + b1ab.rd() + bA.wd(),
                                        lambda e, i=i, bA=bA: e.tensor_scalar_add(
                                            asc2.ap[:, :512], bA.ap[:, :512], b1ab.ap[:, i:i + 1]))
                            asc2.wr(ta, reset=True); bA.r(ta); b1ab.r(ta)
                            tg = p.task("act", [ta] + gsc2.wd() + b1ab.rd() + bG.wd(),
                                        lambda e, i=i, bG=bG: e.activation(gsc2.ap[:, :512], bG.ap[:, :512], AF.Gelu,
                                                                           bias=b1ab.ap[:, MCH + i:MCH + i + 1]))
                            gsc2.wr(tg, reset=True); bG.r(tg); b1ab.r(tg)
                            tmu = p.task("dve", [ta, tg] + ag.wd(),
                                         lambda e, i=i, t0=t0: e.tensor_tensor(
                                             agv[:, i, t0:t0 + 512], asc2.ap[:, :512], gsc2.ap[:, :512], ALU.mult))
                            ag.wr(tmu); asc2.r(tmu); gsc2.r(tmu)
                    for kk2 in range(2):
                        t0 = kk2 * 512
                        mb = [k.bank() for _ in range(NCH)]
                        for kc in range(MCH):
                            pw = w2p[kc % 2]
                            stream_weight_bf16(k, pw, 0, fW2, (layer * MCH + kc) * D, D, stgs, piece=1024)
                            def mmw2(e, kc=kc, t0=t0, pw=pw, mb=mb):
                                last = None
                                for m in range(NCH):
                                    last = e.matmul(mb[m].ap[:, :512], pw.ap[:, m * P:(m + 1) * P],
                                                    agv[:, kc, t0:t0 + 512], start=(kc == 0), stop=(kc == MCH - 1))
                                return last
                            deps = pw.rd() + ag.rd()
                            for m in range(NCH):
                                deps += mb[m].wd()
                            tm2 = p.task("pe", deps, mmw2)
                            for m in range(NCH):
                                mb[m].wr(tm2, reset=(kc == 0))
                            pw.r(tm2); ag.r(tm2)
                        for m in range(NCH):
                            tres = p.task("dve", mb[m].wd() + xs.wd() + b2c.rd(),
                                          lambda e, m=m, t0=t0, tb0=tb0, mb=mb: e.scalar_tensor_tensor(
                                              xsv[:, m, HF + tb0 + t0:HF + tb0 + t0 + 512], mb[m].ap[:, :512],
                                              b2c.ap[:, m:m + 1], xsv[:, m, HF + tb0 + t0:HF + tb0 + t0 + 512],
                                              ALU.add, ALU.add))
                            xs.wr(tres); mb[m].r(tres); b2c.r(tres)
                if layer < _F_LAYERS - 1:
                    halo_exchange_full()
                p.fence()

        if True:
            AR.reset()
            xot2 = AR.alloc("xot2", [P, NCH * 512], F32)
            for kk in range(4):
                t0 = kk * 512
                srcv = xot2.ap.rearrange("p (c t) -> p c t", c=NCH)
                ld = p.dma(srcv, xov[:, :, t0:t0 + 512], xo.rd() + xot2.wd())
                xot2.wr(ld, reset=True)
                tadd = p.task("dve", xot2.rd() + xs.rd(),
                              lambda e, t0=t0, srcv=srcv: e.tensor_tensor(
                                  srcv, srcv, xsv[:, :, HF + t0:HF + t0 + 512], ALU.add))
                xot2.wr(tadd); xs.r(tadd)
                st = p.dma(xov[:, :, t0:t0 + 512], srcv, xot2.rd() + xo.wd())
                xo.wr(st)
                xot2.r(st)
            p.fence()
def build(n_stages):
    k = K()
    nc = k.nc
    with k.stack:
        C = {}
        k._fc_buf = k.sb("c_f32", [P, 64], F32)
        fcload = k.p.dma(k._fc_buf.ap[:, :], k.inp("c_f32_in", [P, 64]), [])
        k.p.preamble = [fcload]
        pid_ap = nc.partition_id_tensor.ap()

        def ctx_task(e):
            rp = e.alloc_register("parity")
            e.reg_load(rp, pid_ap[0:1, 0:1])
            e.reg_alu(rp, rp, 1, ALU.bitwise_and)
            k.p_ctx["my"] = e.snap(rp, min_val=0, max_val=1)
            ro = e.alloc_register("oth")
            e.reg_alu(ro, 1, k.p_ctx["my"], ALU.subtract)
            k.p_ctx["oth"] = e.snap(ro, min_val=0, max_val=1)
            return e.nop()
        k.p.task("sp", [], ctx_task)
        ones = k.sb("c_ones", [P, P], BF16)
        C["ones"] = ones
        ones.wr(k.p.dma(ones.ap[:, :], k.inp("c_ones_in", [P, P], BF16), []))
        Rm = k.sb("c_Rm", [P, P], BF16)
        Rm.wr(k.p.dma(Rm.ap[:, :], k.inp("fRm", [P, P], BF16), []))
        C["Rmat"] = Rm
        blk2 = k.sb("c_blk2", [P, 8 * 16], BF16)
        blk2.wr(k.p.dma(blk2.ap[:, :], k.inp("fblk2", [P, 8 * 16], BF16), []))
        C["blk2"] = blk2
        selh = k.sb("c_selh", [16, 8 * P], BF16)
        selh.wr(k.p.dma(selh.ap[:, :], k.inp("fselh", [16, 8 * P], BF16), []))
        C["selh"] = selh
        masks = k.sb("c_masks", [P, 2 * 4 * 256], BF16)
        masks.wr(k.p.dma(masks.ap[:, :], k.inp("fmasks", [P, 2 * 4 * 256], BF16), []))
        C["masks"] = masks
        C["shxp"] = [k.dram("shx0", [2, P, NCH * HF], shared=True),
                     k.dram("shx1", [2, P, NCH * HF], shared=True)]
        k.init_psum()
        xc = Xchg(k)
        xo = k.dram("xo", [P, NCH * T])
        xin = k.inp("x_fm", [P, NCH * T])
        out = k.outp("y_fm", [P, NCH * T])
        xo.wr(k.p.dma(xo.ap[:, :], xin[:, :], []))
        if n_stages >= 1:
            stage_attn_a(k, xo, C)
        if n_stages >= 2:
            stage_ffn(k, xo, C, xc, "f1")
        if n_stages >= 3:
            stage_mamba(k, xo, C, xc)
        if n_stages >= 4:
            stage_ffn(k, xo, C, xc, "f2")
        k.p.dma(out[:, :], xo.ap[:, :], xo.rd() + xo.readers)
        block = k.E(nc.Block())
        k.p.emit_all(block, lambda name: k.E(nc.semaphore(name)))
    return k


def stage_mamba(k, xo, C, xc):
    nc, p = k.nc, k.p
    ones = C["ones"]
    TF = 2 * T
    xov = xo.ap.rearrange("p (c t) -> p c t", c=NCH)
    mWin = k.inp("mWin", [P, NCH * 2048])
    mXp = k.inp("mXp", [P, NCH * 96])
    mWdt = k.inp("mWdt", [P, NCH * P])
    mWout = k.inp("mWout", [P, NCH * 1024])
    mselbc = k.inp("mselbc", [32, 32 * P], BF16)
    mvec = k.inp("mvec", [P, 200])
    shnx = k.dram("shnx", [2, P, NCH * T], BF16, shared=True)
    shxd = k.dram("shxd", [2, 96, TF], shared=True)
    shop = k.dram("shop", [2, 2, P, NCH * T], shared=True)
    mu_d = k.dram("mu_d", [P, NCH * TF], BF16)
    msz_d = k.dram("msz_d", [P, NCH * TF], BF16)
    mdt_d = k.dram("mdt_d", [P, NCH * TF], BF16)
    mnxl = k.dram("mnxl", [P, NCH * T], BF16)
    mxdl = k.dram("mxdl", [96, TF])
    mxol = k.dram("mxol", [96, TF])
    mol = k.dram("mol", [2, P, NCH * T])
    mf0 = k.dram("mf0", [P, NCH * T])
    mf1 = k.dram("mf1", [P, NCH * T])

    with ExitStack() as stk:
        E = stk.enter_context

        def sb(name, shape, dt=F32):
            return Buf(E(nc.sbuf_tensor("m" + name, list(shape), dt)).ap(), tag="m" + name)

        p.fence()
        AR = Arena(k, "m_arena", 100 * 1024, stack=stk)
        vec = sb("veccols", [P, 200])
        vec.wr(p.dma(vec.ap[:, :], mvec[:, :], []))
        cw = lambda c, kk: vec.ap[:, c * 4 + kk:c * 4 + kk + 1]
        cbb = lambda c: vec.ap[:, 32 + c:33 + c]
        bdt = lambda c: vec.ap[:, 40 + c:41 + c]
        Dv = lambda c: vec.ap[:, 48 + c:49 + c]
        lng = lambda c: vec.ap[:, 184 + c:185 + c]
        lnb = lambda c: vec.ap[:, 192 + c:193 + c]
        Aneg = sb("Aneg", [P, 128])
        ta = p.task("act", vec.rd() + Aneg.wd(),
                    lambda e: e.activation(Aneg.ap[:, :], vec.ap[:, 56:184], AF.Exp))
        Aneg.wr(ta, reset=True); vec.r(ta)
        ta2 = p.task("dve", Aneg.rd(),
                     lambda e: e.tensor_scalar_mul(Aneg.ap[:, :], Aneg.ap[:, :], -1.0))
        Aneg.wr(ta2)
        selbc = sb("selsb", [32, 32 * P], BF16)
        selbc.wr(p.dma(selbc.ap[:, :], mselbc[:, :], []))
        stgs = [sb(f"stg{i}", [P, 1024]) for i in range(2)]
        xdtb = sb("xdtb", [64, TF], BF16)
        xbcb = sb("xbcb", [32, TF], BF16)
        carries = sb("carries", [P, 128])
        tcz = p.task("dve", carries.wd(), lambda e: e.memset(carries.ap[:, :], 0.0))
        carries.wr(tcz, reset=True)

        AR.reset()
        sb2 = AR.alloc
        scr_bf = sb2("scr", [P, NCH * 512], BF16)
        sq_bf = sb2("sq", [P, NCH * 512], BF16)
        rows = sb2("rows", [1, 4 * 512], F32)
        rows_bf = sb2("rowsbf", [1, 4 * 512], BF16)
        asc = sb2("asc", [P, 512], F32)
        xot = sb2("xot", [P, NCH * 512], F32)
        hbf = sb2("hbf", [P, NCH * 512], BF16)
        sbv = scr_bf.ap.rearrange("p (c t) -> p c t", c=NCH)
        sqv = sq_bf.ap.rearrange("p (c t) -> p c t", c=NCH)
        xotv = xot.ap.rearrange("p (c t) -> p c t", c=NCH)
        hbv = hbf.ap.rearrange("p (c t) -> p c t", c=NCH)
        w = 512
        nx_wtoks = []
        for kk in range(4):
            t0 = kk * w
            ld = p.dma(xotv, xov[:, :, t0:t0 + w], xo.rd() + xot.wd())
            xot.wr(ld, reset=True)
            tc = p.task("pool", xot.rd() + scr_bf.wd(),
                        lambda e: [e.tensor_copy(sbv[:, c, :], xotv[:, c, :]) for c in range(NCH)][-1])
            scr_bf.wr(tc, reset=True); xot.r(tc)
            tq = p.task("act", [tc] + sq_bf.wd(), lambda e: e.activation(sqv, sbv, AF.Square))
            sq_bf.wr(tq, reset=True); scr_bf.r(tq)
            bsum, bsq = k.bank(), k.bank()

            def mm_s(e, bsum=bsum):
                last = None
                for ci in range(NCH):
                    last = e.matmul(bsum.ap[0:1, :w], ones.ap[:, 0:1], sbv[:, ci, :],
                                    start=(ci == 0), stop=(ci == NCH - 1))
                return last
            ts = p.task("pe", [tc] + bsum.wd() + ones.rd(), mm_s)
            bsum.wr(ts, reset=True); scr_bf.r(ts)

            def mm_q(e, bsq=bsq):
                last = None
                for ci in range(NCH):
                    last = e.matmul(bsq.ap[0:1, :w], ones.ap[:, 0:1], sqv[:, ci, :],
                                    start=(ci == 0), stop=(ci == NCH - 1))
                return last
            tq2 = p.task("pe", [tq] + bsq.wd() + ones.rd(), mm_q)
            bsq.wr(tq2, reset=True); sq_bf.r(tq2)
            mu = rows.ap[:, 0:w]; m2 = rows.ap[:, 512:512 + w]
            var = rows.ap[:, 1024:1024 + w]; rstd = rows.ap[:, 1536:1536 + w]
            t1 = p.task("dve", [ts] + rows.wd() + bsum.wd(),
                        lambda e, bsum=bsum: e.tensor_scalar_mul(mu, bsum.ap[0:1, :w], 1.0 / D))
            bsum.r(t1)
            t2 = p.task("dve", [tq2, t1] + bsq.wd(),
                        lambda e, bsq=bsq: e.tensor_scalar_mul(m2, bsq.ap[0:1, :w], 1.0 / D))
            bsq.r(t2)
            t3 = p.task("dve", [t2], lambda e: e.tensor_tensor(var, mu, mu, ALU.mult))
            t4 = p.task("dve", [t3], lambda e: e.tensor_tensor(var, m2, var, ALU.subtract))
            t5 = p.task("act", [t4], lambda e, b=k.fconst(LN_EPS): e.activation(rstd, var, AF.Ln, bias=b))
            t6 = p.task("act", [t5], lambda e: e.activation(rstd, rstd, AF.Exp, scale=-0.5))
            t8 = p.task("dve", [t6], lambda e: e.tensor_tensor(mu, mu, rstd, ALU.mult))
            t9 = p.task("dve", [t8], lambda e: e.tensor_scalar_mul(mu, mu, -1.0))
            rows.wr(t9)
            rh = rows_bf.ap[:, 0:w]; rl = rows_bf.ap[:, 512:512 + w]
            mh = rows_bf.ap[:, 1024:1024 + w]; ml = rows_bf.ap[:, 1536:1536 + w]
            c1 = p.task("dve", [t9] + rows_bf.wd(), lambda e: e.tensor_copy(rh, rstd))
            c2 = p.task("dve", [c1], lambda e: e.tensor_tensor(rl, rstd, rh, ALU.subtract))
            c3 = p.task("dve", [c2], lambda e: e.tensor_copy(mh, mu))
            c4 = p.task("dve", [c3], lambda e: e.tensor_tensor(ml, mu, mh, ALU.subtract))
            rows_bf.wr(c4, reset=True); rows.r(c4)
            br1, br2 = k.bank(), k.bank()
            tr1 = p.task("pe", [c4] + br1.wd() + ones.rd(), lambda e, br1=br1: (
                e.matmul(br1.ap[:, :w], ones.ap[0:1, :], rh, start=True, stop=False),
                e.matmul(br1.ap[:, :w], ones.ap[0:1, :], rl, start=False, stop=True))[-1])
            tr2 = p.task("pe", [c4] + br2.wd() + ones.rd(), lambda e, br2=br2: (
                e.matmul(br2.ap[:, :w], ones.ap[0:1, :], mh, start=True, stop=False),
                e.matmul(br2.ap[:, :w], ones.ap[0:1, :], ml, start=False, stop=True))[-1])
            br1.wr(tr1, reset=True); br2.wr(tr2, reset=True); rows_bf.r(tr2)
            for c in range(NCH):
                tB2 = p.task("dve", [tr1] + scr_bf.rd() + asc.wd() + br1.wd(),
                             lambda e, c=c, br1=br1: e.tensor_tensor(asc.ap[:, :w], sbv[:, c, :], br1.ap[:, :w], ALU.mult))
                br1.r(tB2); scr_bf.r(tB2)
                tC2 = p.task("dve", [tB2] + br2.wd(),
                             lambda e, br2=br2: e.tensor_tensor(asc.ap[:, :w], asc.ap[:, :w], br2.ap[:, :w], ALU.add))
                br2.r(tC2)
                asc.wr(tC2, reset=True)
                tH = p.task("dve", [tC2] + hbf.wd() + vec.rd(),
                            lambda e, c=c: e.tensor_scalar(hbv[:, c, :], asc.ap[:, :w],
                                                           lng(c), lnb(c), ALU.mult, ALU.add))
                hbf.wr(tH); vec.r(tH); asc.r(tH)
                tX = p.task("dve", [tC2] + xot.rd(),
                            lambda e, c=c: e.tensor_tensor(xotv[:, c, :], xotv[:, c, :],
                                                           asc.ap[:, :w], ALU.add))
                xot.wr(tX); asc.r(tX)
            st = p.dma(xov[:, :, t0:t0 + w], xotv, xot.rd() + xo.wd())
            xo.wr(st); xot.r(st)
            wt = p.dma(mnxl.ap.rearrange("p (c t) -> p c t", c=NCH)[:, :, kk * 512:(kk + 1) * 512],
                       hbv, hbf.rd() + mnxl.wd())
            mnxl.wr(wt); hbf.r(wt)
            nx_wtoks.append(wt)
        wts = xc_dma_dyn(xc, lambda ctx: (_ds1(shnx.ap, ctx["my"]), mnxl.ap[:, :]),
                         nx_wtoks + mnxl.rd() + shnx.wd())
        shnx.wr(wts); mnxl.r(wts)
        bar1 = xc_bar(xc, [wts])
        p.fence()

        AR.reset()
        sb2 = AR.alloc
        mWinb = sb2("Winb", [P, NCH * 2048], BF16)
        stream_weight_bf16(k, mWinb, 0, mWin, 0, NCH * 2048, stgs, piece=1024)
        nxb = sb2("nxb", [P, NCH * 512], BF16)
        ustg = sb2("ustg", [P, 528], F32)
        uacc = sb2("uacc", [P, 512], F32)
        uaccb = sb2("uaccb", [P, 512], F32)
        usg = sb2("usg", [P, 512], F32)
        ubf = sb2("ubf", [P, 512], BF16)
        uhalo = sb2("uhalo", [P, NCH * 4], F32)
        thz = p.task("dve", uhalo.wd(), lambda e: e.memset(uhalo.ap[:, :], 0.0))
        uhalo.wr(thz, reset=True)
        nxbv = nxb.ap.rearrange("p (c t) -> p c t", c=NCH)
        for tg in range(8):
            h, tl = tg // 4, tg % 4
            ld = p.dma(nxbv, shnx.ap[h].rearrange("p (c t) -> p c t", c=NCH)[:, :, tl * 512:(tl + 1) * 512],
                       [bar1] + nxb.wd())
            nxb.wr(ld, reset=True); shnx.r(ld)
            for mc in range(16):
                bk = k.bank()

                def mmz(e, bk=bk, mc=mc):
                    last = None
                    for kc in range(NCH):
                        last = e.matmul(bk.ap[:, :512], mWinb.ap[:, kc * 2048 + mc * P:kc * 2048 + (mc + 1) * P],
                                        nxbv[:, kc, :], start=(kc == 0), stop=(kc == NCH - 1))
                    return last
                tm = p.task("pe", nxb.rd() + mWinb.rd() + bk.wd(), mmz)
                bk.wr(tm, reset=True); nxb.r(tm); mWinb.r(tm)
                if mc < 8:
                    c = mc
                    te = p.task("act", [tm] + ustg.wd() + bk.wd(),
                                lambda e, bk=bk: e.copy(ustg.ap[:, 3:515], bk.ap[:, :512]))
                    ustg.wr(te, reset=True); bk.r(te)
                    th = p.task("dve", [te] + uhalo.rd(),
                                lambda e, c=c: e.tensor_copy(ustg.ap[:, 0:3], uhalo.ap[:, c * 4:c * 4 + 3]))
                    ustg.wr(th); uhalo.r(th)
                    tt0 = p.task("dve", [th] + uacc.wd() + vec.rd(),
                                 lambda e, c=c, z=k.fconst(0.0, P): e.tensor_scalar(
                                     uacc.ap[:, :], ustg.ap[:, 0:512], cw(c, 0), z, ALU.mult, ALU.add))
                    uacc.wr(tt0, reset=True); ustg.r(tt0); vec.r(tt0)
                    tprev = tt0
                    for kk in range(1, 4):
                        tprev = p.task("dve", [tprev] + vec.rd(),
                                       lambda e, c=c, kk=kk: e.scalar_tensor_tensor(
                                           uacc.ap[:, :], ustg.ap[:, kk:kk + 512], cw(c, kk),
                                           uacc.ap[:, :], ALU.mult, ALU.add))
                        uacc.wr(tprev); ustg.r(tprev); vec.r(tprev)
                    tuh = p.task("dve", [tprev] + uhalo.rd(),
                                 lambda e, c=c: e.tensor_copy(uhalo.ap[:, c * 4:c * 4 + 3], ustg.ap[:, 512:515]))
                    uhalo.wr(tuh); ustg.r(tuh)
                    tab = p.task("dve", [tprev] + uaccb.wd() + vec.rd(),
                                 lambda e, c=c: e.tensor_scalar_add(uaccb.ap[:, :], uacc.ap[:, :], cbb(c)))
                    uaccb.wr(tab, reset=True); uacc.r(tab); vec.r(tab)
                    tsg = p.task("act", [tprev] + usg.wd() + vec.rd(),
                                 lambda e, c=c: e.activation(usg.ap[:, :], uacc.ap[:, :], AF.Sigmoid, bias=cbb(c)))
                    usg.wr(tsg, reset=True); uacc.r(tsg); vec.r(tsg)
                    tmu = p.task("dve", [tab, tsg],
                                 lambda e: e.tensor_tensor(uaccb.ap[:, :], uaccb.ap[:, :], usg.ap[:, :], ALU.mult))
                    uaccb.wr(tmu); usg.r(tmu)
                    tcb = p.task("pool", [tmu] + ubf.wd(),
                                 lambda e: e.tensor_copy(ubf.ap[:, :], uaccb.ap[:, :]))
                    ubf.wr(tcb, reset=True); uaccb.r(tcb)
                    sd = p.dma(mu_d.ap[:, c * TF + tg * 512:c * TF + (tg + 1) * 512], ubf.ap[:, :],
                               ubf.rd() + mu_d.wd())
                    mu_d.wr(sd); ubf.r(sd)
                else:
                    c = mc - 8
                    tsg = p.task("act", [tm] + usg.wd() + bk.wd(),
                                 lambda e, bk=bk: e.activation(usg.ap[:, :], bk.ap[:, :512], AF.Sigmoid))
                    usg.wr(tsg, reset=True); bk.r(tsg)
                    tzm = p.task("dve", [tsg] + uaccb.wd() + bk.wd(),
                                 lambda e, bk=bk: e.tensor_tensor(uaccb.ap[:, :], bk.ap[:, :512],
                                                                  usg.ap[:, :], ALU.mult))
                    uaccb.wr(tzm, reset=True); bk.r(tzm); usg.r(tzm)
                    tcb = p.task("pool", [tzm] + ubf.wd(),
                                 lambda e: e.tensor_copy(ubf.ap[:, :], uaccb.ap[:, :]))
                    ubf.wr(tcb, reset=True); uaccb.r(tcb)
                    sd = p.dma(msz_d.ap[:, c * TF + tg * 512:c * TF + (tg + 1) * 512], ubf.ap[:, :],
                               ubf.rd() + msz_d.wd())
                    msz_d.wr(sd); ubf.r(sd)
        p.fence()

        AR.reset()
        sb2 = AR.alloc
        mXpb = sb2("Xpb", [P, NCH * 96], BF16)
        stream_weight_bf16(k, mXpb, 0, mXp, 0, NCH * 96, stgs, piece=768)
        mWdtb = sb2("Wdtb", [P, NCH * P], BF16)
        stream_weight_bf16(k, mWdtb, 0, mWdt, 0, NCH * P, stgs, piece=1024)
        mub = sb2("mub", [P, NCH * 512], BF16)
        xdmy = sb2("xdmy", [96, TF], F32)
        xdo = sb2("xdo", [96, TF], F32)
        spr = sb2("spr", [P, 512], F32)
        spn = sb2("spn", [P, 512], F32)
        spe = sb2("spe", [P, 512], F32)
        dtbf = sb2("dtbf", [P, 512], BF16)
        mubv = mub.ap.rearrange("p (c t) -> p c t", c=NCH)
        xd_wtoks = []
        for tg in range(8):
            ldu = p.dma(mubv, mu_d.ap.rearrange("p (c t) -> p c t", c=NCH)[:, :, tg * 512:(tg + 1) * 512],
                        mu_d.rd() + mub.wd())
            mub.wr(ldu, reset=True); mu_d.r(ldu)
            bk = k.bank()

            def mmx(e, bk=bk):
                last = None
                for cc in range(NCH):
                    last = e.matmul(bk.ap[0:96, :512], mXpb.ap[:, cc * 96:(cc + 1) * 96],
                                    mubv[:, cc, :], start=(cc == 0), stop=(cc == NCH - 1))
                return last
            tm = p.task("pe", mub.rd() + mXpb.rd() + bk.wd(), mmx)
            bk.wr(tm, reset=True); mub.r(tm); mXpb.r(tm)
            tev = p.task("dve", [tm] + xdmy.wd() + bk.wd(),
                         lambda e, tg=tg, bk=bk: e.tensor_copy(xdmy.ap[0:96, tg * 512:(tg + 1) * 512],
                                                               bk.ap[0:96, :512]))
            xdmy.wr(tev); bk.r(tev)
            wt = p.dma(mxdl.ap[:, tg * 512:(tg + 1) * 512],
                       xdmy.ap[0:96, tg * 512:(tg + 1) * 512], xdmy.rd() + mxdl.wd())
            mxdl.wr(wt); xdmy.r(wt)
            xd_wtoks.append(wt)
        wts2 = xc_dma_dyn(xc, lambda ctx: (_ds1(shxd.ap, ctx["my"]), mxdl.ap[:, :]),
                          xd_wtoks + mxdl.rd() + shxd.wd())
        shxd.wr(wts2); mxdl.r(wts2)
        bar2 = xc_bar(xc, [wts2])
        ldb = xc_dma_dyn(xc, lambda ctx: (mxol.ap[:, :], _ds1(shxd.ap, ctx["oth"])),
                         [bar2] + mxol.wd())
        mxol.wr(ldb, reset=True); shxd.r(ldb)
        ldo = p.dma(xdo.ap[0:96, :], mxol.ap[:, :], mxol.rd() + xdo.wd())
        xdo.wr(ldo, reset=True); mxol.r(ldo)
        tadd = p.task("dve", xdo.rd() + xdmy.rd(),
                      lambda e: e.tensor_tensor(xdmy.ap[0:96, :], xdmy.ap[0:96, :], xdo.ap[0:96, :], ALU.add))
        xdmy.wr(tadd); xdo.r(tadd)
        tc1 = p.task("pool", xdmy.rd() + xdtb.wd(),
                     lambda e: e.tensor_copy(xdtb.ap[:, :], xdmy.ap[0:64, :]))
        xdtb.wr(tc1, reset=True); xdmy.r(tc1)
        tc2 = p.task("pool", xdmy.rd() + xbcb.wd(),
                     lambda e: e.tensor_copy(xbcb.ap[:, :], xdmy.ap[64:96, :]))
        xbcb.wr(tc2, reset=True); xdmy.r(tc2)
        for tg in range(8):
            for mch in range(NCH):
                bk = k.bank()
                tm = p.task("pe", xdtb.rd() + mWdtb.rd() + bk.wd(),
                            lambda e, tg=tg, mch=mch, bk=bk: e.matmul(
                                bk.ap[:, :512], mWdtb.ap[0:64, mch * P:(mch + 1) * P],
                                xdtb.ap[0:64, tg * 512:(tg + 1) * 512], start=True, stop=True))
                bk.wr(tm, reset=True); xdtb.r(tm); mWdtb.r(tm)
                t1 = p.task("dve", [tm] + spr.wd() + vec.rd() + bk.wd(),
                            lambda e, mch=mch, bk=bk: e.tensor_scalar_add(spr.ap[:, :], bk.ap[:, :512], bdt(mch)))
                spr.wr(t1, reset=True); bk.r(t1); vec.r(t1)
                t2 = p.task("act", [t1] + spn.wd(),
                            lambda e: e.activation(spn.ap[:, :], spr.ap[:, :], AF.Abs))
                spn.wr(t2, reset=True); spr.r(t2)
                t3 = p.task("act", [t2] + spe.wd(),
                            lambda e: e.activation(spe.ap[:, :], spn.ap[:, :], AF.Exp, scale=-1.0))
                spe.wr(t3, reset=True); spn.r(t3)
                t4 = p.task("act", [t3], lambda e, b=k.fconst(1.0, P): e.activation(
                    spe.ap[:, :], spe.ap[:, :], AF.Ln, bias=b))
                spe.wr(t4)
                t5 = p.task("dve", [t1] + spr.wd(), lambda e: e.tensor_scalar_max(spr.ap[:, :], spr.ap[:, :], 0.0))
                spr.wr(t5)
                t6 = p.task("dve", [t4, t5], lambda e: e.tensor_tensor(spr.ap[:, :], spr.ap[:, :],
                                                                       spe.ap[:, :], ALU.add))
                spr.wr(t6); spe.r(t6)
                t7 = p.task("pool", [t6] + dtbf.wd(), lambda e: e.tensor_copy(dtbf.ap[:, :], spr.ap[:, :]))
                dtbf.wr(t7, reset=True); spr.r(t7)
                sd = p.dma(mdt_d.ap[:, mch * TF + tg * 512:mch * TF + (tg + 1) * 512], dtbf.ap[:, :],
                           dtbf.rd() + mdt_d.wd())
                mdt_d.wr(sd); dtbf.r(sd)
        p.fence()

        AR.reset()
        sb2 = AR.alloc
        mWoutb = sb2("Woutb", [P, NCH * 1024], BF16)
        stream_weight_bf16(k, mWoutb, 0, mWout, 0, NCH * 1024, stgs, piece=1024)
        Breps = sb2("Breps", [P, 16 * 512], BF16)
        Creps = sb2("Creps", [P, 16 * 512], BF16)
        dtt = sb2("dtt", [P, 512], BF16)
        utt = sb2("utt", [P, 512], BF16)
        szt = sb2("szt", [P, 512], BF16)
        dub = sb2("dub", [P, 512], BF16)
        dAb = sb2("dAb", [P, 512], F32)
        dbub = sb2("dbub", [P, 512], F32)
        hcs = sb2("hcs", [P, 512], F32)
        ytmp = sb2("ytmp", [P, 512], F32)
        y32 = sb2("y32", [P, 512], F32)
        ybf = sb2("ybf", [P, 512], BF16)
        opstg = sb2("opstg", [P, NCH * 512], F32)
        opsv = opstg.ap.rearrange("p (c t) -> p c t", c=NCH)
        op_wtoks = []
        for tg in range(8):
            for j in range(32):
                bk = k.bank()
                tm = p.task("pe", selbc.rd() + xbcb.rd() + bk.wd(),
                            lambda e, j=j, tg=tg, bk=bk: e.matmul(
                                bk.ap[:, :512], selbc.ap[:, j * P:(j + 1) * P],
                                xbcb.ap[0:32, tg * 512:(tg + 1) * 512], start=True, stop=True))
                bk.wr(tm, reset=True); selbc.r(tm); xbcb.r(tm)
                dst = (Breps if j < 16 else Creps)
                tev = p.task("dve", [tm] + dst.wd() + bk.wd(),
                             lambda e, j=j, bk=bk, dst=dst: e.tensor_copy(
                                 dst.ap[:, (j % 16) * 512:(j % 16 + 1) * 512], bk.ap[:, :512]))
                dst.wr(tev); bk.r(tev)
            mb = [k.bank() for _ in range(NCH)]
            for cch in range(NCH):
                l1 = p.dma(dtt.ap[:, :], mdt_d.ap[:, cch * TF + tg * 512:cch * TF + (tg + 1) * 512],
                           mdt_d.rd() + dtt.wd())
                dtt.wr(l1, reset=True); mdt_d.r(l1)
                l2 = p.dma(utt.ap[:, :], mu_d.ap[:, cch * TF + tg * 512:cch * TF + (tg + 1) * 512],
                           mu_d.rd() + utt.wd())
                utt.wr(l2, reset=True); mu_d.r(l2)
                l3 = p.dma(szt.ap[:, :], msz_d.ap[:, cch * TF + tg * 512:cch * TF + (tg + 1) * 512],
                           msz_d.rd() + szt.wd())
                szt.wr(l3, reset=True); msz_d.r(l3)
                tdu = p.task("pool", [l1, l2] + dub.wd() + dtt.rd() + utt.rd(),
                             lambda e: e.tensor_tensor(dub.ap[:, :], dtt.ap[:, :], utt.ap[:, :], ALU.mult))
                dub.wr(tdu, reset=True); dtt.r(tdu); utt.r(tdu)
                for n in range(16):
                    ci = cch * 16 + n
                    tA = p.task("act", dtt.rd() + dAb.wd() + Aneg.rd(),
                                lambda e, ci=ci: e.activation(dAb.ap[:, :], dtt.ap[:, :], AF.Exp,
                                                              scale=Aneg.ap[:, ci:ci + 1]))
                    dAb.wr(tA, reset=True); dtt.r(tA); Aneg.r(tA)
                    tB = p.task("pool", dub.rd() + Breps.rd() + dbub.wd(),
                                lambda e, n=n: e.tensor_tensor(dbub.ap[:, :], dub.ap[:, :],
                                                               Breps.ap[:, n * 512:(n + 1) * 512], ALU.mult))
                    dbub.wr(tB, reset=True); dub.r(tB); Breps.r(tB)
                    tS = p.task("dve", [tA, tB] + hcs.wd() + carries.rd(),
                                lambda e, ci=ci: e.tensor_tensor_scan(
                                    hcs.ap[:, :], dAb.ap[:, :], dbub.ap[:, :],
                                    carries.ap[:, ci:ci + 1], ALU.mult, ALU.add))
                    hcs.wr(tS, reset=True); dAb.r(tS); dbub.r(tS); carries.r(tS)
                    tCr = p.task("dve", [tS] + carries.wd(),
                                 lambda e, ci=ci: e.tensor_copy(carries.ap[:, ci:ci + 1],
                                                                hcs.ap[:, 511:512]))
                    carries.wr(tCr); hcs.r(tCr)
                    if n == 0:
                        tY = p.task("dve", [tS] + y32.wd() + Creps.rd(),
                                    lambda e, n=n: e.tensor_tensor(y32.ap[:, :], hcs.ap[:, :],
                                                                   Creps.ap[:, n * 512:(n + 1) * 512], ALU.mult))
                        y32.wr(tY, reset=True); hcs.r(tY); Creps.r(tY)
                    else:
                        tY1 = p.task("dve", [tS] + ytmp.wd() + Creps.rd(),
                                     lambda e, n=n: e.tensor_tensor(ytmp.ap[:, :], hcs.ap[:, :],
                                                                    Creps.ap[:, n * 512:(n + 1) * 512], ALU.mult))
                        ytmp.wr(tY1, reset=True); hcs.r(tY1); Creps.r(tY1)
                        tY = p.task("dve", [tY1] + y32.rd(),
                                    lambda e: e.tensor_tensor(y32.ap[:, :], y32.ap[:, :],
                                                              ytmp.ap[:, :], ALU.add))
                        y32.wr(tY); ytmp.r(tY)
                tUD = p.task("dve", y32.rd() + utt.rd() + vec.rd(),
                             lambda e, cch=cch: e.scalar_tensor_tensor(
                                 y32.ap[:, :], utt.ap[:, :], Dv(cch), y32.ap[:, :], ALU.mult, ALU.add))
                y32.wr(tUD); utt.r(tUD); vec.r(tUD)
                tYB = p.task("dve", [tUD, l3] + ybf.wd() + szt.rd(),
                             lambda e: e.tensor_tensor(ybf.ap[:, :], y32.ap[:, :], szt.ap[:, :], ALU.mult))
                ybf.wr(tYB, reset=True); y32.r(tYB); szt.r(tYB)

                def mmo(e, cch=cch, mb=mb):
                    last = None
                    for m in range(NCH):
                        last = e.matmul(mb[m].ap[:, :512], mWoutb.ap[:, cch * 1024 + m * P:cch * 1024 + (m + 1) * P],
                                        ybf.ap[:, :], start=(cch == 0), stop=(cch == NCH - 1))
                    return last
                deps = ybf.rd() + mWoutb.rd()
                for m in range(NCH):
                    deps = deps + mb[m].wd()
                tmo = p.task("pe", deps, mmo)
                for m in range(NCH):
                    mb[m].wr(tmo, reset=(cch == 0))
                ybf.r(tmo); mWoutb.r(tmo)
            for m in range(NCH):
                tev = p.task("dve", mb[m].wd() + opstg.wd(),
                             lambda e, m=m, mb=mb: e.tensor_copy(opsv[:, m, :], mb[m].ap[:, :512]))
                opstg.wr(tev); mb[m].r(tev)
            wt = p.dma(mol.ap[tg // 4].rearrange("p (c t) -> p c t", c=NCH)
                       [:, :, (tg % 4) * 512:(tg % 4 + 1) * 512],
                       opsv, opstg.rd() + mol.wd())
            mol.wr(wt); opstg.r(wt)
            op_wtoks.append(wt)
        wts3 = xc_dma_dyn(xc, lambda ctx: (_ds1(shop.ap, ctx["my"]), mol.ap[:, :, :]),
                          op_wtoks + mol.rd() + shop.wd())
        shop.wr(wts3); mol.r(wts3)
        bar3 = xc_bar(xc, [wts3])
        p.fence()

        AR.reset()
        sb2 = AR.alloc
        f0 = sb2("f0", [P, NCH * 512], F32)
        f1 = sb2("f1", [P, NCH * 512], F32)
        xoa = sb2("xoa", [P, NCH * 512], F32)
        f0v = f0.ap.rearrange("p (c t) -> p c t", c=NCH)
        f1v = f1.ap.rearrange("p (c t) -> p c t", c=NCH)
        xoav = xoa.ap.rearrange("p (c t) -> p c t", c=NCH)
        g0 = xc_dma_dyn(xc, lambda ctx: (mf0.ap[:, :], _ds1(shop.ap[0], ctx["my"])),
                        [bar3] + mf0.wd())
        mf0.wr(g0, reset=True); shop.r(g0)
        g1 = xc_dma_dyn(xc, lambda ctx: (mf1.ap[:, :], _ds1(shop.ap[1], ctx["my"])),
                        [bar3] + mf1.wd())
        mf1.wr(g1, reset=True); shop.r(g1)
        for kk in range(4):
            r0 = p.dma(f0v, mf0.ap.rearrange("p (c t) -> p c t", c=NCH)[:, :, kk * 512:(kk + 1) * 512],
                       mf0.rd() + f0.wd())
            f0.wr(r0, reset=True); mf0.r(r0)
            r1 = p.dma(f1v, mf1.ap.rearrange("p (c t) -> p c t", c=NCH)[:, :, kk * 512:(kk + 1) * 512],
                       mf1.rd() + f1.wd())
            f1.wr(r1, reset=True); mf1.r(r1)
            lx = p.dma(xoav, xov[:, :, kk * 512:(kk + 1) * 512], xo.rd() + xoa.wd())
            xoa.wr(lx, reset=True)
            t1 = p.task("dve", f0.rd() + f1.rd(),
                        lambda e: e.tensor_tensor(f0.ap[:, :], f0.ap[:, :], f1.ap[:, :], ALU.add))
            f0.wr(t1); f1.r(t1)
            t2 = p.task("dve", [t1] + xoa.rd(),
                        lambda e: e.tensor_tensor(xoa.ap[:, :], xoa.ap[:, :], f0.ap[:, :], ALU.add))
            xoa.wr(t2); f0.r(t2)
            st = p.dma(xov[:, :, kk * 512:(kk + 1) * 512], xoav, xoa.rd() + xo.wd())
            xo.wr(st); xoa.r(st)
        p.fence()


def kernel(**inputs):
    n_stages = _N_STAGES
    k = build(n_stages)
    maps = host_inputs(inputs)
    fc = np.zeros((P, 64), dtype=np.float32)
    fc[:, :len(k.fconst_vals)] = np.asarray(k.fconst_vals)[None, :]
    for m in maps:
        m["c_f32_in"] = fc
    maps = [{n: m[n] for n in k.ins} for m in maps]
    for n, (shape, dt) in k.ins.items():
        assert tuple(maps[0][n].shape) == shape, (n, maps[0][n].shape, shape)
    res = run_bass_kernel_spmd(k.nc, maps, core_ids=list(range(8)))
    out = np.empty((4, 4096, D), dtype=np.float32)
    for core in range(8):
        bidx, half = core // 2, core % 2
        out[bidx, half * T:(half + 1) * T] = from_fm(np.asarray(res.results[core]["y_fm"]), T)
    return out


DS_SHIFT = 20.0



# version 02
# speedup vs baseline: 1.1496x; 1.1496x over previous
import numpy as np
import ml_dtypes
import concourse.bass as bass
import concourse.mybir as mybir
from concourse.bass_utils import run_bass_kernel_spmd
from contextlib import ExitStack

F32 = mybir.dt.float32
BF16 = mybir.dt.bfloat16
AF = mybir.ActivationFunctionType
ALU = mybir.AluOpType
AX = mybir.AxisListType
BFD = ml_dtypes.bfloat16

D = 1024
T = 2048
NCH = 8
P = 128
LN_EPS = 1e-5
HF = 258
LTF = HF + T
NW = T // 256
MCH = 22
DS_SHIFT = 20.0

ENGINES = ("sp", "pe", "dve", "act", "pool")
ENG_ATTR = {"sp": "sync", "pe": "tensor", "dve": "vector", "act": "scalar", "pool": "gpsimd"}


NDMASEM = 32


class Prog:

    DMA_RANGE = {"sp": (0, 24), "pool": (24, 8)}

    def __init__(self, nc):
        self.nc = nc
        self.tasks = {e: [] for e in ENGINES}
        self.n = {e: 0 for e in ENGINES}
        self.lastw = {}
        self.cc_n = 0
        self.preamble = []
        self.dma_rr = 0
        self.dma_rr_eng = {"sp": 0, "pool": 0}
        self.dma_use = [0] * NDMASEM

    def _merge(self, eng, deps):
        merged = {}
        if self.preamble:
            deps = list(deps) + self.preamble
        for d in deps:
            if d is None:
                continue
            e2, c2 = d
            if e2 == eng and c2 >= self.n.get(eng, 0) + 1:
                continue
            if merged.get(e2, 0) < c2:
                merged[e2] = c2
        waits = []
        for e2, c2 in merged.items():
            if self.lastw.get((eng, e2), 0) >= c2:
                continue
            self.lastw[(eng, e2)] = c2
            waits.append((e2, c2))
        return waits

    def fence(self):
        toks = [(e, self.n[e]) for e in ENGINES if self.n[e] > 0]
        toks += [(f"dq{i}", 16 * self.dma_use[i]) for i in range(NDMASEM) if self.dma_use[i]]
        toks += [(f"cc{i}", 1) for i in range(self.cc_n)]
        self.preamble = toks
        return toks

    def task(self, eng, deps, emit, inc=1):
        waits = self._merge(eng, deps)
        self.n[eng] += inc
        tok = (eng, self.n[eng])
        self.tasks[eng].append((waits, emit, tok))
        return tok

    def dma_sem(self, eng="sp"):
        base, cnt = self.DMA_RANGE[eng]
        si = base + self.dma_rr_eng[eng] % cnt
        self.dma_rr_eng[eng] += 1
        return si

    def dma(self, out, in_, deps, eng="sp", **kw):
        si = self.dma_sem(eng)
        deps = list(deps)
        if self.dma_use[si] > 0:
            deps.append((f"dq{si}", 16 * self.dma_use[si]))
        self.dma_use[si] += 1
        tok = (f"dq{si}", 16 * self.dma_use[si])
        waits = self._merge(eng, deps)
        self.n[eng] += 0
        self.tasks[eng].append((waits, lambda e: e.dma_start(out=out, in_=in_, **kw), tok))
        return tok

    def barrier(self, cc_in_ap, cc_out_ap, deps):
        waits = self._merge("pool", deps)
        self.cc_n += 1
        tok = (f"cc{self.cc_n - 1}", 1)

        def emit(e):
            return e.collective_compute(
                "AllGather", ALU.bypass,
                replica_groups=[[0, 1], [2, 3], [4, 5], [6, 7]],
                ins=[cc_in_ap], outs=[cc_out_ap])

        self.tasks["pool"].append((waits, emit, tok))
        return tok

    def emit_all(self, block, mksem):
        sems = {e: mksem(f"s_{e}") for e in ENGINES}
        for i in range(NDMASEM):
            if self.dma_use[i]:
                sems[f"dq{i}"] = mksem(f"s_dq{i}")
        for i in range(self.cc_n):
            sems[f"cc{i}"] = mksem(f"s_cc{i}")

        def make(eng):
            def body(engine):
                for waits, emit, tok in self.tasks[eng]:
                    for e2, c2 in waits:
                        engine.wait_ge(sems[e2], c2)
                    ins = emit(engine)
                    ins.then_inc(sems[tok[0]], 16 if tok[0].startswith("dq") else 1)
                if eng == "sp":
                    for e2 in ENGINES:
                        if e2 != "sp" and self.n[e2] > 0:
                            engine.wait_ge(sems[e2], self.n[e2])
                    for i in range(NDMASEM):
                        if self.dma_use[i]:
                            engine.wait_ge(sems[f"dq{i}"], 16 * self.dma_use[i])
                    for i in range(self.cc_n):
                        engine.wait_ge(sems[f"cc{i}"], 1)
            return body

        for eng in ENGINES:
            getattr(block, ENG_ATTR[eng])(make(eng))


class Buf:
    __slots__ = ("ap", "_w", "readers", "tag")

    def __init__(self, ap=None, tag=""):
        self.ap = ap
        self._w = {}
        self.readers = []
        self.tag = tag

    def rd(self):
        return [(e, c) for e, c in self._w.items()]

    def wd(self):
        return list(self.readers) + self.rd()

    def r(self, tok):
        self.readers.append(tok)

    def wr(self, tok, reset=False):
        if reset:
            self._w = {}
        e, c = tok
        if self._w.get(e, 0) < c:
            self._w[e] = c
        self.readers = [t for t in self.readers if t[0] != e or t[1] > c] if False else self.readers
        return tok




class Arena:

    def __init__(self, k, name, nbytes, stack=None):
        self.buf = Buf((stack or k.stack).enter_context(
            k.nc.sbuf_tensor(name, [P, nbytes], mybir.dt.uint8)).ap(), tag=name)
        self.nbytes = nbytes
        self.off = 0

    def reset(self):
        self.off = 0

    def alloc(self, name, shape, dt):
        assert len(shape) == 2
        esz = {F32: 4, BF16: 2}[dt]
        width = shape[1] * esz
        off = (self.off + 63) & ~63
        self.off = off + width
        assert self.off <= self.nbytes, (name, self.off, self.nbytes)
        ap = self.buf.ap[0:shape[0], off:off + width].bitcast(dt)
        return Buf(ap, tag=name)


def rotary_tables(n=4096, dim_head=64, window=256):
    half = np.arange(0, dim_head, 2, dtype=np.float64)
    inv_freq = 1.0 / 10000.0 ** (half / dim_head)
    t = np.arange(n, dtype=np.float64)
    freqs = t[:, None] * inv_freq[None, :]
    freqs = np.concatenate([freqs, freqs], -1)
    base_scale = (half + 0.4 * dim_head) / (1.4 * dim_head)
    power = (t - n // 2) / (window // 2)
    scale = base_scale[None, :] ** power[:, None]
    scale = np.concatenate([scale, scale], -1)
    return (np.cos(freqs).astype(np.float32), np.sin(freqs).astype(np.float32),
            scale.astype(np.float32))


def build_R():
    R = np.zeros((64, 64), dtype=np.float32)
    for e in range(32):
        R[e, 32 + e] = -1.0
        R[32 + e, e] = 1.0
    R2 = np.zeros((128, 128), dtype=np.float32)
    R2[:64, :64] = R
    R2[64:, 64:] = R
    return R2


class K:
    def __init__(self):
        self.nc = bass.Bass("TRN2", target_bir_lowering=False)
        self.stack = ExitStack()
        self.E = self.stack.enter_context
        self.p = Prog(self.nc)
        self.ins = {}
        self.in_aps = {}
        self.psum = None
        self._psi = 0
        self.fconst_vals = []
        self._fconst_map = {}
        self._fc_buf = None
        self.p_ctx = {}

    def fconst(self, val, parts=1):
        val = float(val)
        if val not in self._fconst_map:
            idx = len(self.fconst_vals)
            assert idx < 64
            self.fconst_vals.append(val)
            self._fconst_map[val] = idx
        idx = self._fconst_map[val]
        return self._fc_buf.ap[0:parts, idx:idx + 1]

    def inp(self, name, shape, dt=F32):
        if name not in self.in_aps:
            self.ins[name] = (tuple(shape), dt)
            self.in_aps[name] = self.nc.dram_tensor(name, list(shape), dt, kind="ExternalInput").ap()
        return self.in_aps[name]

    def outp(self, name, shape, dt=F32):
        return self.nc.dram_tensor(name, list(shape), dt, kind="ExternalOutput").ap()

    def sb(self, name, shape, dt):
        return Buf(self.E(self.nc.sbuf_tensor(name, list(shape), dt)).ap(), tag=name)

    def dram(self, name, shape, dt=F32, shared=False):
        return Buf(self.nc.dram_tensor(name, list(shape), dt,
                                       addr_space=("Shared" if shared else "Local")).ap(), tag=name)

    def init_psum(self):
        self.psum = [Buf(self.E(self.nc.psum_tensor(f"ps{i}", [P, 512], F32)).ap(), tag=f"ps{i}")
                     for i in range(8)]

    def bank(self):
        b = self.psum[self._psi % 8]
        self._psi += 1
        return b


def stage_attn_a(k: K, xo: Buf, C):
    nc, p = k.nc, k.p
    TT = 512
    ones = C["ones"]
    _stk = ExitStack()

    def _sb(name, shape, dt):
        return Buf(_stk.enter_context(nc.sbuf_tensor(name, list(shape), dt)).ap(), tag=name)

    s_f = _sb("a_s", [P, NCH * TT], F32)
    wbt = _sb("a_wbt", [P, NCH * D], BF16)
    wbft = _sb("a_wbft", [P, NCH * D], BF16)
    for wname, dstb in (("aWBt", wbt), ("aWBft", wbft)):
        src = k.inp(wname, [P, NCH * D])
        for i in range(2):
            sl = slice(i * 4096, (i + 1) * 4096)
            ld = p.dma(s_f.ap[:, :], src[:, sl], s_f.wd())
            s_f.wr(ld, reset=True)
            ct = p.task("pool", s_f.rd() + dstb.wd(),
                        lambda e, d=dstb.ap, s=s_f.ap, sl=sl: e.tensor_copy(d[:, sl], s[:, :]))
            s_f.r(ct)
            dstb.wr(ct)
    w12 = _sb("a_w12", [P, 2 * NCH], BF16)
    w12f = _sb("a_w12f", [P, 2 * NCH], F32)
    w12f.wr(p.dma(w12f.ap[:, :], k.inp("aW12", [P, 2 * NCH]), []))
    t = p.task("dve", w12f.rd() + w12.wd(), lambda e: e.tensor_copy(w12.ap[:, :], w12f.ap[:, :]))
    w12f.r(t); w12.wr(t)
    bvec = _sb("a_b", [1, 1], F32)
    bvec.wr(p.dma(bvec.ap[:, :], k.inp("ab", [1, 1]), []))
    flags = _sb("a_flags", [P, 2], F32)
    flags.wr(p.dma(flags.ap[:, :], k.inp("flags", [P, 2]), []))

    TH = T + 1
    ub = [_sb(f"a_u{i}", [P, NCH * TT], F32) for i in range(2)]
    uhalo = _sb("a_uh", [P, NCH], F32)
    uhalo.wr(p.dma(uhalo.ap[:, :].rearrange("p c -> p c 1" if False else "p (c o) -> p c o", o=1),
                   k.inp("a_uhalo", [P, NCH, 1]), []))
    xov = xo.ap.rearrange("p (c t) -> p c t", c=NCH)

    nx_bf = _sb("a_nxbf", [P, NCH * TH], BF16)
    nxbv = nx_bf.ap.rearrange("p (c t) -> p c t", c=NCH)
    nx32 = [_sb(f"a_nx32_{i}", [P, NCH * TT], F32) for i in range(2)]
    scr_bf = _sb("a_scrbf", [P, NCH * TT], BF16)
    sq_bf = _sb("a_sqbf", [P, NCH * TT], BF16)
    rows = _sb("a_rows", [1, 4 * TT], F32)
    rows_bf = _sb("a_rowsbf", [1, 4 * TT], BF16)
    xf_bf = _sb("a_xf", [P, NCH * TT], BF16)
    xb_bf = _sb("a_xb", [P, NCH * TT], BF16)
    g_bf = _sb("a_g", [1, TT], BF16)
    fixbuf = _sb("a_fix", [P, NCH], F32)

    ntile = T // TT
    tiles = [(i * TT, TT) for i in range(ntile)] + [(T, 1)]

    def emit_ln(kk):
        t0, w = tiles[kk]
        if t0 < T:
            u = ub[kk % 2]
            src = u.ap[:, :NCH * w].rearrange("p (c t) -> p c t", c=NCH)
            ldu = p.dma(src, xov[:, :, t0:t0 + w], xo.rd() + u.wd())
            u.wr(ldu, reset=True)
        else:
            u = uhalo
            src = uhalo.ap.rearrange("p (c o) -> p c o", o=1)
        sbv = scr_bf.ap[:, :NCH * w].rearrange("p (c t) -> p c t", c=NCH)
        tc = p.task("pool", u.rd() + scr_bf.wd(), lambda e, d=sbv, s=src: e.tensor_copy(d, s))
        scr_bf.wr(tc, reset=True); u.r(tc)
        sqv = sq_bf.ap[:, :NCH * w].rearrange("p (c t) -> p c t", c=NCH)
        tq = p.task("act", [tc] + sq_bf.wd(), lambda e, d=sqv, s=sbv: e.activation(d, s, AF.Square))
        sq_bf.wr(tq, reset=True); scr_bf.r(tq)
        bsum, bsq = k.bank(), k.bank()

        def mm_s(e, sbv=sbv, bs=bsum.ap, w=w):
            last = None
            for ci in range(NCH):
                last = e.matmul(bs[0:1, :w], ones.ap[:, 0:1], sbv[:, ci, :], start=(ci == 0), stop=(ci == NCH - 1))
            return last
        ts = p.task("pe", [tc] + bsum.wd() + ones.rd(), mm_s)
        bsum.wr(ts, reset=True); scr_bf.r(ts); ones.r(ts)

        def mm_q(e, sqv=sqv, bq=bsq.ap, w=w):
            last = None
            for ci in range(NCH):
                last = e.matmul(bq[0:1, :w], ones.ap[:, 0:1], sqv[:, ci, :], start=(ci == 0), stop=(ci == NCH - 1))
            return last
        tq2 = p.task("pe", [tq] + bsq.wd() + ones.rd(), mm_q)
        bsq.wr(tq2, reset=True); sq_bf.r(tq2); ones.r(tq2)

        mu = rows.ap[:, 0:w]; m2 = rows.ap[:, TT:TT + w]
        var = rows.ap[:, 2 * TT:2 * TT + w]; rstd = rows.ap[:, 3 * TT:3 * TT + w]
        t1 = p.task("dve", [ts] + rows.wd() + bsum.wd(), lambda e: e.tensor_scalar_mul(mu, bsum.ap[0:1, :w], 1.0 / D))
        bsum.r(t1)
        t2 = p.task("dve", [tq2, t1] + bsq.wd(), lambda e: e.tensor_scalar_mul(m2, bsq.ap[0:1, :w], 1.0 / D))
        bsq.r(t2)
        t3 = p.task("dve", [t2], lambda e: e.tensor_tensor(var, mu, mu, ALU.mult))
        t4 = p.task("dve", [t3], lambda e: e.tensor_tensor(var, m2, var, ALU.subtract))
        t5 = p.task("act", [t4], lambda e, b=k.fconst(LN_EPS): e.activation(rstd, var, AF.Ln, bias=b))
        t6 = p.task("act", [t5], lambda e: e.activation(rstd, rstd, AF.Exp, scale=-0.5))
        rows.wr(t6, reset=True)
        t8 = p.task("dve", [t6], lambda e: e.tensor_tensor(mu, mu, rstd, ALU.mult))
        t9 = p.task("dve", [t8], lambda e: e.tensor_scalar_mul(mu, mu, -1.0))
        rows.wr(t9)
        rh = rows_bf.ap[:, 0:w]; rl = rows_bf.ap[:, TT:TT + w]
        mh = rows_bf.ap[:, 2 * TT:2 * TT + w]; ml = rows_bf.ap[:, 3 * TT:3 * TT + w]
        c1 = p.task("dve", [t9] + rows_bf.wd(), lambda e: e.tensor_copy(rh, rstd))
        c2 = p.task("dve", [c1], lambda e: e.tensor_tensor(rl, rstd, rh, ALU.subtract))
        c3 = p.task("dve", [c2], lambda e: e.tensor_copy(mh, mu))
        c4 = p.task("dve", [c3], lambda e: e.tensor_tensor(ml, mu, mh, ALU.subtract))
        rows_bf.wr(c4, reset=True); rows.r(c4)
        br1, br2 = k.bank(), k.bank()
        tr1 = p.task("pe", [c4] + br1.wd() + ones.rd(), lambda e: (
            e.matmul(br1.ap[:, :w], ones.ap[0:1, :], rh, start=True, stop=False),
            e.matmul(br1.ap[:, :w], ones.ap[0:1, :], rl, start=False, stop=True))[-1])
        tr2 = p.task("pe", [c4] + br2.wd() + ones.rd(), lambda e: (
            e.matmul(br2.ap[:, :w], ones.ap[0:1, :], mh, start=True, stop=False),
            e.matmul(br2.ap[:, :w], ones.ap[0:1, :], ml, start=False, stop=True))[-1])
        br1.wr(tr1, reset=True); br2.wr(tr2, reset=True); rows_bf.r(tr2); ones.r(tr1); ones.r(tr2)
        if t0 < T:
            db = nx32[kk % 2]
            dv = db.ap[:, :NCH * w].rearrange("p (c t) -> p c t", c=NCH)
            tA = p.task("dve", [tr1] + u.rd() + db.wd(),
                        lambda e: e.tensor_tensor(dv, src, br1.ap[:, None, 0:w].broadcast_to([P, NCH, w]), ALU.mult))
            tB = p.task("dve", [tA, tr2],
                        lambda e: e.tensor_tensor(dv, dv, br2.ap[:, None, 0:w].broadcast_to([P, NCH, w]), ALU.add))
            db.wr(tB, reset=True); br1.r(tA); br2.r(tB); u.r(tA)
            tC = p.task("pool", [tB] + nx_bf.wd(),
                        lambda e: e.tensor_copy(nxbv[:, :, t0:t0 + w], dv))
            nx_bf.wr(tC); db.r(tC)
            if _A_PART == 1:
                stn = p.dma(xov[:, :, t0:t0 + w], dv, [tB] + xo.wd())
                xo.wr(stn); db.r(stn)
        else:
            dsth = nxbv[:, :, T:T + 1]
            tA = p.task("dve", [tr1] + u.rd() + nx_bf.wd(),
                        lambda e: e.tensor_tensor(dsth, src, br1.ap[:, None, 0:1].broadcast_to([P, NCH, 1]), ALU.mult))
            tB = p.task("dve", [tA, tr2],
                        lambda e: e.tensor_tensor(dsth, dsth, br2.ap[:, None, 0:1].broadcast_to([P, NCH, 1]), ALU.add))
            nx_bf.wr(tB); br1.r(tA); br2.r(tB); u.r(tA)

    wbtv = wbt.ap.rearrange("p (c e) -> p c e", c=NCH)
    wbftv = wbft.ap.rearrange("p (c e) -> p c e", c=NCH)

    def emit_proj(kk):
        t0, w = tiles[kk]
        nx32b = nx32[kk % 2]
        nx32v = nx32b.ap.rearrange("p (c t) -> p c t", c=NCH)
        for eo in range(NCH):
            bxf, bxb = k.bank(), k.bank()

            def mm_e(e, eo=eo, bf=bxf.ap, bb=bxb.ap):
                last = None
                for ci in range(NCH):
                    last = e.matmul(bf[:, :w], wbtv[:, ci, eo * P:(eo + 1) * P], nxbv[:, ci, t0:t0 + w],
                                    start=(ci == 0), stop=(ci == NCH - 1))
                for ci in range(NCH):
                    last = e.matmul(bb[:, :w], wbftv[:, ci, eo * P:(eo + 1) * P], nxbv[:, ci, t0 + 1:t0 + 1 + w],
                                    start=(ci == 0), stop=(ci == NCH - 1))
                return last
            tmm = p.task("pe", nx_bf.rd() + bxf.wd() + bxb.wd() + wbt.rd() + wbft.rd(), mm_e)
            bxf.wr(tmm, reset=True); bxb.wr(tmm, reset=True)
            nx_bf.r(tmm); wbt.r(tmm); wbft.r(tmm)
            xfv = xf_bf.ap[:, eo * TT:eo * TT + w]
            xbv = xb_bf.ap[:, eo * TT:eo * TT + w]
            sv = s_f.ap[:, eo * TT:eo * TT + w]
            e3 = p.task("dve", [tmm] + s_f.wd() + bxf.wd(), lambda e, d=sv, s=bxf.ap: e.tensor_copy(d, s[:, :w]))
            bxf.r(e3)
            e4 = p.task("dve", [e3] + bxb.wd(), lambda e, d=sv, s=bxb.ap: e.tensor_tensor(d, d, s[:, :w], ALU.add))
            s_f.wr(e4)
            bxb.r(e4)
            e1 = p.task("act", [tmm] + xf_bf.wd() + bxf.wd(), lambda e, d=xfv, s=bxf.ap: e.copy(d, s[:, :w]))
            bxf.r(e1)
            e2 = p.task("act", [tmm, e1] + xb_bf.wd() + bxb.wd(), lambda e, d=xbv, s=bxb.ap: e.copy(d, s[:, :w]))
            xf_bf.wr(e1); xb_bf.wr(e2)
            bxb.r(e2)
        if _A_PART == 2:
            for eo in range(NCH):
                sv = s_f.ap[:, eo * TT:eo * TT + w]
                stn = p.dma(xov[:, eo, t0:t0 + w], sv, s_f.rd() + xo.wd())
                xo.wr(stn); s_f.r(stn)
            return
        if kk == 0:
            t_a = p.task("dve", xf_bf.rd() + nx_bf.rd() + fixbuf.wd(),
                         lambda e: [e.tensor_tensor(fixbuf.ap[:, eo:eo + 1], nxbv[:, eo, 0:1],
                                                    xf_bf.ap[:, eo * TT:eo * TT + 1], ALU.subtract)
                                    for eo in range(NCH)][-1])
            t_b = p.task("dve", [t_a] + flags.rd(),
                         lambda e: e.tensor_scalar_mul(fixbuf.ap[:, :], fixbuf.ap[:, :], flags.ap[:, 0:1]))
            t_c = p.task("dve", [t_b] + xf_bf.wd(),
                         lambda e: [e.tensor_tensor(xf_bf.ap[:, eo * TT:eo * TT + 1],
                                                    xf_bf.ap[:, eo * TT:eo * TT + 1],
                                                    fixbuf.ap[:, eo:eo + 1], ALU.add)
                                    for eo in range(NCH)][-1])
            t_d = p.task("dve", [t_b] + s_f.wd(),
                         lambda e: [e.tensor_tensor(s_f.ap[:, eo * TT:eo * TT + 1],
                                                    s_f.ap[:, eo * TT:eo * TT + 1],
                                                    fixbuf.ap[:, eo:eo + 1], ALU.add)
                                    for eo in range(NCH)][-1])
            s_f.wr(t_d); xf_bf.wr(t_c); fixbuf.wr(t_b, reset=True)
            fixbuf.r(t_d); fixbuf.r(t_c); flags.r(t_b); nx_bf.r(t_a)
        if kk == ntile - 1:
            wl = w - 1
            t_a = p.task("dve", xb_bf.rd() + nx_bf.rd() + fixbuf.wd(),
                         lambda e: [e.tensor_tensor(fixbuf.ap[:, eo:eo + 1], nxbv[:, eo, t0 + wl:t0 + wl + 1],
                                                    xb_bf.ap[:, eo * TT + wl:eo * TT + wl + 1], ALU.subtract)
                                    for eo in range(NCH)][-1])
            t_b = p.task("dve", [t_a] + flags.rd(),
                         lambda e: e.tensor_scalar_mul(fixbuf.ap[:, :], fixbuf.ap[:, :], flags.ap[:, 1:2]))
            t_c = p.task("dve", [t_b] + xb_bf.wd(),
                         lambda e: [e.tensor_tensor(xb_bf.ap[:, eo * TT + wl:eo * TT + wl + 1],
                                                    xb_bf.ap[:, eo * TT + wl:eo * TT + wl + 1],
                                                    fixbuf.ap[:, eo:eo + 1], ALU.add)
                                    for eo in range(NCH)][-1])
            t_d = p.task("dve", [t_b] + s_f.wd(),
                         lambda e: [e.tensor_tensor(s_f.ap[:, eo * TT + wl:eo * TT + wl + 1],
                                                    s_f.ap[:, eo * TT + wl:eo * TT + wl + 1],
                                                    fixbuf.ap[:, eo:eo + 1], ALU.add)
                                    for eo in range(NCH)][-1])
            s_f.wr(t_d); xb_bf.wr(t_c); fixbuf.wr(t_b, reset=True)
            fixbuf.r(t_d); fixbuf.r(t_c); flags.r(t_b); nx_bf.r(t_a)
        bg = k.bank()

        def mm_g(e):
            last = None
            for eo in range(NCH):
                last = e.matmul(bg.ap[0:1, :w], w12.ap[:, 2 * eo:2 * eo + 1],
                                xf_bf.ap[:, eo * TT:eo * TT + w], start=(eo == 0), stop=False)
                last = e.matmul(bg.ap[0:1, :w], w12.ap[:, 2 * eo + 1:2 * eo + 2],
                                xb_bf.ap[:, eo * TT:eo * TT + w], start=False, stop=(eo == NCH - 1))
            return last
        tg = p.task("pe", xf_bf.rd() + xb_bf.rd() + bg.wd() + w12.rd(), mm_g)
        bg.wr(tg, reset=True); xf_bf.r(tg); xb_bf.r(tg); w12.r(tg)
        tsg = p.task("act", [tg] + g_bf.wd() + bvec.rd() + bg.wd(),
                     lambda e: e.activation(g_bf.ap[:, :w], bg.ap[0:1, :w], AF.Sigmoid, bias=bvec.ap[0:1, 0:1]))
        g_bf.wr(tsg, reset=True); bg.r(tsg); bvec.r(tsg)
        bgr = k.bank()
        tgr = p.task("pe", [tsg] + bgr.wd() + ones.rd(),
                     lambda e: e.matmul(bgr.ap[:, :w], ones.ap[0:1, :], g_bf.ap[:, :w], start=True, stop=True))
        bgr.wr(tgr, reset=True); g_bf.r(tgr); ones.r(tgr)
        utile = ub[kk % 2]
        uvt = utile.ap[:, :NCH * w].rearrange("p (c t) -> p c t", c=NCH)
        for eo in range(NCH):
            sv = s_f.ap[:, eo * TT:eo * TT + w]
            nv = nx32v[:, eo, :w]
            uvv = uvt[:, eo, :w]
            tD1 = p.task("dve", [tgr] + s_f.rd() + nx32b.rd(),
                         lambda e, sv=sv, nv=nv: e.tensor_tensor(sv, sv, nv, ALU.subtract))
            tD2 = p.task("dve", [tD1], lambda e, sv=sv: e.tensor_tensor(sv, sv, bgr.ap[:, :w], ALU.mult))
            tD3 = p.task("dve", [tD2], lambda e, sv=sv, nv=nv: e.tensor_tensor(sv, sv, nv, ALU.add))
            tD4 = p.task("dve", [tD3] + utile.rd(), lambda e, sv=sv, uvv=uvv: e.tensor_tensor(sv, sv, uvv, ALU.add))
            s_f.wr(tD4); bgr.r(tD2); nx32b.r(tD3); utile.r(tD4)
            st = p.dma(xov[:, eo, t0:t0 + w], sv, [tD4] + xo.wd())
            xo.wr(st)
            s_f.r(st)

    emit_ln(0)
    for kk in range(1, len(tiles)):
        emit_ln(kk)
        if _A_PART >= 2:
            emit_proj(kk - 1)
    _stk.close()
    p.fence()


def to_fm(u2d):
    t2 = u2d.shape[0]
    v = np.ascontiguousarray(u2d.T.astype(np.float32)).reshape(NCH, P, t2)
    return np.ascontiguousarray(v.transpose(1, 0, 2)).reshape(P, NCH * t2)


def from_fm(arr, t2):
    v = arr.reshape(P, NCH, t2).transpose(1, 0, 2).reshape(D, t2)
    return np.ascontiguousarray(v.T)


def lhsT_fm(Wde):
    dd, ee = Wde.shape
    assert dd == D
    v = np.ascontiguousarray(Wde.astype(np.float32)).reshape(NCH, P, ee)
    return np.ascontiguousarray(v.transpose(1, 0, 2)).reshape(P, NCH * ee)


def host_inputs(full):
    u = np.asarray(full["L0_attn_a_u"], dtype=np.float32)
    WB = np.asarray(full["L0_attn_a_WB"], dtype=np.float32)
    WBf = np.asarray(full["L0_attn_a_WBf"], dtype=np.float32)
    w1 = np.asarray(full["L0_attn_a_w1"], dtype=np.float32)[0]
    w2 = np.asarray(full["L0_attn_a_w2"], dtype=np.float32)[0]
    b = np.asarray(full["L0_attn_a_b"], dtype=np.float32)

    aWBt = lhsT_fm(np.ascontiguousarray(WB.T))
    aWBft = lhsT_fm(np.ascontiguousarray(WBf.T))
    aW12 = np.zeros((P, 2 * NCH), dtype=np.float32)
    for c in range(NCH):
        aW12[:, 2 * c] = w1[c * P:(c + 1) * P]
        aW12[:, 2 * c + 1] = w2[c * P:(c + 1) * P]

    Wqkv = np.asarray(full["L0_ffn_Wqkv"], dtype=np.float32)
    Wo6 = np.asarray(full["L0_ffn_Wo"], dtype=np.float32)
    W16 = np.asarray(full["L0_ffn_W1"], dtype=np.float32)
    W26 = np.asarray(full["L0_ffn_W2"], dtype=np.float32)
    b16 = np.asarray(full["L0_ffn_b1"], dtype=np.float32)
    b26 = np.asarray(full["L0_ffn_b2"], dtype=np.float32)
    lng6 = np.asarray(full["L0_ffn_ln_g"], dtype=np.float32)
    lnb6 = np.asarray(full["L0_ffn_ln_b"], dtype=np.float32)
    qs6 = np.asarray(full["L0_ffn_q_scale"], dtype=np.float32)
    ks6 = np.asarray(full["L0_ffn_k_scale"], dtype=np.float32)

    def lhsT_g(W, nchunks):
        dd, ee = W.shape
        v = np.ascontiguousarray(W).reshape(nchunks, P, ee)
        return np.ascontiguousarray(v.transpose(1, 0, 2)).reshape(P, nchunks * ee)

    fWq = np.concatenate([lhsT_g(Wqkv[l][:, 0:D], NCH) for l in range(6)], 1)
    fWk = np.concatenate([lhsT_g(Wqkv[l][:, D:2 * D], NCH) for l in range(6)], 1)
    fWv = np.concatenate([lhsT_g(Wqkv[l][:, 2 * D:3 * D], NCH) for l in range(6)], 1)
    fWo = np.concatenate([lhsT_g(Wo6[l], NCH) for l in range(6)], 1)
    MP = MCH * P
    f1a_l, f1g_l, f2_l, fb_l = [], [], [], []
    for l in range(6):
        W1a = np.zeros((D, MP), np.float32); W1a[:, :2730] = W16[l][:, :2730]
        W1g = np.zeros((D, MP), np.float32); W1g[:, :2730] = W16[l][:, 2730:]
        a4 = W1a.reshape(NCH, P, MCH, P).transpose(1, 2, 0, 3).reshape(P, MCH * NCH * P)
        g4 = W1g.reshape(NCH, P, MCH, P).transpose(1, 2, 0, 3).reshape(P, MCH * NCH * P)
        f1a_l.append(np.ascontiguousarray(a4)); f1g_l.append(np.ascontiguousarray(g4))
        W2p = np.zeros((MP, D), np.float32); W2p[:2730] = W26[l]
        f2_l.append(lhsT_g(W2p, MCH))
        b1a = np.zeros((MCH * P,), np.float32); b1a[:2730] = b16[l][:2730]
        b1g = np.zeros((MCH * P,), np.float32); b1g[:2730] = b16[l][2730:]
        cols = np.zeros((P, 2 * MCH + 3 * NCH + 4), np.float32)
        cols[:, 0:MCH] = b1a.reshape(MCH, P).T
        cols[:, MCH:2 * MCH] = b1g.reshape(MCH, P).T
        cols[:, 2 * MCH:2 * MCH + NCH] = b26[l].reshape(NCH, P).T
        cols[:, 2 * MCH + NCH:2 * MCH + 2 * NCH] = lng6[l].reshape(NCH, P).T
        cols[:, 2 * MCH + 2 * NCH:2 * MCH + 3 * NCH] = lnb6[l].reshape(NCH, P).T
        e64 = np.arange(P) % 64
        cols[:, 2 * MCH + 3 * NCH + 0] = qs6[l][e64]
        cols[:, 2 * MCH + 3 * NCH + 1] = qs6[l][(e64 + 32) % 64]
        cols[:, 2 * MCH + 3 * NCH + 2] = ks6[l][e64]
        cols[:, 2 * MCH + 3 * NCH + 3] = ks6[l][(e64 + 32) % 64]
        fb_l.append(cols)
    fW1a = np.concatenate(f1a_l, 1); fW1g = np.concatenate(f1g_l, 1)
    fW2 = np.concatenate(f2_l, 1); fbALL = np.concatenate(fb_l, 1)

    cosT, sinT, scaleT = rotary_tables(4096, 64, 256)
    tq1 = cosT * scaleT * 8.0
    tq2 = sinT * scaleT * 8.0
    tk1 = cosT / scaleT
    tk2 = sinT / scaleT
    e64 = np.arange(P) % 64

    def tab_for(half):
        base = half * T
        out = np.zeros((P, 4 * (T + 256)), np.float32)
        for ti, tab in enumerate((tq1, tq2, tk1, tk2)):
            for j in range(T + 256):
                t = base + j - 256
                if t < 0:
                    continue
                out[:, ti * (T + 256) + j] = tab[t][e64]
        return out.astype(BFD)
    tab_h = [tab_for(0), tab_for(1)]

    def mk_masks(first):
        out = np.zeros((P, 2 * 4 * 256), dtype=np.float32)
        for mtype in range(2):
            for jc in range(4):
                for j in range(P):
                    for i in range(256):
                        delta = 256 + i - (jc * 128 + j)
                        ok = 0 <= delta <= 256
                        if mtype == 0 and first and jc < 2:
                            ok = False
                        out[j, (mtype * 4 + jc) * 256 + i] = 1.0 if ok else 0.0
        return out.astype(BFD)
    masks_first = mk_masks(True)
    masks_rest = mk_masks(False)

    R2 = build_R()
    fRm = np.ascontiguousarray(R2.T).astype(BFD)
    fblk2 = np.zeros((P, 8 * 16), np.float32)
    for hp in range(8):
        fblk2[0:64, hp * 16 + 2 * hp] = 1.0
        fblk2[64:128, hp * 16 + 2 * hp + 1] = 1.0
    fblk2 = fblk2.astype(BFD)
    fselh = np.zeros((16, 8 * P), np.float32)
    for hp in range(8):
        fselh[2 * hp, hp * P:hp * P + 64] = 1.0
        fselh[2 * hp + 1, hp * P + 64:(hp + 1) * P] = 1.0
    fselh = fselh.astype(BFD)

    mWin_f = np.asarray(full.get("L0_attn_b_W_in", np.zeros((D, 4096))), dtype=np.float32)
    mXp_f = np.asarray(full.get("L0_attn_b_W_xproj", np.zeros((2048, 96))), dtype=np.float32)
    mWdt_f = np.asarray(full.get("L0_attn_b_W_dt", np.zeros((64, 2048))), dtype=np.float32)
    mWout_f = np.asarray(full.get("L0_attn_b_W_out", np.zeros((2048, D))), dtype=np.float32)
    mcw_f = np.asarray(full.get("L0_attn_b_conv_w", np.zeros((2048, 1, 4))), dtype=np.float32)
    mcb_f = np.asarray(full.get("L0_attn_b_conv_b", np.zeros((2048,))), dtype=np.float32)
    mbdt_f = np.asarray(full.get("L0_attn_b_b_dt", np.zeros((2048,))), dtype=np.float32)
    mD_f = np.asarray(full.get("L0_attn_b_D", np.zeros((2048,))), dtype=np.float32)
    mAlog_f = np.asarray(full.get("L0_attn_b_A_log", np.zeros((2048, 16))), dtype=np.float32)
    mlng_f = np.asarray(full.get("L0_attn_b_ln_g", np.zeros((D,))), dtype=np.float32)
    mlnb_f = np.asarray(full.get("L0_attn_b_ln_b", np.zeros((D,))), dtype=np.float32)

    def mk_mamba(half):
        ch0 = half * 1024
        winsel = np.concatenate([mWin_f[:, ch0:ch0 + 1024],
                                 mWin_f[:, 2048 + ch0:2048 + ch0 + 1024]], axis=1)
        mWin = np.ascontiguousarray(winsel.reshape(NCH, P, 2048).transpose(1, 0, 2)).reshape(P, NCH * 2048)
        xps = mXp_f[ch0:ch0 + 1024]
        mXp = np.ascontiguousarray(xps.reshape(NCH, P, 96).transpose(1, 0, 2)).reshape(P, NCH * 96)
        mWdt = np.zeros((P, NCH * P), np.float32)
        mWdt[0:64, :] = mWdt_f[:, ch0:ch0 + 1024]
        wos = mWout_f[ch0:ch0 + 1024]
        mWout = np.ascontiguousarray(wos.reshape(NCH, P, 1024).transpose(1, 0, 2)).reshape(P, NCH * 1024)
        vec = np.zeros((P, 200), np.float32)
        cws = mcw_f[ch0:ch0 + 1024, 0, :]
        vec[:, 0:32] = cws.reshape(NCH, P, 4).transpose(1, 0, 2).reshape(P, 32)
        vec[:, 32:40] = mcb_f[ch0:ch0 + 1024].reshape(NCH, P).T
        vec[:, 40:48] = mbdt_f[ch0:ch0 + 1024].reshape(NCH, P).T
        vec[:, 48:56] = mD_f[ch0:ch0 + 1024].reshape(NCH, P).T
        vec[:, 56:184] = mAlog_f[ch0:ch0 + 1024].reshape(NCH, P, 16).transpose(1, 0, 2).reshape(P, 128)
        vec[:, 184:192] = mlng_f.reshape(NCH, P).T
        vec[:, 192:200] = mlnb_f.reshape(NCH, P).T
        selbc = np.zeros((32, 32 * P), np.float32)
        for j in range(32):
            selbc[j, j * P:(j + 1) * P] = 1.0
        return dict(mWin=mWin, mXp=mXp, mWdt=mWdt, mWout=mWout,
                    mvec=vec, mselbc=selbc.astype(BFD))
    mamba_maps = [mk_mamba(0), mk_mamba(1)]

    maps = []
    for core in range(8):
        bidx, half = core // 2, core % 2
        rows = u[bidx, half * T:(half + 1) * T]
        fl = np.zeros((P, 2), dtype=np.float32)
        fl[:, 0] = 1.0 if half == 0 else 0.0
        fl[:, 1] = 1.0 if half == 1 else 0.0
        m = {
            "x_fm": to_fm(rows),
            "aWBt": aWBt, "aWBft": aWBft, "aW12": aW12,
            "ab": b.reshape(1, 1).astype(np.float32),
            "flags": fl,
            "c_ones_in": np.ones((P, P), dtype=BFD),
            "fWq": fWq, "fWk": fWk, "fWv": fWv, "fWo": fWo,
            "fW1a": fW1a, "fW1g": fW1g, "fW2": fW2, "fb": fbALL,
            "ftab": tab_h[half],
            "fmasks": masks_first if half == 0 else masks_rest,
            "fRm": fRm, "fblk2": fblk2, "fselh": fselh,
            "fnotfirst": np.full((P, 1), 0.0 if half == 0 else 1.0, dtype=np.float32),
        }
        hrow = u[bidx, T] if half == 0 else np.zeros((D,), dtype=np.float32)
        m["a_uhalo"] = np.ascontiguousarray(
            hrow.astype(np.float32).reshape(NCH, P).transpose(1, 0)[:, :, None])
        m.update(mamba_maps[half])
        maps.append(m)
    return maps


_N_STAGES = 4
_A_PART = 3
_A_SUB = 9
_F_LAYERS = 6
_F_ATTN = True
_F_MLP = True
_F_ASCHED = 4
_F_AWIN = 5
_F_TRD = 1
_F_SCH2 = 2


class Xchg:
    def __init__(self, k):
        self.k = k
        self.bar_in = k.dram("bar_in", [1, 4])
        self.bar_out = k.dram("bar_out", [1, 8])
        self.last_bar = None
        self.last_reads = []
        t0 = k.p.dma(self.bar_in.ap[:, :], k._fc_buf.ap[0:1, 0:4], k._fc_buf.rd())
        self.bar_in.wr(t0)

    def send_recv(self, sh_pair, src_fn, dst_fn, wdeps, rdeps=()):
        k, p = self.k, self.k.p
        if not hasattr(self, "xcount"):
            self.xcount = 0
        sh = sh_pair[self.xcount % 2]
        self.xcount += 1

        wdeps = list(wdeps) + sh.wd() + ([self.last_bar] if self.last_bar else [])
        si = p.dma_sem("sp")
        if p.dma_use[si] > 0:
            wdeps.append((f"dq{si}", 16 * p.dma_use[si]))
        p.dma_use[si] += 1
        wtok = (f"dq{si}", 16 * p.dma_use[si])
        p.tasks["sp"].append((p._merge("sp", wdeps),
                              lambda e: e.dma_start(out=src_fn(k.p_ctx, sh.ap)[0], in_=src_fn(k.p_ctx, sh.ap)[1]), wtok))
        sh.wr(wtok)
        bar = p.barrier(self.bar_in.ap[:, :], self.bar_out.ap[:, :],
                        [wtok] + self.last_reads + ([self.last_bar] if self.last_bar else []))
        self.last_bar = bar
        rdeps2 = [bar] + list(rdeps)
        si = p.dma_sem("sp")
        if p.dma_use[si] > 0:
            rdeps2.append((f"dq{si}", 16 * p.dma_use[si]))
        p.dma_use[si] += 1
        rtok = (f"dq{si}", 16 * p.dma_use[si])
        p.tasks["sp"].append((p._merge("sp", rdeps2),
                              lambda e: e.dma_start(out=dst_fn(k.p_ctx, sh.ap)[0], in_=dst_fn(k.p_ctx, sh.ap)[1]), rtok))
        sh.r(rtok)
        self.last_reads = [rtok]
        return rtok


def _ds1(ap, slot):
    v = ap[bass.ds(slot, 1)]
    return v.squeeze(0)


def xc_dma_dyn(xc, mk_pair, deps):
    k, p = xc.k, xc.k.p
    si = p.dma_sem("sp")
    deps = list(deps)
    if p.dma_use[si] > 0:
        deps.append((f"dq{si}", 16 * p.dma_use[si]))
    p.dma_use[si] += 1
    tok = (f"dq{si}", 16 * p.dma_use[si])

    def emit(e, mk_pair=mk_pair):
        try:
            o, i = mk_pair(k.p_ctx)
            return e.dma_start(out=o, in_=i)
        except Exception:
            import traceback as _tb
            _tb.print_exc()
            try:
                o, i = mk_pair(k.p_ctx)
                print("OUT AP:", o)
                print("IN AP:", i)
            except Exception:
                pass
            raise RuntimeError(f"xc_dma_dyn failed for fn defined at line "
                               f"{mk_pair.__code__.co_firstlineno}")
    p.tasks["sp"].append((p._merge("sp", deps), emit, tok))
    return tok


def xc_bar(xc, deps):
    deps = list(deps) + xc.last_reads + ([xc.last_bar] if xc.last_bar else [])
    b = xc.k.p.barrier(xc.bar_in.ap[:, :], xc.bar_out.ap[:, :], deps)
    xc.last_bar = b
    xc.last_reads = []
    return b


def emit_outer_ln_tile(k, C, xo, xov, dstb, dstv, t0, w, B):
    p = k.p
    ones = C["ones"]
    scr_bf, sq_bf, rows, rows_bf, xo_tile = B["scr"], B["sq"], B["rows"], B["rowsbf"], B["xot"]
    src = xo_tile.ap[:, :NCH * w].rearrange("p (c t) -> p c t", c=NCH)
    ldu = p.dma(src, xov[:, :, t0:t0 + w], xo.rd() + xo_tile.wd())
    xo_tile.wr(ldu, reset=True)
    sbv = scr_bf.ap[:, :NCH * w].rearrange("p (c t) -> p c t", c=NCH)
    tc = p.task("pool", xo_tile.rd() + scr_bf.wd(), lambda e: e.tensor_copy(sbv, src))
    scr_bf.wr(tc, reset=True); xo_tile.r(tc)
    sqv = sq_bf.ap[:, :NCH * w].rearrange("p (c t) -> p c t", c=NCH)
    tq = p.task("act", [tc] + sq_bf.wd(), lambda e: e.activation(sqv, sbv, AF.Square))
    sq_bf.wr(tq, reset=True); scr_bf.r(tq)
    bsum, bsq = k.bank(), k.bank()

    def mm_s(e):
        last = None
        for ci in range(NCH):
            last = e.matmul(bsum.ap[0:1, :w], ones.ap[:, 0:1], sbv[:, ci, :], start=(ci == 0), stop=(ci == NCH - 1))
        return last
    ts = p.task("pe", [tc] + bsum.wd() + ones.rd(), mm_s)
    bsum.wr(ts, reset=True); scr_bf.r(ts); ones.r(ts)

    def mm_q(e):
        last = None
        for ci in range(NCH):
            last = e.matmul(bsq.ap[0:1, :w], ones.ap[:, 0:1], sqv[:, ci, :], start=(ci == 0), stop=(ci == NCH - 1))
        return last
    tq2 = p.task("pe", [tq] + bsq.wd() + ones.rd(), mm_q)
    bsq.wr(tq2, reset=True); sq_bf.r(tq2); ones.r(tq2)
    mu = rows.ap[:, 0:w]; m2 = rows.ap[:, 512:512 + w]
    var = rows.ap[:, 1024:1024 + w]; rstd = rows.ap[:, 1536:1536 + w]
    t1 = p.task("dve", [ts] + rows.wd() + bsum.wd(), lambda e: e.tensor_scalar_mul(mu, bsum.ap[0:1, :w], 1.0 / D))
    bsum.r(t1)
    t2 = p.task("dve", [tq2, t1] + bsq.wd(), lambda e: e.tensor_scalar_mul(m2, bsq.ap[0:1, :w], 1.0 / D))
    bsq.r(t2)
    t3 = p.task("dve", [t2], lambda e: e.tensor_tensor(var, mu, mu, ALU.mult))
    t4 = p.task("dve", [t3], lambda e: e.tensor_tensor(var, m2, var, ALU.subtract))
    t5 = p.task("act", [t4], lambda e, b=k.fconst(LN_EPS): e.activation(rstd, var, AF.Ln, bias=b))
    t6 = p.task("act", [t5], lambda e: e.activation(rstd, rstd, AF.Exp, scale=-0.5))
    rows.wr(t6)
    t8 = p.task("dve", [t6], lambda e: e.tensor_tensor(mu, mu, rstd, ALU.mult))
    t9 = p.task("dve", [t8], lambda e: e.tensor_scalar_mul(mu, mu, -1.0))
    rows.wr(t9)
    rh = rows_bf.ap[:, 0:w]; rl = rows_bf.ap[:, 512:512 + w]
    mh = rows_bf.ap[:, 1024:1024 + w]; ml = rows_bf.ap[:, 1536:1536 + w]
    c1 = p.task("dve", [t9] + rows_bf.wd(), lambda e: e.tensor_copy(rh, rstd))
    c2 = p.task("dve", [c1], lambda e: e.tensor_tensor(rl, rstd, rh, ALU.subtract))
    c3 = p.task("dve", [c2], lambda e: e.tensor_copy(mh, mu))
    c4 = p.task("dve", [c3], lambda e: e.tensor_tensor(ml, mu, mh, ALU.subtract))
    rows_bf.wr(c4, reset=True); rows.r(c4)
    br1, br2 = k.bank(), k.bank()
    tr1 = p.task("pe", [c4] + br1.wd() + ones.rd(), lambda e: (
        e.matmul(br1.ap[:, :w], ones.ap[0:1, :], rh, start=True, stop=False),
        e.matmul(br1.ap[:, :w], ones.ap[0:1, :], rl, start=False, stop=True))[-1])
    tr2 = p.task("pe", [c4] + br2.wd() + ones.rd(), lambda e: (
        e.matmul(br2.ap[:, :w], ones.ap[0:1, :], mh, start=True, stop=False),
        e.matmul(br2.ap[:, :w], ones.ap[0:1, :], ml, start=False, stop=True))[-1])
    br1.wr(tr1, reset=True); br2.wr(tr2, reset=True); rows_bf.r(tr2); ones.r(tr1); ones.r(tr2)
    tA = p.task("dve", [tr1] + xo_tile.rd() + dstb.wd() + br1.wd(),
                lambda e: e.tensor_tensor(dstv, src, br1.ap[:, None, 0:w].broadcast_to([P, NCH, w]), ALU.mult))
    br1.r(tA); xo_tile.r(tA)
    tB2 = p.task("dve", [tA, tr2] + br2.wd(),
                 lambda e: e.tensor_tensor(dstv, dstv, br2.ap[:, None, 0:w].broadcast_to([P, NCH, w]), ALU.add))
    dstb.wr(tB2)
    br2.r(tB2)
    return tB2


def stream_weight_bf16(k, dst, dst_off, src_ap, src_off, n, stgs=None, piece=8192):
    p = k.p
    for i in range(0, n, piece):
        wp = min(piece, n - i)
        ld = p.dma(dst.ap[:, dst_off + i:dst_off + i + wp],
                   src_ap[:, src_off + i:src_off + i + wp], dst.wd(), eng="pool")
        dst.wr(ld)


HF = 258
LTF = HF + T
MCH = 22


def stage_ffn(k, xo, C, xc, tag):
    p = k.p
    nc = k.nc
    ones = C["ones"]
    Rm, blk2, selh, masks = C["Rmat"], C["blk2"], C["selh"], C["masks"]
    shx = C["shxp"]
    xov = xo.ap.rearrange("p (c t) -> p c t", c=NCH)

    fWq = k.inp("fWq", [P, 6 * NCH * D]); fWk = k.inp("fWk", [P, 6 * NCH * D])
    fWv = k.inp("fWv", [P, 6 * NCH * D]); fWo = k.inp("fWo", [P, 6 * NCH * D])
    fW1a = k.inp("fW1a", [P, 6 * MCH * NCH * P]); fW1g = k.inp("fW1g", [P, 6 * MCH * NCH * P])
    fW2 = k.inp("fW2", [P, 6 * MCH * D])
    fb = k.inp("fb", [P, 6 * (2 * MCH + 3 * NCH + 4)])
    ftab = k.inp("ftab", [P, 4 * (T + 256)], BF16)
    fnotfirst = k.inp("fnotfirst", [P, 1])

    with ExitStack() as stk:
        E = stk.enter_context

        def sb(name, shape, dt):
            return Buf(E(nc.sbuf_tensor(tag + name, list(shape), dt)).ap(), tag=tag + name)

        p.fence()
        AR = Arena(k, tag + "arena", 125 * 1024, stack=stk)
        xs = sb("xs", [P, NCH * LTF], F32)
        xsv = xs.ap.rearrange("p (c t) -> p c t", c=NCH)
        nf = sb("nf", [P, 1], F32)
        nf.wr(p.dma(nf.ap[:, :], fnotfirst[:, :], []))
        stgs = None
        colstg = sb("colstg", [P, 2 * NCH], F32)

        def emit_ln_tiles(A, t0, w, src_of_chunk, dst_of_chunk, gb=None, src_buf=None,
                          dst_buf=None, fix_src=None):
            scr_bf, sq_bf, rows, rows_bf, asc = A["scr"], A["sq"], A["rows"], A["rowsbf"], A["asc"]
            sbv = scr_bf.ap[:, :NCH * w].rearrange("p (c t) -> p c t", c=NCH)

            def emsh(e):
                last = None
                for c in range(NCH):
                    last = e.tensor_copy(sbv[:, c, :], src_of_chunk(c))
                return last
            sdep = src_buf if src_buf is not None else xs
            tc = p.task("pool", sdep.rd() + scr_bf.wd(), emsh)
            scr_bf.wr(tc, reset=True); sdep.r(tc)
            if fix_src is not None:
                tc2 = p.task("pool", [tc] + fix_src[1].rd() + scr_bf.wd(),
                             lambda e: e.tensor_copy(sbv[:, NCH // 2:NCH, 0:1], fix_src[0]))
                scr_bf.wr(tc2); fix_src[1].r(tc2)
                tc = tc2
            sqv = sq_bf.ap[:, :NCH * w].rearrange("p (c t) -> p c t", c=NCH)
            tq = p.task("act", [tc] + sq_bf.wd(), lambda e: e.activation(sqv, sbv, AF.Square))
            sq_bf.wr(tq, reset=True); scr_bf.r(tq)
            bsum, bsq = k.bank(), k.bank()

            def mm_s(e):
                last = None
                for ci in range(NCH):
                    last = e.matmul(bsum.ap[0:1, :w], ones.ap[:, 0:1], sbv[:, ci, :],
                                    start=(ci == 0), stop=(ci == NCH - 1))
                return last
            ts = p.task("pe", [tc] + bsum.wd() + ones.rd(), mm_s)
            bsum.wr(ts, reset=True); scr_bf.r(ts)

            def mm_q(e):
                last = None
                for ci in range(NCH):
                    last = e.matmul(bsq.ap[0:1, :w], ones.ap[:, 0:1], sqv[:, ci, :],
                                    start=(ci == 0), stop=(ci == NCH - 1))
                return last
            tq2 = p.task("pe", [tq] + bsq.wd() + ones.rd(), mm_q)
            bsq.wr(tq2, reset=True); sq_bf.r(tq2)
            mu = rows.ap[:, 0:w]; m2 = rows.ap[:, 512:512 + w]
            var = rows.ap[:, 1024:1024 + w]; rstd = rows.ap[:, 1536:1536 + w]
            t1 = p.task("dve", [ts] + rows.wd() + bsum.wd(), lambda e: e.tensor_scalar_mul(mu, bsum.ap[0:1, :w], 1.0 / D))
            bsum.r(t1)
            t2 = p.task("dve", [tq2, t1] + bsq.wd(), lambda e: e.tensor_scalar_mul(m2, bsq.ap[0:1, :w], 1.0 / D))
            bsq.r(t2)
            t3 = p.task("dve", [t2], lambda e: e.tensor_tensor(var, mu, mu, ALU.mult))
            t4 = p.task("dve", [t3], lambda e: e.tensor_tensor(var, m2, var, ALU.subtract))
            t5 = p.task("act", [t4], lambda e, b=k.fconst(LN_EPS): e.activation(rstd, var, AF.Ln, bias=b))
            t6 = p.task("act", [t5], lambda e: e.activation(rstd, rstd, AF.Exp, scale=-0.5))
            t8 = p.task("dve", [t6], lambda e: e.tensor_tensor(mu, mu, rstd, ALU.mult))
            t9 = p.task("dve", [t8], lambda e: e.tensor_scalar_mul(mu, mu, -1.0))
            rows.wr(t9)
            rh = rows_bf.ap[:, 0:w]; rl = rows_bf.ap[:, 512:512 + w]
            mh = rows_bf.ap[:, 1024:1024 + w]; ml = rows_bf.ap[:, 1536:1536 + w]
            c1 = p.task("dve", [t9] + rows_bf.wd(), lambda e: e.tensor_copy(rh, rstd))
            c2 = p.task("dve", [c1], lambda e: e.tensor_tensor(rl, rstd, rh, ALU.subtract))
            c3 = p.task("dve", [c2], lambda e: e.tensor_copy(mh, mu))
            c4 = p.task("dve", [c3], lambda e: e.tensor_tensor(ml, mu, mh, ALU.subtract))
            rows_bf.wr(c4, reset=True); rows.r(c4)
            br1, br2 = k.bank(), k.bank()
            tr1 = p.task("pe", [c4] + br1.wd() + ones.rd(), lambda e: (
                e.matmul(br1.ap[:, :w], ones.ap[0:1, :], rh, start=True, stop=False),
                e.matmul(br1.ap[:, :w], ones.ap[0:1, :], rl, start=False, stop=True))[-1])
            tr2 = p.task("pe", [c4] + br2.wd() + ones.rd(), lambda e: (
                e.matmul(br2.ap[:, :w], ones.ap[0:1, :], mh, start=True, stop=False),
                e.matmul(br2.ap[:, :w], ones.ap[0:1, :], ml, start=False, stop=True))[-1])
            br1.wr(tr1, reset=True); br2.wr(tr2, reset=True); rows_bf.r(tr2)
            toks = []
            for c in range(NCH):
                tB2 = p.task("dve", [tr1] + scr_bf.rd() + asc.wd() + br1.wd(),
                             lambda e, c=c: e.tensor_tensor(asc.ap[:, :w], sbv[:, c, :], br1.ap[:, :w], ALU.mult))
                br1.r(tB2); scr_bf.r(tB2)
                tC2 = p.task("dve", [tB2] + br2.wd(),
                             lambda e: e.tensor_tensor(asc.ap[:, :w], asc.ap[:, :w], br2.ap[:, :w], ALU.add))
                br2.r(tC2)
                asc.wr(tC2, reset=True)
                ddep = [] if dst_buf is None else dst_buf.wd()
                if gb is None:
                    tD = p.task("dve", [tC2] + ddep,
                                lambda e, c=c: e.tensor_copy(dst_of_chunk(c), asc.ap[:, :w]))
                else:
                    gcol, bcol, gbuf = gb
                    tD = p.task("dve", [tC2] + ddep + gbuf.rd(),
                                lambda e, c=c, gcol=gcol, bcol=bcol: e.tensor_scalar(
                                    dst_of_chunk(c), asc.ap[:, :w], gcol(c), bcol(c), ALU.mult, ALU.add))
                    gbuf.r(tD)
                asc.r(tD)
                if dst_buf is not None:
                    dst_buf.wr(tD)
                toks.append(tD)
            return toks

        if True:
            AR.reset()
            sb2 = AR.alloc
            A = {"scr": sb2("scr", [P, NCH * 512], BF16), "sq": sb2("sq", [P, NCH * 512], BF16),
                 "rows": sb2("rows", [1, 4 * 512], F32), "rowsbf": sb2("rowsbf", [1, 4 * 512], BF16),
                 "asc": sb2("asc", [P, 512], F32), "xot": sb2("xot", [P, NCH * 512], F32)}
            xot = A["xot"]
            for kk in range(4):
                t0 = kk * 512
                xsrc = xot.ap.rearrange("p (c t) -> p c t", c=NCH)
                ld = p.dma(xsrc, xov[:, :, t0:t0 + 512], xo.rd() + xot.wd())
                xot.wr(ld, reset=True)
                emit_ln_tiles(A, t0, 512,
                              lambda c, xsrc=xsrc: xsrc[:, c, :],
                              lambda c, t0=t0: xsv[:, c, HF + t0:HF + t0 + 512],
                              src_buf=xot, dst_buf=xs)
            p.fence()

        def halo_exchange_full():
            def wfn(ctx, shap):
                return (_ds1(shap, ctx["my"]).rearrange("p (c t) -> p c t", c=NCH),
                        xsv[:, :, HF + T - HF:HF + T])

            def rfn(ctx, shap):
                return (xsv[:, :, 0:HF], _ds1(shap, ctx["oth"]).rearrange("p (c t) -> p c t", c=NCH))
            rt = xc.send_recv(shx, wfn, rfn, xs.rd(), xs.wd())
            xs.wr(rt)
            tz = p.task("dve", [rt] + xs.wd() + nf.rd(),
                        lambda e: e.tensor_scalar_mul(xsv[:, :, 0:HF], xsv[:, :, 0:HF], nf.ap[:, 0:1]))
            xs.wr(tz)
            nf.r(tz)
            return tz

        def halo_exchange_col():
            tc1 = p.task("dve", xs.rd() + colstg.wd(),
                         lambda e: e.tensor_copy(colstg.ap[:, 0:NCH], xsv[:, :, HF + T - 1]))
            colstg.wr(tc1, reset=True)

            def wfn(ctx, shap):
                return (_ds1(shap, ctx["my"])[:, 0:NCH], colstg.ap[:, 0:NCH])

            def rfn(ctx, shap):
                return (colstg.ap[:, NCH:2 * NCH], _ds1(shap, ctx["oth"])[:, 0:NCH])
            rt = xc.send_recv(shx, wfn, rfn, colstg.rd(), colstg.wd())
            colstg.wr(rt)
            tz = p.task("dve", [rt] + xs.wd() + nf.rd() + colstg.rd(),
                        lambda e: e.tensor_scalar_mul(xsv[:, :, HF - 1], colstg.ap[:, NCH:2 * NCH], nf.ap[:, 0:1]))
            xs.wr(tz)
            nf.r(tz); colstg.r(tz)
            return tz

        halo_exchange_full()

        for layer in range(_F_LAYERS):
            lw = layer * NCH * D
            off = layer * (2 * MCH + 3 * NCH + 4)

            if _F_ATTN:
                p.fence()
                AR.reset()
                sb2 = AR.alloc
                wq = sb2("wq", [P, NCH * D], BF16); wk_ = sb2("wk", [P, NCH * D], BF16)
                wv = sb2("wv", [P, NCH * D], BF16); wo = sb2("wo", [P, NCH * D], BF16)
                kring = sb2("kring", [P, 8 * 4 * 256], BF16)
                qsb = sb2("qsb", [P, 2 * 8 * 256], BF16)
                vring = sb2("vring", [P, 4 * 16 * 65], BF16)
                vsc2 = sb2("vsc2", [P, 2 * 4 * 2 * 65], BF16)
                vsc2p = [Buf(vsc2.ap[:, 0:520], tag="vsc2a"), Buf(vsc2.ap[:, 520:1040], tag="vsc2b")]
                probs = sb2("probs", [P, 2 * 4 * 512], BF16)
                probsp = [Buf(probs.ap[:, 0:2048], tag="probsa"), Buf(probs.ap[:, 2048:4096], tag="probsb")]
                negB = sb2("negB", [P, 4 * 16], F32)
                negBp = [Buf(negB.ap, tag="negBa"), Buf(negB.ap, tag="negBb")]
                ebuf = sb2("ebuf", [P, 4 * 16], F32)
                ebufp = [Buf(ebuf.ap, tag="ebufa"), Buf(ebuf.ap, tag="ebufb")]
                sums = sb2("sums", [1, 512], F32)
                outbf = sb2("outbf", [P, NCH * 256], BF16)
                xbf = [sb2("xbf", [P, NCH * 256], BF16)]
                tabs = sb2("tabs", [P, 4 * 256], BF16)
                qksc = sb2("qksc", [P, 4], F32)
                sqb = sb2("sqb", [P, 256], BF16)
                rn32 = sb2("rn32", [P, 256], F32)
                rnb = sb2("rnb", [P, 256], BF16)
                qh32 = sb2("qh32", [P, 256], F32)
                qhb = sb2("qhb", [P, 256], BF16)
                u1b = sb2("u1b", [P, 256], F32)
                repsb = sb2("repsb", [P, 256], F32)
                svcol = sb2("svcol", [P, 4 * 7], F32)
                svcolv = svcol.ap.rearrange("p (c w) -> p c w", c=4)
                tsv = p.task("pool", xs.rd() + svcol.wd(),
                             lambda e: [e.tensor_copy(
                                 svcolv[:, :, w_ - 1:w_],
                                 xsv[:, NCH // 2:NCH, HF + w_ * 256 - 1:HF + w_ * 256])
                                 for w_ in range(1, NW)][-1])
                svcol.wr(tsv, reset=True); xs.r(tsv)

                stream_weight_bf16(k, wq, 0, fWq, lw, NCH * D, stgs, piece=1024)
                stream_weight_bf16(k, wk_, 0, fWk, lw, NCH * D, stgs, piece=1024)
                stream_weight_bf16(k, wv, 0, fWv, lw, NCH * D, stgs, piece=1024)
                stream_weight_bf16(k, wo, 0, fWo, lw, NCH * D, stgs, piece=1024)
                ldq = p.dma(qksc.ap[:, :], fb[:, off + 2 * MCH + 3 * NCH:off + 2 * MCH + 3 * NCH + 4], qksc.wd())
                qksc.wr(ldq, reset=True)

                vov = vring.ap.rearrange("p (s h e) -> p s h e", s=4, h=16)
                tvo = p.task("dve", vring.wd(), lambda e: e.memset(vov[:, :, :, 64:65], 1.0))
                vring.wr(tvo)

                wqv = wq.ap.rearrange("p (c e) -> p c e", c=NCH)
                wkv2 = wk_.ap.rearrange("p (c e) -> p c e", c=NCH)
                wvv = wv.ap.rearrange("p (c e) -> p c e", c=NCH)
                wov = wo.ap.rearrange("p (c e) -> p c e", c=NCH)

                def kslot(w):
                    return 3 if w == 0 else (w % 3)

                def vslot(jcg):
                    return (jcg + 4) % 4

                def cast_shift_win(wk0, buf):
                    def em(e):
                        last = None
                        for c in range(NCH):
                            off2 = HF + wk0 * 256 - (1 if c >= NCH // 2 else 0)
                            last = e.tensor_copy(buf.ap[:, c * 256:(c + 1) * 256], xsv[:, c, off2:off2 + 256])
                        return last
                    t = p.task("pool", xs.rd() + buf.wd(), em)
                    buf.wr(t, reset=True); xs.r(t)
                    if wk0 >= 1:
                        t2 = p.task("pool", [t] + svcol.rd() + buf.wd(),
                                    lambda e, buf=buf, wk0=wk0: e.tensor_copy(
                                        buf.ap[:, :].rearrange("p (c t) -> p c t", c=NCH)[:, NCH // 2:NCH, 0:1],
                                        svcolv[:, :, wk0 - 1:wk0]))
                        buf.wr(t2); svcol.r(t2)
                        t = t2
                    return t

                def load_tabs(wk0):
                    toks = []
                    for ti in range(4):
                        ld = p.dma(tabs.ap[:, ti * 256:(ti + 1) * 256],
                                   ftab[:, ti * (T + 256) + (wk0 + 1) * 256: ti * (T + 256) + (wk0 + 2) * 256],
                                   tabs.wd())
                        tabs.wr(ld)
                        toks.append(ld)
                    tt = p.task("pool", toks + tabs.wd() + qksc.rd(), lambda e: [
                        e.tensor_scalar_mul(tabs.ap[:, ti * 256:(ti + 1) * 256],
                                            tabs.ap[:, ti * 256:(ti + 1) * 256],
                                            qksc.ap[:, ti:ti + 1]) for ti in range(4)][-1])
                    tabs.wr(tt); qksc.r(tt)
                    return tt, 0

                def rotary(psumb, mmtok, hp, is_q, dst_buf, dst_ap, tabtok, tabsl):
                    W = 256
                    t1 = p.task("act", [mmtok] + sqb.wd() + psumb.wd(),
                                lambda e: e.activation(sqb.ap[:, :W], psumb.ap[:, :W], AF.Square))
                    psumb.r(t1); sqb.wr(t1, reset=True)
                    bn = k.bank()
                    tn = p.task("pe", [t1] + bn.wd() + blk2.rd(),
                                lambda e: e.matmul(bn.ap[0:16, :W], blk2.ap[:, hp * 16:(hp + 1) * 16],
                                                   sqb.ap[:, :W], start=True, stop=True))
                    bn.wr(tn, reset=True); blk2.r(tn); sqb.r(tn)
                    tmx = p.task("dve", [tn] + rn32.wd() + bn.wd(),
                                 lambda e: e.tensor_scalar_max(rn32.ap[0:16, :W], bn.ap[0:16, :W], 1e-24))
                    bn.r(tmx); rn32.wr(tmx, reset=True)
                    tl = p.task("act", [tmx], lambda e: e.activation(rn32.ap[0:16, :W], rn32.ap[0:16, :W], AF.Ln))
                    te = p.task("act", [tl], lambda e: e.activation(rn32.ap[0:16, :W], rn32.ap[0:16, :W], AF.Exp, scale=-0.5))
                    rn32.wr(te)
                    tb = p.task("dve", [te] + rnb.wd(),
                                lambda e: e.tensor_copy(rnb.ap[0:16, :W], rn32.ap[0:16, :W]))
                    rnb.wr(tb, reset=True); rn32.r(tb)
                    brep = k.bank()
                    trep = p.task("pe", [tb] + brep.wd() + selh.rd(),
                                  lambda e: e.matmul(brep.ap[:, :W], selh.ap[:, hp * P:(hp + 1) * P],
                                                     rnb.ap[0:16, :W], start=True, stop=True))
                    brep.wr(trep, reset=True); selh.r(trep); rnb.r(trep)
                    t3a = p.task("act", [trep] + qh32.wd() + brep.wd(),
                                 lambda e: e.copy(qh32.ap[:, :W], brep.ap[:, :W]))
                    brep.r(t3a); qh32.wr(t3a, reset=True)
                    t3 = p.task("dve", [t3a] + psumb.wd(),
                                lambda e: e.tensor_tensor(qh32.ap[:, :W], psumb.ap[:, :W], qh32.ap[:, :W], ALU.mult))
                    psumb.r(t3); qh32.wr(t3)
                    t4 = p.task("pool", [t3] + qhb.wd(), lambda e: e.tensor_copy(qhb.ap[:, :W], qh32.ap[:, :W]))
                    qhb.wr(t4, reset=True); qh32.r(t4)
                    bR = k.bank()
                    t5 = p.task("pe", [t4] + bR.wd() + Rm.rd(),
                                lambda e: e.matmul(bR.ap[:, :W], Rm.ap[:, :], qhb.ap[:, :W], start=True, stop=True))
                    bR.wr(t5, reset=True); Rm.r(t5); qhb.r(t5)
                    ti = 0 if is_q else 2
                    tb1 = tabs.ap[:, ti * 256:(ti + 1) * 256]
                    tb2 = tabs.ap[:, (ti + 1) * 256:(ti + 2) * 256]
                    t6 = p.task("dve", [t4, tabtok] + u1b.wd() + tabs.rd(),
                                lambda e: e.tensor_tensor(u1b.ap[:, :W], qhb.ap[:, :W], tb1, ALU.mult))
                    u1b.wr(t6, reset=True); qhb.r(t6); tabs.r(t6)
                    t7 = p.task("dve", [t5, t6] + bR.wd() + qh32.wd(),
                                lambda e: e.tensor_tensor(qh32.ap[:, :W], bR.ap[:, :W], tb2, ALU.mult))
                    bR.r(t7); qh32.wr(t7, reset=True); tabs.r(t7)
                    t8 = p.task("dve", [t7] + dst_buf.wd(),
                                lambda e: e.tensor_tensor(dst_ap, u1b.ap[:, :W], qh32.ap[:, :W], ALU.add))
                    dst_buf.wr(t8); u1b.r(t8); qh32.r(t8)
                    return t8

                def qkv_win(wk0, parts="qkv"):
                    buf = xbf[0]
                    tc = cast_shift_win(wk0, buf)
                    tabtok, tabsl = (load_tabs(wk0) if ("q" in parts or "k" in parts) else (None, 0))
                    bufv = buf.ap.rearrange("p (c t) -> p c t", c=NCH)
                    for hp in (range(NCH) if "q" in parts or "k" in parts else []):
                        if wk0 >= 0 and "q" in parts:
                            bq = k.bank()

                            def mmq(e, hp=hp, bq=bq, bufv=bufv):
                                last = None
                                for ci in range(NCH):
                                    last = e.matmul(bq.ap[:, :256], wqv[:, ci, hp * P:(hp + 1) * P],
                                                    bufv[:, ci, :], start=(ci == 0), stop=(ci == NCH - 1))
                                return last
                            tq = p.task("pe", [tc] + bq.wd() + wq.rd(), mmq)
                            bq.wr(tq, reset=True); wq.r(tq); buf.r(tq)
                            qoff = 2048 if wk0 == 0 else 0
                            rotary(bq, tq, hp, True, qsb,
                                   qsb.ap[:, qoff + hp * 256:qoff + (hp + 1) * 256], tabtok, tabsl)
                        bk = k.bank()

                        def mmk(e, hp=hp, bk=bk, bufv=bufv):
                            last = None
                            for ci in range(NCH):
                                last = e.matmul(bk.ap[:, :256], wkv2[:, ci, hp * P:(hp + 1) * P],
                                                bufv[:, ci, :], start=(ci == 0), stop=(ci == NCH - 1))
                            return last
                        tk = p.task("pe", [tc] + bk.wd() + wk_.rd(), mmk)
                        bk.wr(tk, reset=True); wk_.r(tk); buf.r(tk)
                        ks = kslot(wk0) if wk0 != -1 else 2
                        rotary(bk, tk, hp, False, kring,
                               kring.ap[:, hp * 1024 + ks * 256: hp * 1024 + ks * 256 + 256],
                               tabtok, tabsl)
                    for s in (range(2) if "v" in parts else []):
                        jcg = 2 * wk0 + s
                        sv = vslot(jcg)
                        bv1, bv2 = k.bank(), k.bank()

                        def mmv(e, s=s, bv1=bv1, bv2=bv2, bufv=bufv):
                            last = None
                            for ci in range(NCH):
                                last = e.matmul(bv1.ap[:, :512], bufv[:, ci, s * 128:(s + 1) * 128],
                                                wvv[:, ci, 0:512], start=(ci == 0), stop=(ci == NCH - 1))
                            for ci in range(NCH):
                                last = e.matmul(bv2.ap[:, :512], bufv[:, ci, s * 128:(s + 1) * 128],
                                                wvv[:, ci, 512:1024], start=(ci == 0), stop=(ci == NCH - 1))
                            return last
                        tv = p.task("pe", [tc] + bv1.wd() + bv2.wd() + wv.rd(), mmv)
                        bv1.wr(tv, reset=True); bv2.wr(tv, reset=True); wv.r(tv); buf.r(tv)
                        vdst = vring.ap[:, sv * 1040:(sv + 1) * 1040].rearrange("p (h e) -> p h e", h=16)
                        ev1 = p.task("dve", [tv] + vring.wd() + bv1.wd(),
                                     lambda e, vdst=vdst, bv1=bv1: e.tensor_copy(
                                         vdst[:, 0:8, 0:64], bv1.ap[:, :512].rearrange("p (h e) -> p h e", h=8)))
                        bv1.r(ev1)
                        ev2 = p.task("dve", [ev1] + bv2.wd(),
                                     lambda e, vdst=vdst, bv2=bv2: e.tensor_copy(
                                         vdst[:, 8:16, 0:64], bv2.ap[:, :512].rearrange("p (h e) -> p h e", h=8)))
                        bv2.r(ev2)
                        vring.wr(ev2)

                def attn_win(w):
                    ks0 = kslot(w - 1) if w >= 1 else 2
                    ks1 = kslot(w)
                    qoff = 2048 if w == 0 else 0
                    mtype = 0 if w == 0 else 1
                    for hp in range(NCH):
                        par = hp % 2
                        negB_, ebuf_ = negBp[par], ebufp[par]
                        probs_, vsc2_ = probsp[par], vsc2p[par]
                        sb_h = {}
                        for jcp in range(2):
                            bh0, bh1 = k.bank(), k.bank()
                            sb_h[(jcp, 0)], sb_h[(jcp, 1)] = bh0, bh1

                            def mmsc(e, jcp=jcp, bh0=bh0, bh1=bh1, hp=hp, qoff=qoff,
                                     ks0=ks0, ks1=ks1):
                                last = None
                                for jc in (2 * jcp, 2 * jcp + 1):
                                    kcol = hp * 1024 + (ks0 if jc < 2 else ks1) * 256 + (jc % 2) * 128
                                    for h2 in range(2):
                                        bb = (bh0, bh1)[h2]
                                        last = e.matmul(bb.ap[:, (jc % 2) * 256:(jc % 2) * 256 + 256],
                                                        kring.ap[h2 * 64:(h2 + 1) * 64, kcol:kcol + 128],
                                                        qsb.ap[h2 * 64:(h2 + 1) * 64,
                                                               qoff + hp * 256:qoff + (hp + 1) * 256],
                                                        start=True, stop=True)
                                return last
                            tsc = p.task("pe", kring.rd() + qsb.rd() + bh0.wd() + bh1.wd(), mmsc)
                            bh0.wr(tsc, reset=True); bh1.wr(tsc, reset=True)
                            kring.r(tsc); qsb.r(tsc)
                            if not _F_TRD:
                                continue
                            for h2 in range(2):
                                cb = h2 * 32 + hp * 4 + jcp * 2
                                trd = p.task("dve", [tsc] + negB_.wd() + sb_h[(jcp, h2)].wd(),
                                             lambda e, cb=cb, bb=sb_h[(jcp, h2)], negB_=negB_: e.tensor_reduce(
                                                 negB_.ap[:, cb:cb + 2],
                                                 bb.ap[:, :512].rearrange("p (j i) -> p j i", j=2),
                                                 AX.X, ALU.max, negate=True))
                                negB_.wr(trd); sb_h[(jcp, h2)].r(trd)
                        if _F_AWIN < 2:
                            continue
                        teb = p.task("act", negB_.rd() + ebuf_.wd(),
                                     lambda e, hp=hp, b=k.fconst(-DS_SHIFT, P), negB_=negB_, ebuf_=ebuf_: [e.activation(
                                         ebuf_.ap[:, h2 * 32 + hp * 4 + jcp * 2:h2 * 32 + hp * 4 + jcp * 2 + 2],
                                         negB_.ap[:, h2 * 32 + hp * 4 + jcp * 2:h2 * 32 + hp * 4 + jcp * 2 + 2],
                                         AF.Exp, scale=-1.0, bias=b)
                                         for h2 in range(2) for jcp in range(2)][-1])
                        ebuf_.wr(teb); negB_.r(teb)
                        for jc in range(4):
                            for h2 in range(2):
                                cb = h2 * 32 + hp * 4 + jc
                                tex = p.task("act", negB_.rd() + probs_.wd() + sb_h[(jc // 2, h2)].wd(),
                                             lambda e, jc=jc, h2=h2, cb=cb, bsc=sb_h[(jc // 2, h2)], negB_=negB_, probs_=probs_: e.activation(
                                                 probs_.ap[:, jc * 512 + h2 * 256:jc * 512 + h2 * 256 + 256],
                                                 bsc.ap[:, (jc % 2) * 256:(jc % 2) * 256 + 256], AF.Exp,
                                                 bias=negB_.ap[:, cb:cb + 1]))
                                probs_.wr(tex); sb_h[(jc // 2, h2)].r(tex); negB_.r(tex)
                        for jc in range(4):
                            tmk = p.task("pool", probs_.rd() + masks.rd(),
                                         lambda e, jc=jc, mtype=mtype, probs_=probs_: e.tensor_tensor(
                                             probs_.ap[:, jc * 512:(jc + 1) * 512].rearrange("p (h i) -> p h i", h=2),
                                             probs_.ap[:, jc * 512:(jc + 1) * 512].rearrange("p (h i) -> p h i", h=2),
                                             masks.ap[:, (mtype * 4 + jc) * 256:(mtype * 4 + jc + 1) * 256]
                                             [:, None, :].broadcast_to([P, 2, 256]), ALU.mult))
                            probs_.wr(tmk); masks.r(tmk)
                        if _F_AWIN < 3:
                            continue
                        vs2v = vsc2_.ap.rearrange("p (jc h e) -> p jc h e", jc=4, h=2)
                        ebv = ebuf_.ap.rearrange("p (h c) -> p h c", h=2)
                        for jc in range(4):
                            sv = vslot(2 * (w - 1) + jc)
                            tvs = p.task("pool", vring.rd() + ebuf_.rd() + vsc2_.wd(),
                                         lambda e, jc=jc, sv=sv, hp=hp, vs2v=vs2v, ebv=ebv: e.tensor_tensor(
                                             vs2v[:, jc, :, :],
                                             vring.ap[:, sv * 1040 + 2 * hp * 65: sv * 1040 + 2 * hp * 65 + 130]
                                             .rearrange("p (h e) -> p h e", h=2),
                                             ebv[:, :, hp * 4 + jc:hp * 4 + jc + 1]
                                             .broadcast_to([P, 2, 65]), ALU.mult))
                            vsc2_.wr(tvs); vring.r(tvs); ebuf_.r(tvs)
                        bav = k.bank()

                        def mmav(e, hp=hp, bav=bav, vs2v=vs2v, probs_=probs_):
                            last = None
                            for h2 in range(2):
                                for jc in range(4):
                                    last = e.matmul(bav.ap[0:65, h2 * 256:(h2 + 1) * 256],
                                                    vs2v[:, jc, h2, :],
                                                    probs_.ap[:, jc * 512 + h2 * 256:jc * 512 + h2 * 256 + 256],
                                                    start=(jc == 0), stop=(jc == 3))
                            return last
                        tav = p.task("pe", probs_.rd() + vsc2_.rd() + bav.wd(), mmav)
                        bav.wr(tav, reset=True); probs_.r(tav); vsc2_.r(tav)
                        if _F_AWIN < 4:
                            continue
                        ts1 = p.task("act", [tav] + sums.wd() + bav.wd(),
                                     lambda e, bav=bav: (e.copy(sums.ap[0:1, 0:256], bav.ap[64:65, 0:256]),
                                                e.copy(sums.ap[0:1, 256:512], bav.ap[64:65, 256:512]))[-1])
                        sums.wr(ts1, reset=True); bav.r(ts1)
                        trl = p.task("act", [ts1], lambda e: e.activation(sums.ap[0:1, :], sums.ap[0:1, :], AF.Ln))
                        sums.wr(trl)
                        tr = p.task("act", [trl], lambda e: e.activation(sums.ap[0:1, :], sums.ap[0:1, :], AF.Exp, scale=-1.0))
                        sums.wr(tr)
                        tsb = p.task("dve", [tr] + rnb.wd(), lambda e: e.tensor_copy(rnb.ap[0:1, 0:256], sums.ap[0:1, 0:256]))
                        rnb.wr(tsb, reset=True); sums.r(tsb)
                        brp = k.bank()
                        trp = p.task("pe", [tsb] + brp.wd() + ones.rd(),
                                     lambda e, brp=brp: e.matmul(brp.ap[0:64, 0:256], ones.ap[0:1, 0:64], rnb.ap[0:1, 0:256],
                                                        start=True, stop=True))
                        brp.wr(trp, reset=True)
                        tsb3 = p.task("dve", [trp] + rnb.wd() + sums.rd(),
                                      lambda e: e.tensor_copy(rnb.ap[0:1, 0:256], sums.ap[0:1, 256:512]))
                        rnb.wr(tsb3)
                        trp2 = p.task("pe", [tsb3] + brp.wd() + ones.rd(),
                                      lambda e, brp=brp: e.matmul(brp.ap[64:128, 0:256], ones.ap[0:1, 0:64], rnb.ap[0:1, 0:256],
                                                         start=True, stop=True))
                        brp.wr(trp2); ones.r(trp2); rnb.r(trp2)
                        tre = p.task("act", [trp2] + repsb.wd() + brp.wd(),
                                     lambda e, brp=brp: e.copy(repsb.ap[:, 0:256], brp.ap[:, 0:256]))
                        brp.r(tre); repsb.wr(tre, reset=True)
                        for h2 in range(2):
                            tdv = p.task("dve", [tre, tav] + outbf.wd() + bav.wd(),
                                         lambda e, h2=h2, hp=hp, bav=bav: e.tensor_tensor(
                                             outbf.ap[h2 * 64:(h2 + 1) * 64, hp * 256:(hp + 1) * 256],
                                             bav.ap[0:64, h2 * 256:(h2 + 1) * 256],
                                             repsb.ap[h2 * 64:(h2 + 1) * 64, 0:256], ALU.mult))
                            outbf.wr(tdv); bav.r(tdv); repsb.r(tdv)
                    for m in (range(NCH) if _F_AWIN >= 5 else []):
                        bo = k.bank()

                        def mmo(e, m=m, bo=bo):
                            last = None
                            for kc in range(NCH):
                                last = e.matmul(bo.ap[:, :256], wov[:, kc, m * P:(m + 1) * P],
                                                outbf.ap[:, kc * 256:(kc + 1) * 256],
                                                start=(kc == 0), stop=(kc == NCH - 1))
                            return last
                        tmo = p.task("pe", outbf.rd() + bo.wd() + wo.rd(), mmo)
                        bo.wr(tmo, reset=True); outbf.r(tmo); wo.r(tmo)
                        tres = p.task("dve", [tmo] + xs.wd() + bo.wd(),
                                      lambda e, m=m, bo=bo, w=w: e.tensor_tensor(
                                          xsv[:, m, HF + w * 256:HF + (w + 1) * 256],
                                          xsv[:, m, HF + w * 256:HF + (w + 1) * 256],
                                          bo.ap[:, :256], ALU.add))
                        xs.wr(tres); bo.r(tres)

                if _F_ASCHED >= 1:
                    qkv_win(0)
                if _F_ASCHED >= 2:
                    for w_ in range(1, NW):
                        qkv_win(w_)
                        if _F_ASCHED >= 3:
                            attn_win(w_)
                if _F_ASCHED >= 4:
                    qkv_win(-1)
                    qkv_win(0, parts="v")
                    halo_exchange_col()
                    attn_win(0)
                p.fence()

            if _F_MLP:
                p.fence()
                AR.reset()
                sb2 = AR.alloc
                A = {"scr": sb2("scr", [P, NCH * 512], BF16), "sq": sb2("sq", [P, NCH * 512], BF16),
                     "rows": sb2("rows", [1, 4 * 512], F32), "rowsbf": sb2("rowsbf", [1, 4 * 512], BF16),
                     "asc": sb2("asc", [P, 512], F32)}
                xnbf = sb2("xnbf", [P, NCH * 1024], BF16)
                ag = sb2("ag", [P, MCH * 1024], BF16)
                w1p = [sb2(f"w1p{i}", [P, 2 * NCH * P], BF16) for i in range(2)]
                w2p = [sb2(f"w2p{i}", [P, D], BF16) for i in range(2)]
                asc2 = sb2("asc2", [P, 512], F32)
                gsc2 = sb2("gsc2", [P, 512], F32)
                lngb = sb2("lngb", [P, 2 * NCH], F32)
                b1ab = sb2("b1ab", [P, 2 * MCH], F32)
                b2c = sb2("b2c", [P, NCH], F32)
                svcol2 = sb2("svcol2", [P, 4], F32)
                tsv2 = p.task("pool", xs.rd() + svcol2.wd(),
                              lambda e: e.tensor_copy(
                                  svcol2.ap[:, 0:4].rearrange("p (c o) -> p c o", o=1),
                                  xsv[:, NCH // 2:NCH, HF + 1023:HF + 1024]))
                svcol2.wr(tsv2, reset=True); xs.r(tsv2)
                ldb = p.dma(b1ab.ap[:, :], fb[:, off:off + 2 * MCH], b1ab.wd())
                b1ab.wr(ldb, reset=True)
                ldb2 = p.dma(b2c.ap[:, :], fb[:, off + 2 * MCH:off + 2 * MCH + NCH], b2c.wd())
                b2c.wr(ldb2, reset=True)
                ldg = p.dma(lngb.ap[:, :], fb[:, off + 2 * MCH + NCH:off + 2 * MCH + 3 * NCH], lngb.wd())
                lngb.wr(ldg, reset=True)
                xnv = xnbf.ap.rearrange("p (c t) -> p c t", c=NCH)
                agv = ag.ap.rearrange("p (c t) -> p c t", c=MCH)

                for th in range(2):
                    tb0 = th * 1024
                    kks = ((1, 0) if th == 0 else (0, 1))
                    for kk2 in kks:
                        t0 = tb0 + kk2 * 512

                        def src_of(c, t0=t0):
                            off2 = HF + t0 - (1 if c >= NCH // 2 else 0)
                            return xsv[:, c, off2:off2 + 512]
                        fx = None
                        if th == 1 and kk2 == 0:
                            fx = (svcol2.ap[:, 0:4].rearrange("p (c o) -> p c o", o=1), svcol2)
                        emit_ln_tiles(A, t0, 512, src_of,
                                      lambda c, t0=t0, tb0=tb0: xnv[:, c, t0 - tb0:t0 - tb0 + 512],
                                      gb=(lambda c: lngb.ap[:, c:c + 1],
                                          lambda c: lngb.ap[:, NCH + c:NCH + c + 1], lngb),
                                      dst_buf=xnbf, fix_src=fx)
                    for i in range(MCH):
                        pc = w1p[i % 2]
                        stream_weight_bf16(k, pc, 0, fW1a, (layer * MCH + i) * NCH * P, NCH * P, stgs, piece=1024)
                        stream_weight_bf16(k, pc, NCH * P, fW1g, (layer * MCH + i) * NCH * P, NCH * P, stgs, piece=1024)
                        pcv = pc.ap.rearrange("p (g c e) -> p g c e", g=2, c=NCH)
                        for kk2 in range(2):
                            t0 = kk2 * 512
                            bA, bG = k.bank(), k.bank()

                            def mma(e, bA=bA, bG=bG, t0=t0, pcv=pcv):
                                last = None
                                for ci in range(NCH):
                                    last = e.matmul(bA.ap[:, :512], pcv[:, 0, ci, :], xnv[:, ci, t0:t0 + 512],
                                                    start=(ci == 0), stop=(ci == NCH - 1))
                                for ci in range(NCH):
                                    last = e.matmul(bG.ap[:, :512], pcv[:, 1, ci, :], xnv[:, ci, t0:t0 + 512],
                                                    start=(ci == 0), stop=(ci == NCH - 1))
                                return last
                            tm = p.task("pe", pc.rd() + xnbf.rd() + bA.wd() + bG.wd(), mma)
                            bA.wr(tm, reset=True); bG.wr(tm, reset=True); pc.r(tm); xnbf.r(tm)
                            ta = p.task("dve", [tm] + asc2.wd() + b1ab.rd() + bA.wd(),
                                        lambda e, i=i, bA=bA: e.tensor_scalar_add(
                                            asc2.ap[:, :512], bA.ap[:, :512], b1ab.ap[:, i:i + 1]))
                            asc2.wr(ta, reset=True); bA.r(ta); b1ab.r(ta)
                            tg = p.task("act", [ta] + gsc2.wd() + b1ab.rd() + bG.wd(),
                                        lambda e, i=i, bG=bG: e.activation(gsc2.ap[:, :512], bG.ap[:, :512], AF.Gelu,
                                                                           bias=b1ab.ap[:, MCH + i:MCH + i + 1]))
                            gsc2.wr(tg, reset=True); bG.r(tg); b1ab.r(tg)
                            tmu = p.task("dve", [ta, tg] + ag.wd(),
                                         lambda e, i=i, t0=t0: e.tensor_tensor(
                                             agv[:, i, t0:t0 + 512], asc2.ap[:, :512], gsc2.ap[:, :512], ALU.mult))
                            ag.wr(tmu); asc2.r(tmu); gsc2.r(tmu)
                    for kk2 in range(2):
                        t0 = kk2 * 512
                        mb = [k.bank() for _ in range(NCH)]
                        for kc in range(MCH):
                            pw = w2p[kc % 2]
                            stream_weight_bf16(k, pw, 0, fW2, (layer * MCH + kc) * D, D, stgs, piece=1024)
                            def mmw2(e, kc=kc, t0=t0, pw=pw, mb=mb):
                                last = None
                                for m in range(NCH):
                                    last = e.matmul(mb[m].ap[:, :512], pw.ap[:, m * P:(m + 1) * P],
                                                    agv[:, kc, t0:t0 + 512], start=(kc == 0), stop=(kc == MCH - 1))
                                return last
                            deps = pw.rd() + ag.rd()
                            for m in range(NCH):
                                deps += mb[m].wd()
                            tm2 = p.task("pe", deps, mmw2)
                            for m in range(NCH):
                                mb[m].wr(tm2, reset=(kc == 0))
                            pw.r(tm2); ag.r(tm2)
                        for m in range(NCH):
                            tres = p.task("dve", mb[m].wd() + xs.wd() + b2c.rd(),
                                          lambda e, m=m, t0=t0, tb0=tb0, mb=mb: e.scalar_tensor_tensor(
                                              xsv[:, m, HF + tb0 + t0:HF + tb0 + t0 + 512], mb[m].ap[:, :512],
                                              b2c.ap[:, m:m + 1], xsv[:, m, HF + tb0 + t0:HF + tb0 + t0 + 512],
                                              ALU.add, ALU.add))
                            xs.wr(tres); mb[m].r(tres); b2c.r(tres)
                if layer < _F_LAYERS - 1:
                    halo_exchange_full()
                p.fence()

        if True:
            AR.reset()
            xot2 = AR.alloc("xot2", [P, NCH * 512], F32)
            for kk in range(4):
                t0 = kk * 512
                srcv = xot2.ap.rearrange("p (c t) -> p c t", c=NCH)
                ld = p.dma(srcv, xov[:, :, t0:t0 + 512], xo.rd() + xot2.wd())
                xot2.wr(ld, reset=True)
                tadd = p.task("dve", xot2.rd() + xs.rd(),
                              lambda e, t0=t0, srcv=srcv: e.tensor_tensor(
                                  srcv, srcv, xsv[:, :, HF + t0:HF + t0 + 512], ALU.add))
                xot2.wr(tadd); xs.r(tadd)
                st = p.dma(xov[:, :, t0:t0 + 512], srcv, xot2.rd() + xo.wd())
                xo.wr(st)
                xot2.r(st)
            p.fence()
def build(n_stages):
    k = K()
    nc = k.nc
    with k.stack:
        C = {}
        k._fc_buf = k.sb("c_f32", [P, 64], F32)
        fcload = k.p.dma(k._fc_buf.ap[:, :], k.inp("c_f32_in", [P, 64]), [])
        k.p.preamble = [fcload]
        pid_ap = nc.partition_id_tensor.ap()

        def ctx_task(e):
            rp = e.alloc_register("parity")
            e.reg_load(rp, pid_ap[0:1, 0:1])
            e.reg_alu(rp, rp, 1, ALU.bitwise_and)
            k.p_ctx["my"] = e.snap(rp, min_val=0, max_val=1)
            ro = e.alloc_register("oth")
            e.reg_alu(ro, 1, k.p_ctx["my"], ALU.subtract)
            k.p_ctx["oth"] = e.snap(ro, min_val=0, max_val=1)
            return e.nop()
        k.p.task("sp", [], ctx_task)
        ones = k.sb("c_ones", [P, P], BF16)
        C["ones"] = ones
        ones.wr(k.p.dma(ones.ap[:, :], k.inp("c_ones_in", [P, P], BF16), []))
        Rm = k.sb("c_Rm", [P, P], BF16)
        Rm.wr(k.p.dma(Rm.ap[:, :], k.inp("fRm", [P, P], BF16), []))
        C["Rmat"] = Rm
        blk2 = k.sb("c_blk2", [P, 8 * 16], BF16)
        blk2.wr(k.p.dma(blk2.ap[:, :], k.inp("fblk2", [P, 8 * 16], BF16), []))
        C["blk2"] = blk2
        selh = k.sb("c_selh", [16, 8 * P], BF16)
        selh.wr(k.p.dma(selh.ap[:, :], k.inp("fselh", [16, 8 * P], BF16), []))
        C["selh"] = selh
        masks = k.sb("c_masks", [P, 2 * 4 * 256], BF16)
        masks.wr(k.p.dma(masks.ap[:, :], k.inp("fmasks", [P, 2 * 4 * 256], BF16), []))
        C["masks"] = masks
        C["shxp"] = [k.dram("shx0", [2, P, NCH * HF], shared=True),
                     k.dram("shx1", [2, P, NCH * HF], shared=True)]
        k.init_psum()
        xc = Xchg(k)
        xo = k.dram("xo", [P, NCH * T])
        xin = k.inp("x_fm", [P, NCH * T])
        out = k.outp("y_fm", [P, NCH * T])
        xo.wr(k.p.dma(xo.ap[:, :], xin[:, :], []))
        if n_stages >= 1:
            stage_attn_a(k, xo, C)
        if n_stages >= 2:
            stage_ffn(k, xo, C, xc, "f1")
        if n_stages >= 3:
            stage_mamba(k, xo, C, xc)
        if n_stages >= 4:
            stage_ffn(k, xo, C, xc, "f2")
        k.p.dma(out[:, :], xo.ap[:, :], xo.rd() + xo.readers)
        block = k.E(nc.Block())
        k.p.emit_all(block, lambda name: k.E(nc.semaphore(name)))
    return k


def stage_mamba(k, xo, C, xc):
    nc, p = k.nc, k.p
    ones = C["ones"]
    TF = 2 * T
    xov = xo.ap.rearrange("p (c t) -> p c t", c=NCH)
    mWin = k.inp("mWin", [P, NCH * 2048])
    mXp = k.inp("mXp", [P, NCH * 96])
    mWdt = k.inp("mWdt", [P, NCH * P])
    mWout = k.inp("mWout", [P, NCH * 1024])
    mselbc = k.inp("mselbc", [32, 32 * P], BF16)
    mvec = k.inp("mvec", [P, 200])
    shnx = k.dram("shnx", [2, P, NCH * T], BF16, shared=True)
    shxd = k.dram("shxd", [2, 96, TF], shared=True)
    shop = k.dram("shop", [2, 2, P, NCH * T], shared=True)
    mu_d = k.dram("mu_d", [P, NCH * TF], BF16)
    msz_d = k.dram("msz_d", [P, NCH * TF], BF16)
    mdt_d = k.dram("mdt_d", [P, NCH * TF], BF16)
    mnxl = k.dram("mnxl", [P, NCH * T], BF16)
    mxdl = k.dram("mxdl", [96, TF])
    mxol = k.dram("mxol", [96, TF])
    mol = k.dram("mol", [2, P, NCH * T])
    mf0 = k.dram("mf0", [P, NCH * T])
    mf1 = k.dram("mf1", [P, NCH * T])

    with ExitStack() as stk:
        E = stk.enter_context

        def sb(name, shape, dt=F32):
            return Buf(E(nc.sbuf_tensor("m" + name, list(shape), dt)).ap(), tag="m" + name)

        p.fence()
        AR = Arena(k, "m_arena", 100 * 1024, stack=stk)
        vec = sb("veccols", [P, 200])
        vec.wr(p.dma(vec.ap[:, :], mvec[:, :], []))
        cw = lambda c, kk: vec.ap[:, c * 4 + kk:c * 4 + kk + 1]
        cbb = lambda c: vec.ap[:, 32 + c:33 + c]
        bdt = lambda c: vec.ap[:, 40 + c:41 + c]
        Dv = lambda c: vec.ap[:, 48 + c:49 + c]
        lng = lambda c: vec.ap[:, 184 + c:185 + c]
        lnb = lambda c: vec.ap[:, 192 + c:193 + c]
        Aneg = sb("Aneg", [P, 128])
        ta = p.task("act", vec.rd() + Aneg.wd(),
                    lambda e: e.activation(Aneg.ap[:, :], vec.ap[:, 56:184], AF.Exp))
        Aneg.wr(ta, reset=True); vec.r(ta)
        ta2 = p.task("dve", Aneg.rd(),
                     lambda e: e.tensor_scalar_mul(Aneg.ap[:, :], Aneg.ap[:, :], -1.0))
        Aneg.wr(ta2)
        selbc = sb("selsb", [32, 32 * P], BF16)
        selbc.wr(p.dma(selbc.ap[:, :], mselbc[:, :], []))
        stgs = None
        xdtb = sb("xdtb", [64, TF], BF16)
        xbcb = sb("xbcb", [32, TF], BF16)
        carries = sb("carries", [P, 128])
        tcz = p.task("dve", carries.wd(), lambda e: e.memset(carries.ap[:, :], 0.0))
        carries.wr(tcz, reset=True)

        AR.reset()
        sb2 = AR.alloc
        scr_bf = sb2("scr", [P, NCH * 512], BF16)
        sq_bf = sb2("sq", [P, NCH * 512], BF16)
        rows = sb2("rows", [1, 4 * 512], F32)
        rows_bf = sb2("rowsbf", [1, 4 * 512], BF16)
        asc = sb2("asc", [P, 512], F32)
        xot = sb2("xot", [P, NCH * 512], F32)
        hbf = sb2("hbf", [P, NCH * 512], BF16)
        sbv = scr_bf.ap.rearrange("p (c t) -> p c t", c=NCH)
        sqv = sq_bf.ap.rearrange("p (c t) -> p c t", c=NCH)
        xotv = xot.ap.rearrange("p (c t) -> p c t", c=NCH)
        hbv = hbf.ap.rearrange("p (c t) -> p c t", c=NCH)
        w = 512
        nx_wtoks = []
        for kk in range(4):
            t0 = kk * w
            ld = p.dma(xotv, xov[:, :, t0:t0 + w], xo.rd() + xot.wd())
            xot.wr(ld, reset=True)
            tc = p.task("pool", xot.rd() + scr_bf.wd(),
                        lambda e: [e.tensor_copy(sbv[:, c, :], xotv[:, c, :]) for c in range(NCH)][-1])
            scr_bf.wr(tc, reset=True); xot.r(tc)
            tq = p.task("act", [tc] + sq_bf.wd(), lambda e: e.activation(sqv, sbv, AF.Square))
            sq_bf.wr(tq, reset=True); scr_bf.r(tq)
            bsum, bsq = k.bank(), k.bank()

            def mm_s(e, bsum=bsum):
                last = None
                for ci in range(NCH):
                    last = e.matmul(bsum.ap[0:1, :w], ones.ap[:, 0:1], sbv[:, ci, :],
                                    start=(ci == 0), stop=(ci == NCH - 1))
                return last
            ts = p.task("pe", [tc] + bsum.wd() + ones.rd(), mm_s)
            bsum.wr(ts, reset=True); scr_bf.r(ts)

            def mm_q(e, bsq=bsq):
                last = None
                for ci in range(NCH):
                    last = e.matmul(bsq.ap[0:1, :w], ones.ap[:, 0:1], sqv[:, ci, :],
                                    start=(ci == 0), stop=(ci == NCH - 1))
                return last
            tq2 = p.task("pe", [tq] + bsq.wd() + ones.rd(), mm_q)
            bsq.wr(tq2, reset=True); sq_bf.r(tq2)
            mu = rows.ap[:, 0:w]; m2 = rows.ap[:, 512:512 + w]
            var = rows.ap[:, 1024:1024 + w]; rstd = rows.ap[:, 1536:1536 + w]
            t1 = p.task("dve", [ts] + rows.wd() + bsum.wd(),
                        lambda e, bsum=bsum: e.tensor_scalar_mul(mu, bsum.ap[0:1, :w], 1.0 / D))
            bsum.r(t1)
            t2 = p.task("dve", [tq2, t1] + bsq.wd(),
                        lambda e, bsq=bsq: e.tensor_scalar_mul(m2, bsq.ap[0:1, :w], 1.0 / D))
            bsq.r(t2)
            t3 = p.task("dve", [t2], lambda e: e.tensor_tensor(var, mu, mu, ALU.mult))
            t4 = p.task("dve", [t3], lambda e: e.tensor_tensor(var, m2, var, ALU.subtract))
            t5 = p.task("act", [t4], lambda e, b=k.fconst(LN_EPS): e.activation(rstd, var, AF.Ln, bias=b))
            t6 = p.task("act", [t5], lambda e: e.activation(rstd, rstd, AF.Exp, scale=-0.5))
            t8 = p.task("dve", [t6], lambda e: e.tensor_tensor(mu, mu, rstd, ALU.mult))
            t9 = p.task("dve", [t8], lambda e: e.tensor_scalar_mul(mu, mu, -1.0))
            rows.wr(t9)
            rh = rows_bf.ap[:, 0:w]; rl = rows_bf.ap[:, 512:512 + w]
            mh = rows_bf.ap[:, 1024:1024 + w]; ml = rows_bf.ap[:, 1536:1536 + w]
            c1 = p.task("dve", [t9] + rows_bf.wd(), lambda e: e.tensor_copy(rh, rstd))
            c2 = p.task("dve", [c1], lambda e: e.tensor_tensor(rl, rstd, rh, ALU.subtract))
            c3 = p.task("dve", [c2], lambda e: e.tensor_copy(mh, mu))
            c4 = p.task("dve", [c3], lambda e: e.tensor_tensor(ml, mu, mh, ALU.subtract))
            rows_bf.wr(c4, reset=True); rows.r(c4)
            br1, br2 = k.bank(), k.bank()
            tr1 = p.task("pe", [c4] + br1.wd() + ones.rd(), lambda e, br1=br1: (
                e.matmul(br1.ap[:, :w], ones.ap[0:1, :], rh, start=True, stop=False),
                e.matmul(br1.ap[:, :w], ones.ap[0:1, :], rl, start=False, stop=True))[-1])
            tr2 = p.task("pe", [c4] + br2.wd() + ones.rd(), lambda e, br2=br2: (
                e.matmul(br2.ap[:, :w], ones.ap[0:1, :], mh, start=True, stop=False),
                e.matmul(br2.ap[:, :w], ones.ap[0:1, :], ml, start=False, stop=True))[-1])
            br1.wr(tr1, reset=True); br2.wr(tr2, reset=True); rows_bf.r(tr2)
            for c in range(NCH):
                tB2 = p.task("dve", [tr1] + scr_bf.rd() + asc.wd() + br1.wd(),
                             lambda e, c=c, br1=br1: e.tensor_tensor(asc.ap[:, :w], sbv[:, c, :], br1.ap[:, :w], ALU.mult))
                br1.r(tB2); scr_bf.r(tB2)
                tC2 = p.task("dve", [tB2] + br2.wd(),
                             lambda e, br2=br2: e.tensor_tensor(asc.ap[:, :w], asc.ap[:, :w], br2.ap[:, :w], ALU.add))
                br2.r(tC2)
                asc.wr(tC2, reset=True)
                tH = p.task("dve", [tC2] + hbf.wd() + vec.rd(),
                            lambda e, c=c: e.tensor_scalar(hbv[:, c, :], asc.ap[:, :w],
                                                           lng(c), lnb(c), ALU.mult, ALU.add))
                hbf.wr(tH); vec.r(tH); asc.r(tH)
                tX = p.task("dve", [tC2] + xot.rd(),
                            lambda e, c=c: e.tensor_tensor(xotv[:, c, :], xotv[:, c, :],
                                                           asc.ap[:, :w], ALU.add))
                xot.wr(tX); asc.r(tX)
            st = p.dma(xov[:, :, t0:t0 + w], xotv, xot.rd() + xo.wd())
            xo.wr(st); xot.r(st)
            wt = p.dma(mnxl.ap.rearrange("p (c t) -> p c t", c=NCH)[:, :, kk * 512:(kk + 1) * 512],
                       hbv, hbf.rd() + mnxl.wd())
            mnxl.wr(wt); hbf.r(wt)
            nx_wtoks.append(wt)
        wts = xc_dma_dyn(xc, lambda ctx: (_ds1(shnx.ap, ctx["my"]), mnxl.ap[:, :]),
                         nx_wtoks + mnxl.rd() + shnx.wd())
        shnx.wr(wts); mnxl.r(wts)
        bar1 = xc_bar(xc, [wts])
        p.fence()

        AR.reset()
        sb2 = AR.alloc
        mWinb = sb2("Winb", [P, NCH * 2048], BF16)
        stream_weight_bf16(k, mWinb, 0, mWin, 0, NCH * 2048, stgs, piece=1024)
        nxb = sb2("nxb", [P, NCH * 512], BF16)
        ustg = sb2("ustg", [P, 528], F32)
        uacc = sb2("uacc", [P, 512], F32)
        uaccb = sb2("uaccb", [P, 512], F32)
        usg = sb2("usg", [P, 512], F32)
        ubf = sb2("ubf", [P, 512], BF16)
        uhalo = sb2("uhalo", [P, NCH * 4], F32)
        thz = p.task("dve", uhalo.wd(), lambda e: e.memset(uhalo.ap[:, :], 0.0))
        uhalo.wr(thz, reset=True)
        nxbv = nxb.ap.rearrange("p (c t) -> p c t", c=NCH)
        for tg in range(8):
            h, tl = tg // 4, tg % 4
            ld = p.dma(nxbv, shnx.ap[h].rearrange("p (c t) -> p c t", c=NCH)[:, :, tl * 512:(tl + 1) * 512],
                       [bar1] + nxb.wd())
            nxb.wr(ld, reset=True); shnx.r(ld)
            for mc in range(16):
                bk = k.bank()

                def mmz(e, bk=bk, mc=mc):
                    last = None
                    for kc in range(NCH):
                        last = e.matmul(bk.ap[:, :512], mWinb.ap[:, kc * 2048 + mc * P:kc * 2048 + (mc + 1) * P],
                                        nxbv[:, kc, :], start=(kc == 0), stop=(kc == NCH - 1))
                    return last
                tm = p.task("pe", nxb.rd() + mWinb.rd() + bk.wd(), mmz)
                bk.wr(tm, reset=True); nxb.r(tm); mWinb.r(tm)
                if mc < 8:
                    c = mc
                    te = p.task("act", [tm] + ustg.wd() + bk.wd(),
                                lambda e, bk=bk: e.copy(ustg.ap[:, 3:515], bk.ap[:, :512]))
                    ustg.wr(te, reset=True); bk.r(te)
                    th = p.task("dve", [te] + uhalo.rd(),
                                lambda e, c=c: e.tensor_copy(ustg.ap[:, 0:3], uhalo.ap[:, c * 4:c * 4 + 3]))
                    ustg.wr(th); uhalo.r(th)
                    tt0 = p.task("dve", [th] + uacc.wd() + vec.rd(),
                                 lambda e, c=c, z=k.fconst(0.0, P): e.tensor_scalar(
                                     uacc.ap[:, :], ustg.ap[:, 0:512], cw(c, 0), z, ALU.mult, ALU.add))
                    uacc.wr(tt0, reset=True); ustg.r(tt0); vec.r(tt0)
                    tprev = tt0
                    for kk in range(1, 4):
                        tprev = p.task("dve", [tprev] + vec.rd(),
                                       lambda e, c=c, kk=kk: e.scalar_tensor_tensor(
                                           uacc.ap[:, :], ustg.ap[:, kk:kk + 512], cw(c, kk),
                                           uacc.ap[:, :], ALU.mult, ALU.add))
                        uacc.wr(tprev); ustg.r(tprev); vec.r(tprev)
                    tuh = p.task("dve", [tprev] + uhalo.rd(),
                                 lambda e, c=c: e.tensor_copy(uhalo.ap[:, c * 4:c * 4 + 3], ustg.ap[:, 512:515]))
                    uhalo.wr(tuh); ustg.r(tuh)
                    tab = p.task("dve", [tprev] + uaccb.wd() + vec.rd(),
                                 lambda e, c=c: e.tensor_scalar_add(uaccb.ap[:, :], uacc.ap[:, :], cbb(c)))
                    uaccb.wr(tab, reset=True); uacc.r(tab); vec.r(tab)
                    tsg = p.task("act", [tprev] + usg.wd() + vec.rd(),
                                 lambda e, c=c: e.activation(usg.ap[:, :], uacc.ap[:, :], AF.Sigmoid, bias=cbb(c)))
                    usg.wr(tsg, reset=True); uacc.r(tsg); vec.r(tsg)
                    tmu = p.task("dve", [tab, tsg],
                                 lambda e: e.tensor_tensor(uaccb.ap[:, :], uaccb.ap[:, :], usg.ap[:, :], ALU.mult))
                    uaccb.wr(tmu); usg.r(tmu)
                    tcb = p.task("pool", [tmu] + ubf.wd(),
                                 lambda e: e.tensor_copy(ubf.ap[:, :], uaccb.ap[:, :]))
                    ubf.wr(tcb, reset=True); uaccb.r(tcb)
                    sd = p.dma(mu_d.ap[:, c * TF + tg * 512:c * TF + (tg + 1) * 512], ubf.ap[:, :],
                               ubf.rd() + mu_d.wd())
                    mu_d.wr(sd); ubf.r(sd)
                else:
                    c = mc - 8
                    tsg = p.task("act", [tm] + usg.wd() + bk.wd(),
                                 lambda e, bk=bk: e.activation(usg.ap[:, :], bk.ap[:, :512], AF.Sigmoid))
                    usg.wr(tsg, reset=True); bk.r(tsg)
                    tzm = p.task("dve", [tsg] + uaccb.wd() + bk.wd(),
                                 lambda e, bk=bk: e.tensor_tensor(uaccb.ap[:, :], bk.ap[:, :512],
                                                                  usg.ap[:, :], ALU.mult))
                    uaccb.wr(tzm, reset=True); bk.r(tzm); usg.r(tzm)
                    tcb = p.task("pool", [tzm] + ubf.wd(),
                                 lambda e: e.tensor_copy(ubf.ap[:, :], uaccb.ap[:, :]))
                    ubf.wr(tcb, reset=True); uaccb.r(tcb)
                    sd = p.dma(msz_d.ap[:, c * TF + tg * 512:c * TF + (tg + 1) * 512], ubf.ap[:, :],
                               ubf.rd() + msz_d.wd())
                    msz_d.wr(sd); ubf.r(sd)
        p.fence()

        AR.reset()
        sb2 = AR.alloc
        mXpb = sb2("Xpb", [P, NCH * 96], BF16)
        stream_weight_bf16(k, mXpb, 0, mXp, 0, NCH * 96, stgs, piece=768)
        mWdtb = sb2("Wdtb", [P, NCH * P], BF16)
        stream_weight_bf16(k, mWdtb, 0, mWdt, 0, NCH * P, stgs, piece=1024)
        mub = sb2("mub", [P, NCH * 512], BF16)
        xdmy = sb2("xdmy", [96, TF], F32)
        xdo = sb2("xdo", [96, TF], F32)
        spr = sb2("spr", [P, 512], F32)
        spn = sb2("spn", [P, 512], F32)
        spe = sb2("spe", [P, 512], F32)
        dtbf = sb2("dtbf", [P, 512], BF16)
        mubv = mub.ap.rearrange("p (c t) -> p c t", c=NCH)
        xd_wtoks = []
        for tg in range(8):
            ldu = p.dma(mubv, mu_d.ap.rearrange("p (c t) -> p c t", c=NCH)[:, :, tg * 512:(tg + 1) * 512],
                        mu_d.rd() + mub.wd())
            mub.wr(ldu, reset=True); mu_d.r(ldu)
            bk = k.bank()

            def mmx(e, bk=bk):
                last = None
                for cc in range(NCH):
                    last = e.matmul(bk.ap[0:96, :512], mXpb.ap[:, cc * 96:(cc + 1) * 96],
                                    mubv[:, cc, :], start=(cc == 0), stop=(cc == NCH - 1))
                return last
            tm = p.task("pe", mub.rd() + mXpb.rd() + bk.wd(), mmx)
            bk.wr(tm, reset=True); mub.r(tm); mXpb.r(tm)
            tev = p.task("dve", [tm] + xdmy.wd() + bk.wd(),
                         lambda e, tg=tg, bk=bk: e.tensor_copy(xdmy.ap[0:96, tg * 512:(tg + 1) * 512],
                                                               bk.ap[0:96, :512]))
            xdmy.wr(tev); bk.r(tev)
            wt = p.dma(mxdl.ap[:, tg * 512:(tg + 1) * 512],
                       xdmy.ap[0:96, tg * 512:(tg + 1) * 512], xdmy.rd() + mxdl.wd())
            mxdl.wr(wt); xdmy.r(wt)
            xd_wtoks.append(wt)
        wts2 = xc_dma_dyn(xc, lambda ctx: (_ds1(shxd.ap, ctx["my"]), mxdl.ap[:, :]),
                          xd_wtoks + mxdl.rd() + shxd.wd())
        shxd.wr(wts2); mxdl.r(wts2)
        bar2 = xc_bar(xc, [wts2])
        ldb = xc_dma_dyn(xc, lambda ctx: (mxol.ap[:, :], _ds1(shxd.ap, ctx["oth"])),
                         [bar2] + mxol.wd())
        mxol.wr(ldb, reset=True); shxd.r(ldb)
        ldo = p.dma(xdo.ap[0:96, :], mxol.ap[:, :], mxol.rd() + xdo.wd())
        xdo.wr(ldo, reset=True); mxol.r(ldo)
        tadd = p.task("dve", xdo.rd() + xdmy.rd(),
                      lambda e: e.tensor_tensor(xdmy.ap[0:96, :], xdmy.ap[0:96, :], xdo.ap[0:96, :], ALU.add))
        xdmy.wr(tadd); xdo.r(tadd)
        tc1 = p.task("pool", xdmy.rd() + xdtb.wd(),
                     lambda e: e.tensor_copy(xdtb.ap[:, :], xdmy.ap[0:64, :]))
        xdtb.wr(tc1, reset=True); xdmy.r(tc1)
        tc2 = p.task("pool", xdmy.rd() + xbcb.wd(),
                     lambda e: e.tensor_copy(xbcb.ap[:, :], xdmy.ap[64:96, :]))
        xbcb.wr(tc2, reset=True); xdmy.r(tc2)
        for tg in range(8):
            for mch in range(NCH):
                bk = k.bank()
                tm = p.task("pe", xdtb.rd() + mWdtb.rd() + bk.wd(),
                            lambda e, tg=tg, mch=mch, bk=bk: e.matmul(
                                bk.ap[:, :512], mWdtb.ap[0:64, mch * P:(mch + 1) * P],
                                xdtb.ap[0:64, tg * 512:(tg + 1) * 512], start=True, stop=True))
                bk.wr(tm, reset=True); xdtb.r(tm); mWdtb.r(tm)
                t1 = p.task("dve", [tm] + spr.wd() + vec.rd() + bk.wd(),
                            lambda e, mch=mch, bk=bk: e.tensor_scalar_add(spr.ap[:, :], bk.ap[:, :512], bdt(mch)))
                spr.wr(t1, reset=True); bk.r(t1); vec.r(t1)
                t2 = p.task("act", [t1] + spn.wd(),
                            lambda e: e.activation(spn.ap[:, :], spr.ap[:, :], AF.Abs))
                spn.wr(t2, reset=True); spr.r(t2)
                t3 = p.task("act", [t2] + spe.wd(),
                            lambda e: e.activation(spe.ap[:, :], spn.ap[:, :], AF.Exp, scale=-1.0))
                spe.wr(t3, reset=True); spn.r(t3)
                t4 = p.task("act", [t3], lambda e, b=k.fconst(1.0, P): e.activation(
                    spe.ap[:, :], spe.ap[:, :], AF.Ln, bias=b))
                spe.wr(t4)
                t5 = p.task("dve", [t1] + spr.wd(), lambda e: e.tensor_scalar_max(spr.ap[:, :], spr.ap[:, :], 0.0))
                spr.wr(t5)
                t6 = p.task("dve", [t4, t5], lambda e: e.tensor_tensor(spr.ap[:, :], spr.ap[:, :],
                                                                       spe.ap[:, :], ALU.add))
                spr.wr(t6); spe.r(t6)
                t7 = p.task("pool", [t6] + dtbf.wd(), lambda e: e.tensor_copy(dtbf.ap[:, :], spr.ap[:, :]))
                dtbf.wr(t7, reset=True); spr.r(t7)
                sd = p.dma(mdt_d.ap[:, mch * TF + tg * 512:mch * TF + (tg + 1) * 512], dtbf.ap[:, :],
                           dtbf.rd() + mdt_d.wd())
                mdt_d.wr(sd); dtbf.r(sd)
        p.fence()

        AR.reset()
        sb2 = AR.alloc
        mWoutb = sb2("Woutb", [P, NCH * 1024], BF16)
        stream_weight_bf16(k, mWoutb, 0, mWout, 0, NCH * 1024, stgs, piece=1024)
        Breps = sb2("Breps", [P, 16 * 512], BF16)
        Creps = sb2("Creps", [P, 16 * 512], BF16)
        dtt = sb2("dtt", [P, 512], BF16)
        utt = sb2("utt", [P, 512], BF16)
        szt = sb2("szt", [P, 512], BF16)
        dub = sb2("dub", [P, 512], BF16)
        dAb = sb2("dAb", [P, 512], F32)
        dbub = sb2("dbub", [P, 512], F32)
        hcs = sb2("hcs", [P, 512], F32)
        ytmp = sb2("ytmp", [P, 512], F32)
        y32 = sb2("y32", [P, 512], F32)
        ybf = sb2("ybf", [P, 512], BF16)
        opstg = sb2("opstg", [P, NCH * 512], F32)
        opsv = opstg.ap.rearrange("p (c t) -> p c t", c=NCH)
        op_wtoks = []
        for tg in range(8):
            for j in range(32):
                bk = k.bank()
                tm = p.task("pe", selbc.rd() + xbcb.rd() + bk.wd(),
                            lambda e, j=j, tg=tg, bk=bk: e.matmul(
                                bk.ap[:, :512], selbc.ap[:, j * P:(j + 1) * P],
                                xbcb.ap[0:32, tg * 512:(tg + 1) * 512], start=True, stop=True))
                bk.wr(tm, reset=True); selbc.r(tm); xbcb.r(tm)
                dst = (Breps if j < 16 else Creps)
                tev = p.task("dve", [tm] + dst.wd() + bk.wd(),
                             lambda e, j=j, bk=bk, dst=dst: e.tensor_copy(
                                 dst.ap[:, (j % 16) * 512:(j % 16 + 1) * 512], bk.ap[:, :512]))
                dst.wr(tev); bk.r(tev)
            mb = [k.bank() for _ in range(NCH)]
            for cch in range(NCH):
                l1 = p.dma(dtt.ap[:, :], mdt_d.ap[:, cch * TF + tg * 512:cch * TF + (tg + 1) * 512],
                           mdt_d.rd() + dtt.wd())
                dtt.wr(l1, reset=True); mdt_d.r(l1)
                l2 = p.dma(utt.ap[:, :], mu_d.ap[:, cch * TF + tg * 512:cch * TF + (tg + 1) * 512],
                           mu_d.rd() + utt.wd())
                utt.wr(l2, reset=True); mu_d.r(l2)
                l3 = p.dma(szt.ap[:, :], msz_d.ap[:, cch * TF + tg * 512:cch * TF + (tg + 1) * 512],
                           msz_d.rd() + szt.wd())
                szt.wr(l3, reset=True); msz_d.r(l3)
                tdu = p.task("pool", [l1, l2] + dub.wd() + dtt.rd() + utt.rd(),
                             lambda e: e.tensor_tensor(dub.ap[:, :], dtt.ap[:, :], utt.ap[:, :], ALU.mult))
                dub.wr(tdu, reset=True); dtt.r(tdu); utt.r(tdu)
                for n in range(16):
                    ci = cch * 16 + n
                    tA = p.task("act", dtt.rd() + dAb.wd() + Aneg.rd(),
                                lambda e, ci=ci: e.activation(dAb.ap[:, :], dtt.ap[:, :], AF.Exp,
                                                              scale=Aneg.ap[:, ci:ci + 1]))
                    dAb.wr(tA, reset=True); dtt.r(tA); Aneg.r(tA)
                    tB = p.task("pool", dub.rd() + Breps.rd() + dbub.wd(),
                                lambda e, n=n: e.tensor_tensor(dbub.ap[:, :], dub.ap[:, :],
                                                               Breps.ap[:, n * 512:(n + 1) * 512], ALU.mult))
                    dbub.wr(tB, reset=True); dub.r(tB); Breps.r(tB)
                    tS = p.task("dve", [tA, tB] + hcs.wd() + carries.rd(),
                                lambda e, ci=ci: e.tensor_tensor_scan(
                                    hcs.ap[:, :], dAb.ap[:, :], dbub.ap[:, :],
                                    carries.ap[:, ci:ci + 1], ALU.mult, ALU.add))
                    hcs.wr(tS, reset=True); dAb.r(tS); dbub.r(tS); carries.r(tS)
                    tCr = p.task("dve", [tS] + carries.wd(),
                                 lambda e, ci=ci: e.tensor_copy(carries.ap[:, ci:ci + 1],
                                                                hcs.ap[:, 511:512]))
                    carries.wr(tCr); hcs.r(tCr)
                    if n == 0:
                        tY = p.task("dve", [tS] + y32.wd() + Creps.rd(),
                                    lambda e, n=n: e.tensor_tensor(y32.ap[:, :], hcs.ap[:, :],
                                                                   Creps.ap[:, n * 512:(n + 1) * 512], ALU.mult))
                        y32.wr(tY, reset=True); hcs.r(tY); Creps.r(tY)
                    else:
                        tY1 = p.task("dve", [tS] + ytmp.wd() + Creps.rd(),
                                     lambda e, n=n: e.tensor_tensor(ytmp.ap[:, :], hcs.ap[:, :],
                                                                    Creps.ap[:, n * 512:(n + 1) * 512], ALU.mult))
                        ytmp.wr(tY1, reset=True); hcs.r(tY1); Creps.r(tY1)
                        tY = p.task("dve", [tY1] + y32.rd(),
                                    lambda e: e.tensor_tensor(y32.ap[:, :], y32.ap[:, :],
                                                              ytmp.ap[:, :], ALU.add))
                        y32.wr(tY); ytmp.r(tY)
                tUD = p.task("dve", y32.rd() + utt.rd() + vec.rd(),
                             lambda e, cch=cch: e.scalar_tensor_tensor(
                                 y32.ap[:, :], utt.ap[:, :], Dv(cch), y32.ap[:, :], ALU.mult, ALU.add))
                y32.wr(tUD); utt.r(tUD); vec.r(tUD)
                tYB = p.task("dve", [tUD, l3] + ybf.wd() + szt.rd(),
                             lambda e: e.tensor_tensor(ybf.ap[:, :], y32.ap[:, :], szt.ap[:, :], ALU.mult))
                ybf.wr(tYB, reset=True); y32.r(tYB); szt.r(tYB)

                def mmo(e, cch=cch, mb=mb):
                    last = None
                    for m in range(NCH):
                        last = e.matmul(mb[m].ap[:, :512], mWoutb.ap[:, cch * 1024 + m * P:cch * 1024 + (m + 1) * P],
                                        ybf.ap[:, :], start=(cch == 0), stop=(cch == NCH - 1))
                    return last
                deps = ybf.rd() + mWoutb.rd()
                for m in range(NCH):
                    deps = deps + mb[m].wd()
                tmo = p.task("pe", deps, mmo)
                for m in range(NCH):
                    mb[m].wr(tmo, reset=(cch == 0))
                ybf.r(tmo); mWoutb.r(tmo)
            for m in range(NCH):
                tev = p.task("dve", mb[m].wd() + opstg.wd(),
                             lambda e, m=m, mb=mb: e.tensor_copy(opsv[:, m, :], mb[m].ap[:, :512]))
                opstg.wr(tev); mb[m].r(tev)
            wt = p.dma(mol.ap[tg // 4].rearrange("p (c t) -> p c t", c=NCH)
                       [:, :, (tg % 4) * 512:(tg % 4 + 1) * 512],
                       opsv, opstg.rd() + mol.wd())
            mol.wr(wt); opstg.r(wt)
            op_wtoks.append(wt)
        wts3 = xc_dma_dyn(xc, lambda ctx: (_ds1(shop.ap, ctx["my"]), mol.ap[:, :, :]),
                          op_wtoks + mol.rd() + shop.wd())
        shop.wr(wts3); mol.r(wts3)
        bar3 = xc_bar(xc, [wts3])
        p.fence()

        AR.reset()
        sb2 = AR.alloc
        f0 = sb2("f0", [P, NCH * 512], F32)
        f1 = sb2("f1", [P, NCH * 512], F32)
        xoa = sb2("xoa", [P, NCH * 512], F32)
        f0v = f0.ap.rearrange("p (c t) -> p c t", c=NCH)
        f1v = f1.ap.rearrange("p (c t) -> p c t", c=NCH)
        xoav = xoa.ap.rearrange("p (c t) -> p c t", c=NCH)
        g0 = xc_dma_dyn(xc, lambda ctx: (mf0.ap[:, :], _ds1(shop.ap[0], ctx["my"])),
                        [bar3] + mf0.wd())
        mf0.wr(g0, reset=True); shop.r(g0)
        g1 = xc_dma_dyn(xc, lambda ctx: (mf1.ap[:, :], _ds1(shop.ap[1], ctx["my"])),
                        [bar3] + mf1.wd())
        mf1.wr(g1, reset=True); shop.r(g1)
        for kk in range(4):
            r0 = p.dma(f0v, mf0.ap.rearrange("p (c t) -> p c t", c=NCH)[:, :, kk * 512:(kk + 1) * 512],
                       mf0.rd() + f0.wd())
            f0.wr(r0, reset=True); mf0.r(r0)
            r1 = p.dma(f1v, mf1.ap.rearrange("p (c t) -> p c t", c=NCH)[:, :, kk * 512:(kk + 1) * 512],
                       mf1.rd() + f1.wd())
            f1.wr(r1, reset=True); mf1.r(r1)
            lx = p.dma(xoav, xov[:, :, kk * 512:(kk + 1) * 512], xo.rd() + xoa.wd())
            xoa.wr(lx, reset=True)
            t1 = p.task("dve", f0.rd() + f1.rd(),
                        lambda e: e.tensor_tensor(f0.ap[:, :], f0.ap[:, :], f1.ap[:, :], ALU.add))
            f0.wr(t1); f1.r(t1)
            t2 = p.task("dve", [t1] + xoa.rd(),
                        lambda e: e.tensor_tensor(xoa.ap[:, :], xoa.ap[:, :], f0.ap[:, :], ALU.add))
            xoa.wr(t2); f0.r(t2)
            st = p.dma(xov[:, :, kk * 512:(kk + 1) * 512], xoav, xoa.rd() + xo.wd())
            xo.wr(st); xoa.r(st)
        p.fence()


def kernel(**inputs):
    n_stages = _N_STAGES
    k = build(n_stages)
    maps = host_inputs(inputs)
    fc = np.zeros((P, 64), dtype=np.float32)
    fc[:, :len(k.fconst_vals)] = np.asarray(k.fconst_vals)[None, :]
    for m in maps:
        m["c_f32_in"] = fc
    maps = [{n: m[n] for n in k.ins} for m in maps]
    for n, (shape, dt) in k.ins.items():
        assert tuple(maps[0][n].shape) == shape, (n, maps[0][n].shape, shape)
    res = run_bass_kernel_spmd(k.nc, maps, core_ids=list(range(8)))
    out = np.empty((4, 4096, D), dtype=np.float32)
    for core in range(8):
        bidx, half = core // 2, core % 2
        out[bidx, half * T:(half + 1) * T] = from_fm(np.asarray(res.results[core]["y_fm"]), T)
    return out


DS_SHIFT = 20.0

